# version 12
# baseline (speedup 1.0000x reference)
import contextlib
import math

import numpy as np
import ml_dtypes

import concourse.bass as bass
import concourse.mybir as mybir
from concourse.ap import AP
from concourse.bass_utils import run_bass_kernel_spmd

F32 = mybir.dt.float32
BF16 = mybir.dt.bfloat16
AF = mybir.ActivationFunctionType
ALU = mybir.AluOpType
AX = mybir.AxisListType

N_CORES = 8
S = 2048
B_LOC = 4
T_LOC = S * B_LOC
D = 1024
H = 16
HD = 64
NCH = 8
NTILE = T_LOC // 128
LN_EPS0 = 1e-5
RMS_EPS = 1e-6
KQ_EPS = 1e-6
MOE_E = 8
MOE_F = 2048
CAP = (N_CORES * T_LOC) // MOE_E
VD = 16 * 65

BFNP = ml_dtypes.bfloat16


class CH:
    __slots__ = ("name", "n", "sem")

    def __init__(self, name):
        self.name = name
        self.n = 0
        self.sem = None


class Prog:
    ENGINES = ("sync", "tensor", "vector", "scalar", "gpsimd")

    def __init__(self, nc):
        self.nc = nc
        self.ops = {e: [] for e in self.ENGINES}
        self.ch = {e: CH("c_" + e) for e in self.ENGINES}
        self.dch = {q: CH("d_" + q) for q in ("g", "sy", "sc")}
        self.chains = list(self.ch.values()) + list(self.dch.values())
        self.last_w = {}
        self.readers = {}

    def _dep_add(self, deps, entry, self_chain, is_pe):
        for c, v in entry.items():
            if is_pe and c is self.ch["tensor"]:
                continue
            deps[c] = max(deps.get(c, 0), v)

    def op(self, eng, emit, reads=(), writes=(), no_inc=False, dq=None,
           extra_waits=(), inc_amt=None):
        chain = self.dch[dq] if dq else self.ch[eng]
        is_pe = (eng == "tensor") and dq is None
        deps = {}
        for (c, v) in extra_waits:
            deps[c] = max(deps.get(c, 0), v)
        for b in reads:
            if b in self.last_w:
                self._dep_add(deps, self.last_w[b], chain, is_pe)
        for b in writes:
            if b in self.last_w:
                self._dep_add(deps, self.last_w[b], chain, is_pe)
            if b in self.readers:
                self._dep_add(deps, self.readers[b], chain, is_pe)
        amt = inc_amt if inc_amt is not None else (16 if dq else 1)
        if no_inc:
            val = chain.n + amt
            inc = None
        else:
            chain.n += amt
            val = chain.n
            inc = chain
        tok = (chain, val)
        for b in reads:
            self.readers.setdefault(b, {})
            self.readers[b][chain] = val
        for b in writes:
            self.last_w[b] = {chain: val}
            self.readers[b] = {}
        waits = sorted(((c, v) for c, v in deps.items()), key=lambda t: t[0].name)
        self.ops[eng].append((waits, emit, inc, amt))
        return tok

    def dma(self, q, out, in_, reads=(), writes=(), **kw):
        eng = {"g": "gpsimd", "sy": "sync", "sc": "scalar"}[q]
        return self.op(eng, (lambda e, o=out, i=in_, kw=kw: e.dma_start(out=o, in_=i, **kw)),
                       reads=reads, writes=writes, dq=q)

    def barrier_token(self):
        return [(c, c.n) for c in self.chains if c.n > 0]

    def stage_barrier(self):
        tok = self.barrier_token()
        for eng in self.ENGINES:
            self.op(eng, (lambda e: e.nop()), extra_waits=tok)

    def finalize(self, es):
        for c in self.chains:
            c.sem = es.enter_context(self.nc.semaphore(c.name))
        blk = es.enter_context(self.nc.Block())
        for ename in self.ENGINES:
            ops = self.ops[ename]
            if not ops:
                continue

            def mk(ops):
                def body(eng):
                    hw = {}
                    for waits, emit, inc, amt in ops:
                        for (c, v) in waits:
                            if hw.get(id(c), 0) < v:
                                eng.wait_ge(c.sem, v)
                                hw[id(c)] = v
                        ins = emit(eng)
                        if inc is not None:
                            ins.then_inc(inc.sem, amt)
                return body

            getattr(blk, ename)(mk(ops))


def host_consts():
    c = {}
    c["c_idf32"] = np.eye(128, dtype=np.float32)
    c["c_idbf"] = np.eye(128, dtype=np.float32).astype(BFNP)
    i = np.arange(128)[:, None]
    cc = np.arange(256)[None, :]
    c["c_mask01"] = np.where(i <= cc - 128, 1.0, 0.0).astype(np.float32)
    c["c_onesrow"] = np.ones((1, 128), dtype=np.float32)
    hm = np.zeros((128, NCH, H), dtype=np.float32)
    he = np.zeros((16, NCH, 128), dtype=np.float32)
    for ch in range(NCH):
        hm[0:64, ch, 2 * ch] = 1.0
        hm[64:128, ch, 2 * ch + 1] = 1.0
        he[2 * ch, ch, 0:64] = 1.0
        he[2 * ch + 1, ch, 64:128] = 1.0
    c["c_headmask"] = np.ascontiguousarray(hm.reshape(128, NCH * H))
    c["c_headexp"] = np.ascontiguousarray(he.reshape(16, NCH * 128))
    oht, negrow, doff, nd = l1_bias_consts()
    c["c_oht"] = oht
    c["c_negrow"] = np.broadcast_to(negrow, (16, nd)).copy()
    iota = np.broadcast_to(np.arange(8, dtype=np.float32), (128, 8)).copy()
    c["c_iota8"] = iota
    c["c_iotaB"] = (iota - 1000.0).astype(np.float32)
    c["c_p2e"] = (np.arange(128)[:, None] % 8 == np.arange(8)[None, :]).astype(np.float32)
    return c


def l1_bias_consts():
    NB = 32
    MAXD = 128
    DOFF = 1024
    ND = 4096
    d = np.arange(ND) - DOFF
    rp = np.maximum(d, 0)
    max_exact = NB // 2
    is_small = rp < max_exact
    safe = np.maximum(rp, 1).astype(np.float32)
    large = max_exact + (
        (np.log(safe / np.float32(max_exact)) / np.float32(math.log(MAXD / max_exact)))
        * (NB - max_exact)
    ).astype(np.int32)
    large = np.minimum(large, NB - 1)
    bucket = np.where(is_small, rp, large)
    oht = np.zeros((NB, ND), dtype=np.float32)
    valid = d >= 1
    oht[bucket[valid], np.arange(ND)[valid]] = 1.0
    negrow = np.where(d <= 0, -1e9, 0.0).astype(np.float32)
    return oht, negrow, DOFF, ND


def v3(t, p_pitch, offset, dims):
    return AP(t[0:1, 0:1].tensor, offset, dims)


def emit_stats_tail(P, sb, nm_name, rs_name, eps, inner_name=None, inner_eps=None):
    mv = sb["s_mv"]
    mean_view = v3(mv, None, 0, [[128, 128], [2, 64]])
    var_view = v3(mv, None, 1, [[128, 128], [2, 64]])
    P.op("vector", (lambda e, o=sb[nm_name][:, :], i=mean_view:
                    e.tensor_scalar(o, i, -1.0, None, ALU.mult)),
         reads=["s_mv"], writes=[nm_name])
    fc0 = fconst(P, sb, 0.0)
    bias_eps = fconst(P, sb, eps)
    P.op("scalar", (lambda e, o=sb["s_t64"][:, :], i=var_view, b=bias_eps:
                    e.activation(o, i, AF.Ln, bias=b)),
         reads=["s_mv", "cpool"], writes=["s_t64"])
    P.op("scalar", (lambda e, o=sb[rs_name][:, :], i=sb["s_t64"][:, :]:
                    e.activation(o, i, AF.Exp, scale=-0.5, bias=fc0)),
         reads=["s_t64", "cpool"], writes=[rs_name])
    if inner_name is not None:
        P.op("vector", (lambda e, o=sb["s_t64"][:, :], a=sb[rs_name][:, :]:
                        e.tensor_mul(o, a, a)),
             reads=[rs_name], writes=["s_t64"])
        P.op("vector", (lambda e, o=sb["s_t64"][:, :], a=sb["s_t64"][:, :], b=var_view:
                        e.tensor_mul(o, a, b)),
             reads=["s_t64", "s_mv"], writes=["s_t64"])
        bias_ieps = fconst(P, sb, inner_eps)
        P.op("scalar", (lambda e, o=sb["s_t64b"][:, :], i=sb["s_t64"][:, :], b=bias_ieps:
                        e.activation(o, i, AF.Ln, bias=b)),
             reads=["s_t64", "cpool"], writes=["s_t64b"])
        P.op("scalar", (lambda e, o=sb[inner_name][:, :], i=sb["s_t64b"][:, :]:
                        e.activation(o, i, AF.Exp, scale=-0.5, bias=fc0)),
             reads=["s_t64b", "cpool"], writes=[inner_name])


def emit_stats_tiles(P, sb, x, tiles):
    bn6 = sb["s_bn6"]
    mv = sb["s_mv"]
    for ti in tiles:
        xb = sb["xs%d" % (ti % 2)]
        xtag = "xs%d" % (ti % 2)
        P.dma("g", xb[:, :], x[ti, :, :], writes=[xtag])
        P.op("vector", (lambda e, o=bn6[:, 0:6], i=xb[:, 0:512]: e.bn_stats(o, i)),
             reads=[xtag], writes=["s_bn6"])
        P.op("vector", (lambda e, o=bn6[:, 6:12], i=xb[:, 512:1024]: e.bn_stats(o, i)),
             reads=[xtag], writes=["s_bn6"])
        P.op("vector", (lambda e, o=mv[:, 2 * ti:2 * ti + 2], i=bn6[:, :]: e.bn_aggr(o, i)),
             reads=["s_bn6"], writes=["s_mv"])


def emit_stats_pass(P, dr, sb, x_dram_name, nm_name, rs_name, eps, inner_name=None,
                    inner_eps=None):
    x = dr[x_dram_name].ap().rearrange("(n p) d -> n p d", p=128)
    emit_stats_tiles(P, sb, x, range(NTILE))
    emit_stats_tail(P, sb, nm_name, rs_name, eps, inner_name, inner_eps)


def emit_load_consts(P, dr, sb, names):
    for nm in names:
        P.dma("g", sb[nm][:, :], dr[nm].ap(), writes=[nm])


def fconst(P, sb, val):
    d = sb.setdefault("_cpool_map", {})
    if val not in d:
        idx = len(d)
        assert idx < 16
        d[val] = idx
        P.op("vector", (lambda e, idx=idx, val=val: e.memset(
            sb["cpool"][:, idx:idx + 1], float(val))), writes=["cpool"])
    return sb["cpool"][:, d[val]:d[val] + 1]


S1_WNAMES = ["Wq", "Wk", "Wv", "Wr", "Wk2", "Wv2"]


def emit_stage1(P, dr, sb):
    x = dr["x_in"].ap().rearrange("(n p) d -> n p d", p=128)
    xB = dr["xB"].ap().rearrange("(n p) d -> n p d", p=128)
    v0 = dr["v0_tm"].ap().rearrange("(n p) d -> n p d", p=128)
    q0 = dr["q0_fm"].ap()
    k0 = dr["k0_fm"].ap()
    ssq0 = dr["ssq0"].ap()
    ssk0 = dr["ssk0"].ap()

    xn32 = sb["xn32g"]
    xnbf = sb["xnbf"]
    dbuf = sb["dbuf"]
    xrb = sb["xrb"]
    xkb = sb["xkb"]
    kact = sb["kact"]

    emit_stats_pass(P, dr, sb, "x_in", "s_nm", "s_rs", LN_EPS0)

    for nm in S1_WNAMES:
        P.dma("g", dr[nm + "_bf"].ap(), dr[nm].ap(), writes=[nm + "_bf"])
        for kc in range(NCH):
            P.dma("sc", sb["WT_" + nm][:, kc * D:(kc + 1) * D],
                  dr[nm + "_bf"].ap()[:, kc * 128:(kc + 1) * 128],
                  reads=[nm + "_bf"], writes=["WT_" + nm], transpose=True)
    P.dma("g", sb["mu_ch"][:, :], dr["mu_ch"].ap(), writes=["mu_ch"])
    P.op("vector", (lambda e: e.tensor_scalar(sb["mu1"][:, :], sb["mu_ch"][:, :],
                                              -1.0, None, ALU.add)),
         reads=["mu_ch"], writes=["mu1"])
    for t in ("v_sb0", "v_sb1"):
        P.op("gpsimd", (lambda e, t=t: e.memset(
            AP(sb[t][0:1, 0:1].tensor, 64, [[VD, 128], [65, 16]]), 1.0)), writes=[t])

    def xn32_v(kc, j0, n, extra=None):
        return AP(xn32[0:1, 0:1].tensor, kc * 513 + j0,
                  [[NCH * 513, 128], [1, n]] if extra is None else extra)

    def emit_group(g):
        seq_start = (g % 4 == 0)
        if seq_start:
            P.op("vector", (lambda e: e.memset(
                AP(xn32[0:1, 0:1].tensor, 0, [[NCH * 513, 128], [513, NCH], [1, 1]]), 0.0)),
                 writes=["xn32g_c0"], reads=[])
        for ti in range(4):
            gi = 4 * g + ti
            xs = sb["xs%d" % (ti % 2)]
            xst = "xs%d" % (ti % 2)
            P.dma("g", xs[:, :], x[gi, :, :], writes=[xst])
            P.op("vector", (lambda e, xs=xs, gi=gi: e.tensor_scalar(
                sb["xn_tm"][:, :], xs[:, :],
                sb["s_nm"][:, gi:gi + 1], sb["s_rs"][:, gi:gi + 1],
                ALU.add, ALU.mult)),
                 reads=[xst, "s_nm", "s_rs"], writes=["xn_tm"])
            for kc in range(NCH):
                pslc = sb["ptf"][:, kc * 128:(kc + 1) * 128]
                P.op("tensor", (lambda e, kc=kc, pslc=pslc: e.transpose(
                    pslc, sb["xn_tm"][:, kc * 128:(kc + 1) * 128], sb["c_idf32"][:, :])),
                     reads=["xn_tm", "c_idf32"],
                     writes=["ptf%d" % kc, "ptfB%d" % (kc // 4)])
                P.op("vector", (lambda e, kc=kc, ti=ti, pslc=pslc: e.tensor_copy(
                    xn32_v(kc, 1 + ti * 128, 128), pslc)),
                     reads=["ptf%d" % kc], writes=["xn32g", "ptfB%d" % (kc // 4)])
                P.op("scalar", (lambda e, kc=kc, ti=ti, pslc=pslc: e.copy(
                    AP(xnbf[0:1, 0:1].tensor, kc * 512 + ti * 128,
                       [[NCH * 512, 128], [1, 128]]), pslc)),
                     reads=["ptf%d" % kc], writes=["xnbf", "ptfB%d" % (kc // 4)])
        for kc in range(NCH):
            P.op("vector", (lambda e, kc=kc: e.tensor_sub(
                dbuf[:, :], xn32_v(kc, 1, 512), xn32_v(kc, 0, 512))),
                 reads=["xn32g", "xn32g_c0"], writes=["dbuf"])
            for (dst, dstt, mucol) in ((xrb, "xrb", 0), (xkb, "xkb", 1)):
                P.op("vector", (lambda e, kc=kc, dst=dst, mucol=mucol: e.scalar_tensor_tensor(
                    AP(dst[0:1, 0:1].tensor, kc * 512, [[NCH * 512, 128], [1, 512]]),
                    dbuf[:, :],
                    sb["mu1"][:, mucol * NCH + kc: mucol * NCH + kc + 1],
                    xn32_v(kc, 1, 512),
                    op0=ALU.mult, op1=ALU.add)),
                     reads=["dbuf", "mu1", "xn32g"], writes=[dstt])
        P.op("vector", (lambda e: e.tensor_copy(
            AP(xn32[0:1, 0:1].tensor, 0, [[NCH * 513, 128], [513, NCH], [1, 1]]),
            AP(xn32[0:1, 0:1].tensor, 512, [[NCH * 513, 128], [513, NCH], [1, 1]]))),
             reads=["xn32g"], writes=["xn32g_c0", "xn32g_c0w"])

        for dc in range(NCH):
            pg = sb["pg%d" % (dc % 2)]
            pgt = "pg%d" % (dc % 2)
            for kc in range(NCH):
                P.op("tensor", (lambda e, dc=dc, kc=kc, pg=pg: e.matmul(
                    pg[:, :], sb["WT_Wk2"][:, kc * D + dc * 128: kc * D + dc * 128 + 128],
                    AP(xkb[0:1, 0:1].tensor, kc * 512, [[NCH * 512, 128], [1, 512]]),
                    start=(kc == 0), stop=(kc == NCH - 1))),
                     reads=["WT_Wk2", "xkb"], writes=[pgt], no_inc=(kc < NCH - 1))
            P.op("vector", (lambda e, pg=pg: e.tensor_scalar(
                sb["relu_t"][:, :], pg[:, :], 0.0, None, ALU.max)),
                 reads=[pgt], writes=["relu_t"])
            P.op("scalar", (lambda e, dc=dc, b=fconst(P, sb, 0.0): e.activation(
                AP(kact[0:1, 0:1].tensor, dc * 512, [[NCH * 512, 128], [1, 512]]),
                sb["relu_t"][:, :], AF.Square, bias=b)),
                 reads=["relu_t", "cpool"], writes=["kact"])

        for ti in range(4):
            gi = 4 * g + ti
            xf = sb["xs%d" % (ti % 2)]
            xft = "xs%d" % (ti % 2)
            P.dma("g", xf[:, :], x[gi, :, :], writes=[xft])
            for dh in range(2):
                pg = sb["pg2"]
                for kc in range(NCH):
                    P.op("tensor", (lambda e, ti=ti, dh=dh, kc=kc, pg=pg: e.matmul(
                        pg[:, :],
                        AP(xrb[0:1, 0:1].tensor, kc * 512 + ti * 128,
                           [[NCH * 512, 128], [1, 128]]),
                        sb["WT_Wr"][:, kc * D + dh * 512: kc * D + dh * 512 + 512],
                        start=(kc == 0), stop=(kc == NCH - 1))),
                         reads=["xrb", "WT_Wr"], writes=["pg2"], no_inc=(kc < NCH - 1))
                P.op("scalar", (lambda e, dh=dh, pg=pg, b=fconst(P, sb, 0.0): e.activation(
                    sb["r_sb"][:, dh * 512:dh * 512 + 512], pg[:, :], AF.Sigmoid, bias=b)),
                     reads=["pg2", "cpool"], writes=["r_sb"])
            for dh in range(2):
                pg = sb["pg3"]
                for kc in range(NCH):
                    P.op("tensor", (lambda e, ti=ti, dh=dh, kc=kc, pg=pg: e.matmul(
                        pg[:, :],
                        AP(kact[0:1, 0:1].tensor, kc * 512 + ti * 128,
                           [[NCH * 512, 128], [1, 128]]),
                        sb["WT_Wv2"][:, kc * D + dh * 512: kc * D + dh * 512 + 512],
                        start=(kc == 0), stop=(kc == NCH - 1))),
                         reads=["kact", "WT_Wv2"], writes=["pg3"], no_inc=(kc < NCH - 1))
                P.op("vector", (lambda e, dh=dh, pg=pg: e.tensor_mul(
                    sb["b_sb"][:, :], pg[:, :], sb["r_sb"][:, dh * 512:dh * 512 + 512])),
                     reads=["pg3", "r_sb"], writes=["b_sb"])
                P.op("vector", (lambda e, dh=dh, xf=xf: e.tensor_add(
                    sb["xB_sb"][:, dh * 512:dh * 512 + 512], sb["b_sb"][:, :],
                    xf[:, dh * 512:dh * 512 + 512])),
                     reads=["b_sb", xft], writes=["xB_sb"])
            P.dma("sy", xB[gi, :, :], sb["xB_sb"][:, :], reads=["xB_sb"], writes=["xB_sb_dma"])
            for dh in range(2):
                pg = sb["pg%d" % (dh % 2)]
                pgt = "pg%d" % (dh % 2)
                for kc in range(NCH):
                    P.op("tensor", (lambda e, ti=ti, dh=dh, kc=kc, pg=pg: e.matmul(
                        pg[:, :],
                        AP(xnbf[0:1, 0:1].tensor, kc * 512 + ti * 128,
                           [[NCH * 512, 128], [1, 128]]),
                        sb["WT_Wv"][:, kc * D + dh * 512: kc * D + dh * 512 + 512],
                        start=(kc == 0), stop=(kc == NCH - 1))),
                         reads=["xnbf", "WT_Wv"], writes=[pgt], no_inc=(kc < NCH - 1))
                vsb = sb["v_sb%d" % (ti % 2)]
                P.op("vector", (lambda e, dh=dh, pg=pg, vsb=vsb: e.tensor_copy(
                    AP(vsb[0:1, 0:1].tensor, dh * 32, [[VD, 128], [1, 32], [65, 16]]),
                    AP(pg[0:1, 0:1].tensor, 0, [[512, 128], [16, 32], [1, 16]]))),
                     reads=[pgt], writes=["v_sb%d" % (ti % 2)])
            P.dma("sy", v0[gi, :, :], sb["v_sb%d" % (ti % 2)][:, :],
                  reads=["v_sb%d" % (ti % 2)], writes=["v_dma%d" % (ti % 2)])

        for (wname, dst_fm, ss_dst) in (("Wq", q0, ssq0), ("Wk", k0, ssk0)):
            for dc in range(NCH):
                pg = sb["pg%d" % (2 + dc % 2)]
                pgt = "pg%d" % (2 + dc % 2)
                for kc in range(NCH):
                    P.op("tensor", (lambda e, wname=wname, dc=dc, kc=kc, pg=pg: e.matmul(
                        pg[:, :],
                        sb["WT_" + wname][:, kc * D + dc * 128: kc * D + dc * 128 + 128],
                        AP(xnbf[0:1, 0:1].tensor, kc * 512, [[NCH * 512, 128], [1, 512]]),
                        start=(kc == 0), stop=(kc == NCH - 1))),
                         reads=["WT_" + wname, "xnbf"], writes=[pgt], no_inc=(kc < NCH - 1))
                qsb = sb["q_sb%d" % (dc % 2)]
                P.op("scalar", (lambda e, pg=pg, qsb=qsb: e.copy(qsb[:, :], pg[:, :])),
                     reads=[pgt], writes=["q_sb%d" % (dc % 2)])
                P.dma("sy", AP(dst_fm.tensor, (dc * 128) * T_LOC + g * 512,
                               [[T_LOC, 128], [1, 512]]),
                      qsb[:, :], reads=["q_sb%d" % (dc % 2)], writes=["qdma%d" % (dc % 2)])
                P.op("scalar", (lambda e, pg=pg, b=fconst(P, sb, 0.0): e.activation(
                    sb["q2"][:, :], pg[:, :], AF.Square, bias=b)),
                     reads=[pgt, "cpool"], writes=["q2"])
                P.op("tensor", (lambda e, dc=dc: e.matmul(
                    sb["pss"][0:16, :], sb["c_headmask"][:, dc * 16:dc * 16 + 16],
                    sb["q2"][:, :], start=(dc == 0), stop=(dc == NCH - 1))),
                     reads=["c_headmask", "q2"], writes=["pss"], no_inc=(dc < NCH - 1))
            P.op("vector", (lambda e: e.tensor_copy(sb["ss_sb"][0:16, :], sb["pss"][0:16, :])),
                 reads=["pss"], writes=["ss_sb"])
            P.dma("sy", AP(ss_dst.tensor, g * 512, [[T_LOC, 16], [1, 512]]),
                  sb["ss_sb"][0:16, :], reads=["ss_sb"], writes=["ss_dma"])

    for g in range(16):
        emit_group(g)

    emit_stats_tiles(P, sb, x, range(4))
    emit_stats_tail(P, sb, "s_nm", "s_rs", LN_EPS0)
    emit_group(0)


def emit_stage2(P, dr, sb):
    P.op("gpsimd", (lambda e: e.memset(sb["onesq"][:, :], 1.0)), writes=["onesq"])
    P.dma("g", dr["Wo_bf"].ap(), dr["Wo"].ap(), writes=["Wo_bf"])
    for kc in range(NCH):
        P.dma("sc", sb["WT_Wo"][:, kc * D:(kc + 1) * D],
              dr["Wo_bf"].ap()[:, kc * 128:(kc + 1) * 128],
              reads=["Wo_bf"], writes=["WT_Wo"], transpose=True)
    emit_load_consts(P, dr, sb, ["c_mask01", "c_idf32", "c_onesrow", "gamma_bc", "beta_bc"])
    fc0 = fconst(P, sb, 0.0)

    q0 = dr["q0_fm"].ap()
    k0 = dr["k0_fm"].ap()
    v0 = dr["v0_tm"].ap()
    xB = dr["xB"].ap().rearrange("(n p) d -> n p d", p=128)
    x1 = dr["x1"].ap().rearrange("(n p) d -> n p d", p=128)

    Vs, O32 = sb["Vs"], sb["O32"]

    for b in range(B_LOC):
        for vch in range(4):
            P.dma("g", Vs[:, vch * 4 * VD:(vch + 1) * 4 * VD],
                  AP(v0.tensor, (b * S + vch * 4 * 128) * VD,
                     [[VD, 128], [128 * VD, 4], [1, VD]]),
                  writes=["Vs%d" % vch])
        for (nm, dst0) in (("ssq0", 0), ("ssk0", 1)):
            for c4 in range(4):
                P.dma("g", sb["recx"][0:16, 0:512],
                      AP(dr[nm].ap().tensor, b * S + c4 * 512, [[T_LOC, 16], [1, 512]]),
                      writes=["recx"])
                P.op("vector", (lambda e, c4=c4: e.reduce_max(
                    sb["mx2"][0:16, 8 + c4:9 + c4], sb["recx"][0:16, 0:512], axis=AX.X)),
                     reads=["recx"], writes=["mx2p"])
            P.op("vector", (lambda e, dst0=dst0: e.reduce_max(
                sb["mx2"][0:16, dst0:dst0 + 1], sb["mx2"][0:16, 8:12], axis=AX.X)),
                 reads=["mx2p"], writes=["mx2"])
        P.op("vector", (lambda e: e.tensor_mul(sb["mx2"][0:16, 2:3], sb["mx2"][0:16, 0:1],
                                               sb["mx2"][0:16, 1:2])),
             reads=["mx2"], writes=["mx2b"])
        P.op("scalar", (lambda e: e.activation(sb["mx2"][0:16, 3:4], sb["mx2"][0:16, 2:3],
                                               AF.Ln, bias=fc0[0:16, :])),
             reads=["mx2b", "cpool"], writes=["mx2c"])
        P.op("scalar", (lambda e: e.activation(sb["mx2"][0:16, 4:5], sb["mx2"][0:16, 3:4],
                                               AF.Exp, scale=0.5, bias=fc0[0:16, :])),
             reads=["mx2c", "cpool"], writes=["mx2d"])
        P.op("vector", (lambda e: e.tensor_scalar(sb["mx2"][0:16, 5:6], sb["mx2"][0:16, 4:5],
                                                  -0.127551, None, ALU.mult)),
             reads=["mx2d"], writes=["mx2e"])
        P.op("tensor", (lambda e: e.transpose(sb["pr"][0:1, 0:16], sb["mx2"][0:16, 5:6],
                                              sb["c_idf32"][0:16, 0:16])),
             reads=["mx2e", "c_idf32"], writes=["prA"])
        P.op("vector", (lambda e: e.tensor_copy(sb["negC_row"][0:1, 0:16], sb["pr"][0:1, 0:16])),
             reads=["prA"], writes=["negC_row"])
        P.op("tensor", (lambda e: e.matmul(sb["pr"][:, 0:16], sb["c_onesrow"][0:1, :],
                                           sb["negC_row"][0:1, 0:16], start=True, stop=True)),
             reads=["c_onesrow", "negC_row"], writes=["prA"])
        P.op("vector", (lambda e: e.tensor_copy(sb["Cb"][:, :], sb["pr"][:, 0:16])),
             reads=["prA"], writes=["Cb"])

        for hp in range(NCH):
            hA, hB = 2 * hp, 2 * hp + 1
            for (tb, src) in (("Qhr", q0), ("Khr", k0)):
                t = "%s%d" % (tb, hp % 2)
                P.dma("g", sb[t][:, :],
                      AP(src.tensor, (hp * 128) * T_LOC + b * S, [[T_LOC, 128], [1, S]]),
                      writes=[t])
            Qhr = sb["Qhr%d" % (hp % 2)]
            Khr = sb["Khr%d" % (hp % 2)]
            qt = "Qhr%d" % (hp % 2)
            kt_t = "Khr%d" % (hp % 2)
            for qc in range(4):
                nkt = 4 * qc + 4

                def s2_qk(kt, qc=qc, Qhr=Qhr, Khr=Khr, qt=qt, kt_t=kt_t):
                    qoff = max(0, (kt - 4 * qc) * 128)
                    w = 512 - qoff
                    par = kt % 2
                    for (hh, pL, base, tp) in ((hA, "pL%dA" % par, 0, (0, 0)),
                                               (hB, "pL%dB" % par, 64, (64, 0))):
                        P.op("tensor", (lambda e, kt=kt, qc=qc, qoff=qoff, w=w,
                                        pL=pL, base=base, tp=tp,
                                        Qhr=Qhr, Khr=Khr: e.matmul(
                            sb[pL][:, qoff:512],
                            AP(Khr[0:1, 0:1].tensor, base * S + kt * 128,
                               [[S, 64], [1, 128]]),
                            AP(Qhr[0:1, 0:1].tensor, base * S + qc * 512 + qoff,
                               [[S, 64], [1, w]]),
                            start=True, stop=True, tile_position=tp)),
                             reads=[qt, kt_t], writes=[pL])

                def s2_post(kt, qc=qc, nkt=nkt, hA=hA, hB=hB):
                    qoff = max(0, (kt - 4 * qc) * 128)
                    par = kt % 2
                    for (hh, pL, pt) in ((hA, "pL%dA" % par, "ptA%d" % par),
                                         (hB, "pL%dB" % par, "ptB%d" % par)):
                        P.op("scalar", (lambda e, pL=pL, pt=pt, qoff=qoff, hh=hh: e.activation(
                            sb[pt][:, qoff:512], sb[pL][:, qoff:512], AF.Exp,
                            scale=0.125, bias=sb["Cb"][:, hh:hh + 1])),
                             reads=[pL, "Cb"], writes=[pt])
                        if kt >= 4 * qc:
                            P.op("vector", (lambda e, pt=pt, qoff=qoff: e.tensor_tensor(
                                sb[pt][:, qoff:qoff + 128], sb[pt][:, qoff:qoff + 128],
                                sb["c_mask01"][:, 128:256], ALU.mult)),
                                 reads=["c_mask01", pt], writes=[pt])
                    first, last = (kt == 0), (kt == nkt - 1)
                    for (hh, pt, po) in ((hA, "ptA%d" % par, "poA"),
                                         (hB, "ptB%d" % par, "poB")):
                        P.op("tensor", (lambda e, hh=hh, pt=pt, po=po, kt=kt,
                                        qoff=qoff, first=first, last=last: e.matmul(
                            sb[po][0:65, qoff:512],
                            AP(Vs[0:1, 0:1].tensor, kt * VD + hh * 65,
                               [[16 * VD, 128], [1, 65]]),
                            sb[pt][:, qoff:512],
                            start=first, stop=last)),
                             reads=["Vs%d" % (kt // 4), pt], writes=[po],
                             no_inc=not last)

                for kt in range(nkt):
                    s2_qk(kt)
                    if kt > 0:
                        s2_post(kt - 1)
                s2_post(nkt - 1)
                p2 = qc % 2
                sgA, sgAt = sb["sgA%d" % p2], "sgA%d" % p2
                sgB, sgBt = sb["sgB%d" % p2], "sgB%d" % p2
                dn, dnt = sb["dn%d" % p2], "dn%d" % p2
                st2, st2t = sb["stg2"], "stg2"
                P.op("vector", (lambda e, sgA=sgA: e.tensor_copy(
                    sgA[0:65, 0:512], sb["poA"][0:65, 0:512])),
                     reads=["poA"], writes=[sgAt])
                P.op("scalar", (lambda e, sgB=sgB: e.copy(
                    sgB[0:65, 0:512], sb["poB"][0:65, 0:512])),
                     reads=["poB"], writes=[sgBt])
                P.op("vector", (lambda e, dn=dn, sgA=sgA: e.reciprocal(
                    dn[64:65, 0:512], sgA[64:65, 0:512])),
                     reads=[sgAt], writes=[dnt + "A"])
                P.op("vector", (lambda e, dn=dn, sgB=sgB: e.reciprocal(
                    dn[64:65, 512:1024], sgB[64:65, 0:512])),
                     reads=[sgBt], writes=[dnt + "B"])
                P.op("tensor", (lambda e, dn=dn: e.matmul(
                    sb["pr"][0:64, 0:512], sb["onesq"][64:65, 0:64],
                    dn[64:65, 0:512],
                    start=True, stop=True, tile_position=(64, 0))),
                     reads=[dnt + "A", "onesq"], writes=["prA"])
                P.op("tensor", (lambda e, dn=dn: e.matmul(
                    sb["pr"][0:64, 512:1024], sb["onesq"][64:65, 0:64],
                    dn[64:65, 512:1024],
                    start=True, stop=True, tile_position=(64, 0))),
                     reads=[dnt + "B", "onesq"], writes=["pmv"])
                P.op("vector", (lambda e, sgA=sgA, hp=hp, qc=qc: e.tensor_tensor(
                    AP(O32[0:1, 0:1].tensor, hp * S + qc * 512,
                       [[NCH * S, 64], [1, 512]]),
                    sgA[0:64, 0:512], sb["pr"][0:64, 0:512], ALU.mult)),
                     reads=[sgAt, "prA"], writes=["O32"])
                P.op("vector", (lambda e, sgB=sgB, st2=st2: e.tensor_tensor(
                    st2[0:64, 0:512], sgB[0:64, 0:512],
                    sb["pr"][0:64, 512:1024], ALU.mult)),
                     reads=[sgBt, "pmv"], writes=[st2t])
                P.op("tensor", (lambda e, st2=st2: e.matmul(
                    sb["pr"][64:128, 0:512], sb["c_idf32"][0:64, 0:64],
                    st2[0:64, 0:512],
                    start=True, stop=True, tile_position=(0, 64))),
                     reads=[st2t, "c_idf32"], writes=["prA"])
                P.op("vector", (lambda e, hp=hp, qc=qc: e.tensor_copy(
                    AP(O32[0:1, 0:1].tensor, 64 * (NCH * S) + hp * S + qc * 512,
                       [[NCH * S, 64], [1, 512]]),
                    sb["pr"][64:128, 0:512])),
                     reads=["prA"], writes=["O32"])

        for g in range(4):
            for ti in range(4):
                tt = 4 * g + ti
                for kc in range(NCH):
                    slot = kc % 4
                    bankn = "pL%d%s" % (slot % 2, "A" if slot < 2 else "B")
                    pslc = sb[bankn][:, (kc // 4) * 128 + (slot // 2) * 256:][:, 0:128]
                    P.op("tensor", (lambda e, kc=kc, tt=tt, pslc=pslc: e.transpose(
                        pslc,
                        AP(O32[0:1, 0:1].tensor, kc * S + tt * 128, [[NCH * S, 128], [1, 128]]),
                        sb["c_idf32"][:, :])),
                         reads=["O32", "c_idf32"], writes=[bankn])
                    P.op("vector", (lambda e, kc=kc, pslc=pslc: e.tensor_copy(
                        sb["otm"][:, kc * 128:(kc + 1) * 128], pslc)),
                         reads=[bankn], writes=["otm"])
                P.op("vector", (lambda e: e.bn_stats(sb["bn6"][:, 0:6], sb["otm"][:, 0:512])),
                     reads=["otm"], writes=["bn6"])
                P.op("vector", (lambda e: e.bn_stats(sb["bn6"][:, 6:12], sb["otm"][:, 512:1024])),
                     reads=["otm"], writes=["bn6"])
                P.op("vector", (lambda e: e.bn_aggr(sb["mv"][:, 0:2], sb["bn6"][:, :])),
                     reads=["bn6"], writes=["mv"])
                P.op("vector", (lambda e: e.tensor_scalar(sb["mv"][:, 2:3], sb["mv"][:, 0:1],
                                                          -1.0, None, ALU.mult)),
                     reads=["mv"], writes=["mv2"])
                P.op("scalar", (lambda e, bb=fconst(P, sb, LN_EPS0): e.activation(
                    sb["mv"][:, 3:4], sb["mv"][:, 1:2], AF.Ln, bias=bb)),
                     reads=["mv", "cpool"], writes=["mv3"])
                P.op("scalar", (lambda e: e.activation(sb["mv"][:, 4:5], sb["mv"][:, 3:4],
                                                       AF.Exp, scale=-0.5, bias=fc0)),
                     reads=["mv3", "cpool"], writes=["mv4"])
                P.op("vector", (lambda e: e.tensor_scalar(
                    sb["otm"][:, :], sb["otm"][:, :], sb["mv"][:, 2:3], sb["mv"][:, 4:5],
                    ALU.add, ALU.mult)),
                     reads=["mv2", "mv4"], writes=["otm"])
                P.op("vector", (lambda e: e.tensor_mul(sb["otm"][:, :], sb["otm"][:, :],
                                                       sb["gamma_bc"][:, :])),
                     reads=["gamma_bc"], writes=["otm"])
                P.op("vector", (lambda e: e.tensor_tensor(sb["lnbf"][:, :], sb["otm"][:, :],
                                                          sb["beta_bc"][:, :], ALU.add)),
                     reads=["otm", "beta_bc"], writes=["lnbf"])
                for kc in range(NCH):
                    P.dma("sc", AP(sb["lnfm"][0:1, 0:1].tensor, kc * 512 + ti * 128,
                                   [[NCH * 512, 128], [1, 128]]),
                          sb["lnbf"][:, kc * 128:(kc + 1) * 128],
                          reads=["lnbf"], writes=["lnfm"], transpose=True)
            for ti in range(4):
                tt = 4 * g + ti
                gi = 16 * b + tt
                xbt = sb["otm"]
                P.dma("g", xbt[:, :], xB[gi, :, :], writes=["otm"])
                for dh in range(2):
                    po = "poA" if dh == 0 else "poB"
                    for kc in range(NCH):
                        P.op("tensor", (lambda e, ti=ti, dh=dh, kc=kc, po=po: e.matmul(
                            sb[po][:, :],
                            AP(sb["lnfm"][0:1, 0:1].tensor, kc * 512 + ti * 128,
                               [[NCH * 512, 128], [1, 128]]),
                            sb["WT_Wo"][:, kc * D + dh * 512: kc * D + dh * 512 + 512],
                            start=(kc == 0), stop=(kc == NCH - 1))),
                             reads=["lnfm", "WT_Wo"], writes=[po], no_inc=(kc < NCH - 1))
                    P.op("vector", (lambda e, dh=dh, po=po, xbt=xbt: e.tensor_add(
                        xbt[:, dh * 512:(dh + 1) * 512], sb[po][:, :],
                        xbt[:, dh * 512:(dh + 1) * 512])),
                         reads=[po], writes=["otm"])
                P.dma("sy", x1[gi, :, :], sb["otm"][:, :], reads=["otm"], writes=["x1dma"])


def emit_stage3(P, dr, sb):
    fc0 = fconst(P, sb, 0.0)
    emit_load_consts(P, dr, sb, ["c_idf32", "c_headmask", "ln1_bc", "scale128"])
    for nm in ("Wq1", "Wk1", "Wv1"):
        P.dma("g", sb[nm][:, :].rearrange("p (kc d) -> p kc d", kc=NCH),
              dr[nm].ap().rearrange("(kc p) d -> p kc d", p=128), writes=[nm])
    P.dma("g", sb["Wg"][:, :].rearrange("p (kc d) -> p kc d", kc=NCH),
          dr["Wg"].ap().rearrange("(kc p) d -> p kc d", p=128), writes=["Wg"])
    for t in ("v_sb0", "v_sb1"):
        P.op("gpsimd", (lambda e, t=t: e.memset(
            AP(sb[t][0:1, 0:1].tensor, 64, [[VD, 128], [65, 16]]), 1.0)), writes=[t])

    emit_stats_pass(P, dr, sb, "x1", "s_nm", "s_rs", LN_EPS0,
                    inner_name="s_ri", inner_eps=RMS_EPS)

    x = dr["x1"].ap().rearrange("(n p) d -> n p d", p=128)
    xn32d = dr["xn32"].ap().rearrange("(n p) d -> n p d", p=128)
    v1 = dr["v1_tm"].ap().rearrange("(n p) d -> n p d", p=128)
    q1 = dr["q1_fm"].ap()
    k1 = dr["k1_fm"].ap()
    xnfmd = dr["xn_fm_bf"].ap()
    probsd = dr["probs"].ap().rearrange("(n p) d -> n p d", p=128)
    pTd = dr["pT"].ap()

    xnfm32 = sb["xn_fm32"]
    xnfmbf = sb["xn_fm_bf"]
    xnifm = sb["xni_fm"]

    for g in range(16):
        for ti in range(4):
            gi = 4 * g + ti
            xs = sb["xs%d" % (ti % 2)]
            xst = "xs%d" % (ti % 2)
            P.dma("g", xs[:, :], x[gi, :, :], writes=[xst])
            P.op("vector", (lambda e, xs=xs, gi=gi: e.tensor_scalar(
                sb["xn_tm"][:, :], xs[:, :],
                sb["s_nm"][:, gi:gi + 1], sb["s_rs"][:, gi:gi + 1],
                ALU.add, ALU.mult)),
                 reads=[xst, "s_nm", "s_rs"], writes=["xn_tm"])
            P.dma("sy", xn32d[gi, :, :], sb["xn_tm"][:, :], reads=["xn_tm"], writes=["xndma"])
            P.op("vector", (lambda e, gi=gi: e.tensor_scalar(
                sb["xni_t"][:, :], sb["xn_tm"][:, :], sb["s_ri"][:, gi:gi + 1], None,
                ALU.mult)),
                 reads=["xn_tm", "s_ri"], writes=["xni_t"])
            P.op("vector", (lambda e: e.tensor_tensor(
                sb["xni_bf"][:, :], sb["xni_t"][:, :], sb["ln1_bc"][:, :], ALU.mult)),
                 reads=["xni_t", "ln1_bc"], writes=["xni_bf"])
            for kc in range(NCH):
                P.dma("sc", AP(xnifm[0:1, 0:1].tensor, kc * 512 + ti * 128,
                               [[NCH * 512, 128], [1, 128]]),
                      sb["xni_bf"][:, kc * 128:(kc + 1) * 128],
                      reads=["xni_bf"], writes=["xni_fm"], transpose=True)
            for kc in range(NCH):
                pslc = sb["ptf"][:, kc * 128:(kc + 1) * 128]
                P.op("tensor", (lambda e, kc=kc, pslc=pslc: e.transpose(
                    pslc, sb["xn_tm"][:, kc * 128:(kc + 1) * 128], sb["c_idf32"][:, :])),
                     reads=["xn_tm", "c_idf32"],
                     writes=["ptf%d" % kc, "ptfB%d" % (kc // 4)])
                P.op("vector", (lambda e, kc=kc, ti=ti, pslc=pslc: e.tensor_copy(
                    AP(xnfm32[0:1, 0:1].tensor, kc * 512 + ti * 128,
                       [[NCH * 512, 128], [1, 128]]), pslc)),
                     reads=["ptf%d" % kc], writes=["xn_fm32", "ptfB%d" % (kc // 4)])
                P.op("scalar", (lambda e, kc=kc, ti=ti, pslc=pslc: e.copy(
                    AP(xnfmbf[0:1, 0:1].tensor, kc * 512 + ti * 128,
                       [[NCH * 512, 128], [1, 128]]), pslc)),
                     reads=["ptf%d" % kc], writes=["xn_fm_bf", "ptfB%d" % (kc // 4)])
            for kc in range(NCH):
                P.op("tensor", (lambda e, kc=kc, ti=ti: e.matmul(
                    sb["pssq"][:, 0:8],
                    AP(xnfm32[0:1, 0:1].tensor, kc * 512 + ti * 128,
                       [[NCH * 512, 128], [1, 128]]),
                    sb["Wg"][:, kc * 8:(kc + 1) * 8],
                    start=(kc == 0), stop=(kc == NCH - 1))),
                     reads=["xn_fm32", "Wg"], writes=["pssq"], no_inc=(kc < NCH - 1))
            P.op("vector", (lambda e: e.tensor_copy(sb["lg"][:, 0:8], sb["pssq"][:, 0:8])),
                 reads=["pssq"], writes=["lg"])
            P.op("vector", (lambda e: e.reduce_max(sb["lg"][:, 8:9], sb["lg"][:, 0:8],
                                                   axis=AX.X)),
                 reads=["lg"], writes=["lgmx"])
            P.op("vector", (lambda e: e.tensor_scalar(sb["lg"][:, 9:10], sb["lg"][:, 8:9],
                                                      -1.0, None, ALU.mult)),
                 reads=["lgmx"], writes=["lgnm"])
            P.op("scalar", (lambda e: e.activation(sb["lg"][:, 10:18], sb["lg"][:, 0:8],
                                                   AF.Exp, bias=sb["lg"][:, 9:10])),
                 reads=["lg", "lgnm"], writes=["lge"])
            P.op("vector", (lambda e: e.reduce_sum(sb["lg"][:, 18:19], sb["lg"][:, 10:18],
                                                   axis=AX.X)),
                 reads=["lge"], writes=["lgs"])
            P.op("vector", (lambda e: e.reciprocal(sb["lg"][:, 19:20], sb["lg"][:, 18:19])),
                 reads=["lgs"], writes=["lgr"])
            P.op("vector", (lambda e: e.tensor_scalar(
                sb["probs_t"][:, 0:8], sb["lg"][:, 10:18], sb["lg"][:, 19:20], None,
                ALU.mult)),
                 reads=["lge", "lgr"], writes=["probs_t"])
            P.dma("sy", probsd[gi, :, :], sb["probs_t"][:, 0:8],
                  reads=["probs_t"], writes=["probsdma"])
            P.op("tensor", (lambda e: e.transpose(
                sb["pssq"][0:8, 8:136], sb["probs_t"][:, 0:8], sb["c_idf32"][:, :])),
                 reads=["probs_t", "c_idf32"], writes=["pssq"])
            P.op("vector", (lambda e: e.tensor_copy(sb["pT_t"][0:8, 0:128],
                                                    sb["pssq"][0:8, 8:136])),
                 reads=["pssq"], writes=["pT_t"])
            P.dma("sy", AP(pTd.tensor, gi * 128, [[T_LOC, 8], [1, 128]]),
                  sb["pT_t"][0:8, 0:128], reads=["pT_t"], writes=["pTdma"])

        P.dma("sy", AP(xnfmd.tensor, g * 512, [[T_LOC, 128], [128 * T_LOC, NCH], [1, 512]]),
              AP(xnfmbf[0:1, 0:1].tensor, 0, [[NCH * 512, 128], [512, NCH], [1, 512]]),
              reads=["xn_fm_bf"], writes=["xnfmdma"])

        for (wname, dst_fm, ss_t) in (("Wq1", q1, "pssq"), ("Wk1", k1, "pssk")):
            for dc in range(NCH):
                pg = sb["pg%d" % (dc % 2)]
                pgt = "pg%d" % (dc % 2)
                for kc in range(NCH):
                    P.op("tensor", (lambda e, wname=wname, dc=dc, kc=kc, pg=pg: e.matmul(
                        pg[:, :],
                        sb[wname][:, kc * D + dc * 128: kc * D + dc * 128 + 128],
                        AP(xnifm[0:1, 0:1].tensor, kc * 512, [[NCH * 512, 128], [1, 512]]),
                        start=(kc == 0), stop=(kc == NCH - 1))),
                         reads=[wname, "xni_fm"], writes=[pgt], no_inc=(kc < NCH - 1))
                qsb = sb["q_sb%d" % (dc % 2)]
                P.op("scalar", (lambda e, pg=pg, qsb=qsb: e.copy(qsb[:, :], pg[:, :])),
                     reads=[pgt], writes=["q_sb%d" % (dc % 2)])
                P.dma("sy", AP(dst_fm.tensor, (dc * 128) * T_LOC + g * 512,
                               [[T_LOC, 128], [1, 512]]),
                      qsb[:, :], reads=["q_sb%d" % (dc % 2)], writes=["qdma%d" % (dc % 2)])
                P.op("scalar", (lambda e, pg=pg, b=fc0: e.activation(
                    sb["q2"][:, :], pg[:, :], AF.Square, bias=b)),
                     reads=[pgt, "cpool"], writes=["q2"])
                P.op("tensor", (lambda e, dc=dc, ss_t=ss_t: e.matmul(
                    sb[ss_t][0:16, :], sb["c_headmask"][:, dc * 16:dc * 16 + 16],
                    sb["q2"][:, :], start=(dc == 0), stop=(dc == NCH - 1))),
                     reads=["c_headmask", "q2"], writes=[ss_t], no_inc=(dc < NCH - 1))
            P.op("vector", (lambda e, ss_t=ss_t: e.tensor_copy(
                sb["ss_sb"][0:16, :], sb[ss_t][0:16, :])),
                 reads=[ss_t], writes=["ss_sb"])
            ssd = dr["ssq1"] if wname == "Wq1" else dr["ssk1"]
            P.dma("sy", AP(ssd.ap().tensor, g * 512, [[T_LOC, 16], [1, 512]]),
                  sb["ss_sb"][0:16, :], reads=["ss_sb"], writes=["ss_dma"])
        for ti in range(4):
            gi = 4 * g + ti
            for dh in range(2):
                pg = sb["pg%d" % (2 + dh % 2)]
                pgt = "pg%d" % (2 + dh % 2)
                for kc in range(NCH):
                    P.op("tensor", (lambda e, ti=ti, dh=dh, kc=kc, pg=pg: e.matmul(
                        pg[:, :],
                        AP(xnifm[0:1, 0:1].tensor, kc * 512 + ti * 128,
                           [[NCH * 512, 128], [1, 128]]),
                        sb["Wv1"][:, kc * D + dh * 512: kc * D + dh * 512 + 512],
                        start=(kc == 0), stop=(kc == NCH - 1))),
                         reads=["xni_fm", "Wv1"], writes=[pgt], no_inc=(kc < NCH - 1))
                vsb = sb["v_sb%d" % (ti % 2)]
                P.op("vector", (lambda e, dh=dh, pg=pg, vsb=vsb: e.tensor_copy(
                    AP(vsb[0:1, 0:1].tensor, dh * 8 * 65, [[VD, 128], [65, 8], [1, 64]]),
                    AP(pg[0:1, 0:1].tensor, 0, [[512, 128], [64, 8], [1, 64]]))),
                     reads=[pgt], writes=["v_sb%d" % (ti % 2)])
            P.dma("sy", v1[gi, :, :], sb["v_sb%d" % (ti % 2)][:, :],
                  reads=["v_sb%d" % (ti % 2)], writes=["v_dma%d" % (ti % 2)])

    for (ssn, out_n, fold_scale) in (("ssq1", "sqv", True), ("ssk1", "skv", False)):
        P.dma("g", sb["ssl"][:, :], dr[ssn].ap().rearrange("h (c t) -> (h c) t", c=8),
              reads=["ss_dma"], writes=["ssl"])
        P.op("scalar", (lambda e, b=fconst(P, sb, KQ_EPS): e.activation(
            sb["ssl2"][:, :], sb["ssl"][:, :], AF.Ln, bias=b)),
             reads=["ssl", "cpool"], writes=["ssl2"])
        P.op("scalar", (lambda e, b=fc0: e.activation(
            sb["ssl"][:, :], sb["ssl2"][:, :], AF.Exp, scale=-0.5, bias=b)),
             reads=["ssl2", "cpool"], writes=["ssl"])
        if fold_scale:
            P.op("vector", (lambda e: e.tensor_scalar(
                sb["ssl"][:, :], sb["ssl"][:, :], sb["scale128"][:, 0:1], None,
                ALU.mult)),
                 reads=["scale128"], writes=["ssl"])
        P.dma("sy", dr[out_n].ap().rearrange("h (c t) -> (h c) t", c=8),
              sb["ssl"][:, :], reads=["ssl"], writes=["ssodma"])


N_BISECT = 42


def emit_stage4g(P, dr, sb, eng="vector", collective=False):
    fcBIG = 1e30
    emit_load_consts(P, dr, sb, ["c_iota8", "c_iotaB", "c_onesrow", "c_p2e"])
    if collective:
        rg = [list(range(N_CORES))]
        P.op("gpsimd", (lambda e: e.collective_compute(
            "AllGather", ALU.bypass, replica_groups=rg,
            ins=[dr["pT"].ap()], outs=[dr["probs_gathered"].ap()])),
             reads=["pTdma"], writes=["pgath"], inc_amt=16)
    P.dma("g", sb["G"][0:64, :], dr["probs_gathered"].ap()[:, 0:4096],
          reads=["pgath"], writes=["G"])
    P.dma("g", sb["G"][64:128, :], dr["probs_gathered"].ap()[:, 4096:8192],
          reads=["pgath"], writes=["G"])
    P.op(eng, (lambda e: e.memset(sb["bis"][0:1, 0:8], 0.0)), writes=["loE"])
    P.op(eng, (lambda e: e.memset(sb["bis"][0:1, 8:16], 1.0)), writes=["hiE"])
    for it in range(N_BISECT):
        P.op(eng, (lambda e: e.tensor_add(sb["bis"][0:1, 16:24], sb["bis"][0:1, 0:8],
                                          sb["bis"][0:1, 8:16])),
             reads=["loE", "hiE"], writes=["midE"])
        P.op(eng, (lambda e: e.tensor_scalar(sb["bis"][0:1, 16:24], sb["bis"][0:1, 16:24],
                                             0.5, None, ALU.mult)),
             reads=[], writes=["midE"])
        P.op(eng, (lambda e: e.tensor_copy(
            AP(sb["bis"][0:1, 0:1].tensor, 32, [[192, 1], [1, 8], [64, 2], [8, 8]]),
            AP(sb["bis"][0:1, 0:1].tensor, 16, [[192, 1], [1, 8], [0, 2], [0, 8]]))),
             reads=["midE"], writes=["midrow"])
        P.dma("sy", dr["mid128"].ap(), sb["bis"][0:1, 32:160],
              reads=["midrow"], writes=["mid128d"])
        P.dma("g", sb["mid_pp"][:, 0:1],
              dr["mid128"].ap().rearrange("(p one) -> p one", one=1),
              reads=["mid128d"], writes=["mid_pp"])
        P.op(eng, (lambda e: e.tensor_scalar(sb["cmpb"][:, :], sb["G"][:, :],
                                             sb["mid_pp"][:, 0:1], None, ALU.is_gt)),
             reads=["G", "mid_pp"], writes=["cmpb"])
        P.op(eng, (lambda e: e.reduce_sum(sb["cnt_pp"][:, 0:1], sb["cmpb"][:, :],
                                          axis=AX.X)),
             reads=["cmpb"], writes=["cnt_pp"])
        P.dma("sy", dr["cnt128"].ap(), sb["cnt_pp"][:, 0:1],
              reads=["cnt_pp"], writes=["cnt128d"])
        P.dma("g", sb["bis"][0:1, 32:160],
              dr["cnt128"].ap().rearrange("(one p) -> one p", one=1),
              reads=["cnt128d"], writes=["cntrow"])
        P.op(eng, (lambda e: e.tensor_reduce(
            AP(sb["bis"][0:1, 0:1].tensor, 24, [[192, 1], [1, 8], [1, 1], [1, 1]]),
            AP(sb["bis"][0:1, 0:1].tensor, 32, [[192, 1], [1, 8], [64, 2], [8, 8]]),
            AX.XY, ALU.add)),
             reads=["cntrow"], writes=["cntE"])
        P.op(eng, (lambda e: e.tensor_scalar(sb["bis"][0:1, 160:168], sb["bis"][0:1, 24:32],
                                             float(CAP), None, ALU.is_ge)),
             reads=["cntE"], writes=["selE"])
        P.op(eng, (lambda e: e.tensor_sub(sb["bis"][0:1, 168:176], sb["bis"][0:1, 16:24],
                                          sb["bis"][0:1, 0:8])),
             reads=["midE", "loE"], writes=["dml"])
        P.op(eng, (lambda e: e.tensor_mul(sb["bis"][0:1, 168:176], sb["bis"][0:1, 168:176],
                                          sb["bis"][0:1, 160:168])),
             reads=["selE", "dml"], writes=["dml"])
        P.op(eng, (lambda e: e.tensor_add(sb["bis"][0:1, 0:8], sb["bis"][0:1, 0:8],
                                          sb["bis"][0:1, 168:176])),
             reads=["dml", "loE"], writes=["loE"])
        P.op(eng, (lambda e: e.tensor_sub(sb["bis"][0:1, 168:176], sb["bis"][0:1, 8:16],
                                          sb["bis"][0:1, 16:24])),
             reads=["hiE", "midE"], writes=["dhm"])
        P.op(eng, (lambda e: e.tensor_mul(sb["bis"][0:1, 168:176], sb["bis"][0:1, 168:176],
                                          sb["bis"][0:1, 160:168])),
             reads=["selE", "dhm"], writes=["dhm"])
        P.op(eng, (lambda e: e.tensor_add(sb["bis"][0:1, 8:16], sb["bis"][0:1, 16:24],
                                          sb["bis"][0:1, 168:176])),
             reads=["dhm", "midE"], writes=["hiE"])
    P.dma("sy", dr["thr8"].ap(), sb["bis"][0:1, 0:8], reads=["loE"], writes=["thr8d"])
    P.dma("g", sb["thr_bc"][:, :],
          AP(dr["thr8"].ap().tensor, 0, [[0, 128], [1, 8]]),
          reads=["thr8d"], writes=["thr_bc"])

    probs = dr["probs"].ap().rearrange("(n p) d -> n p d", p=128)
    gates = dr["gates"].ap().rearrange("(n p) d -> n p d", p=128)
    g8 = sb["g8"]
    for tile in range(NTILE):
        P.dma("g", g8[:, 0:8], probs[tile, :, :], writes=["g8"])
        P.op("vector", (lambda e: e.tensor_tensor(g8[:, 8:16], g8[:, 0:8],
                                                  sb["thr_bc"][:, :], ALU.is_gt)),
             reads=["g8", "thr_bc"], writes=["selT"])
        P.op("vector", (lambda e: e.reduce_max(g8[:, 16:17], g8[:, 8:16], axis=AX.X)),
             reads=["selT"], writes=["asg"])
        P.op("vector", (lambda e: e.tensor_mul(g8[:, 24:32], g8[:, 8:16], g8[:, 0:8])),
             reads=["selT", "g8"], writes=["m1"])
        P.op("vector", (lambda e: e.tensor_scalar(g8[:, 32:40], g8[:, 8:16], -1.0, fcBIG,
                                                  ALU.add, ALU.mult)),
             reads=["selT"], writes=["m2"])
        P.op("vector", (lambda e: e.tensor_add(g8[:, 24:32], g8[:, 24:32], g8[:, 32:40])),
             reads=["m1", "m2"], writes=["m1"])
        P.op("vector", (lambda e: e.reduce_max(g8[:, 17:18], g8[:, 24:32], axis=AX.X)),
             reads=["m1"], writes=["mx1"])
        P.op("vector", (lambda e: e.tensor_scalar(g8[:, 24:32], g8[:, 24:32],
                                                  g8[:, 17:18], None, ALU.is_ge)),
             reads=["m1", "mx1"], writes=["eq1"])
        P.op("vector", (lambda e: e.tensor_mul(g8[:, 24:32], g8[:, 24:32],
                                               sb["c_iotaB"][:, :])),
             reads=["eq1", "c_iotaB"], writes=["eq1"])
        P.op("vector", (lambda e: e.tensor_scalar(g8[:, 24:32], g8[:, 24:32], 1000.0, None,
                                                  ALU.add)),
             reads=["eq1"], writes=["eq1"])
        P.op("vector", (lambda e: e.tensor_reduce(g8[:, 18:19], g8[:, 24:32], AX.X,
                                                  ALU.min)),
             reads=["eq1"], writes=["idx1"])
        P.op("vector", (lambda e: e.reduce_max(g8[:, 19:20], g8[:, 0:8], axis=AX.X)),
             reads=["g8"], writes=["mx2b"])
        P.op("vector", (lambda e: e.tensor_scalar(g8[:, 32:40], g8[:, 0:8],
                                                  g8[:, 19:20], None, ALU.is_ge)),
             reads=["g8", "mx2b"], writes=["eq2"])
        P.op("vector", (lambda e: e.tensor_mul(g8[:, 32:40], g8[:, 32:40],
                                               sb["c_iotaB"][:, :])),
             reads=["eq2", "c_iotaB"], writes=["eq2"])
        P.op("vector", (lambda e: e.tensor_scalar(g8[:, 32:40], g8[:, 32:40], 1000.0, None,
                                                  ALU.add)),
             reads=["eq2"], writes=["eq2"])
        P.op("vector", (lambda e: e.tensor_reduce(g8[:, 20:21], g8[:, 32:40], AX.X,
                                                  ALU.min)),
             reads=["eq2"], writes=["idx2"])
        P.op("vector", (lambda e: e.tensor_sub(g8[:, 21:22], g8[:, 18:19], g8[:, 20:21])),
             reads=["idx1", "idx2"], writes=["dd"])
        P.op("vector", (lambda e: e.scalar_tensor_tensor(
            g8[:, 22:23], g8[:, 21:22], g8[:, 16:17], g8[:, 20:21],
            op0=ALU.mult, op1=ALU.add)),
             reads=["dd", "asg", "idx2"], writes=["idxf"])
        P.op("vector", (lambda e: e.tensor_scalar(g8[:, 32:40], sb["c_iota8"][:, :],
                                                  g8[:, 22:23], None, ALU.is_equal)),
             reads=["c_iota8", "idxf"], writes=["oh"])
        P.op("vector", (lambda e: e.tensor_mul(g8[:, 32:40], g8[:, 32:40], g8[:, 0:8])),
             reads=["oh", "g8"], writes=["oh"])
        P.dma("sy", gates[tile, :, :], g8[:, 32:40], reads=["oh"], writes=["gdma"])


ND_B = 4096
Z0R = 2432
WU = 1024


def emit_stage4a(P, dr, sb):
    fc0 = fconst(P, sb, 0.0)
    emit_load_consts(P, dr, sb, ["c_idf32", "c_onesrow", "c_headexp", "scale16"])
    P.op("gpsimd", (lambda e: e.memset(sb["onesq"][:, :], 1.0)), writes=["onesq"])
    P.op("gpsimd", (lambda e: e.memset(sb["onescol_bf"][:, :], 1.0)), writes=["onescol_bf"])
    P.dma("g", sb["Wo1"][:, :].rearrange("p (kc d) -> p kc d", kc=NCH),
          dr["Wo1"].ap().rearrange("(kc p) d -> p kc d", p=128), writes=["Wo1"])
    P.dma("g", sb["relb"][0:32, 0:16], dr["rel_bias"].ap(), writes=["relb"])

    nchk = ND_B // 512
    for ch in range(nchk):
        P.dma("g", sb["ohts"][0:32, :], dr["c_oht"].ap()[:, ch * 512:(ch + 1) * 512],
              writes=["ohts"])
        P.dma("g", sb["ngr"][0:16, :], dr["c_negrow"].ap()[:, ch * 512:(ch + 1) * 512],
              writes=["ngr"])
        P.op("tensor", (lambda e: e.matmul(
            sb["pmv"][0:16, 512:1024], sb["relb"][0:32, 0:16],
            sb["ohts"][0:32, :], start=True, stop=True)),
             reads=["relb", "ohts"], writes=["pmv"])
        P.op("vector", (lambda e: e.tensor_scalar(
            sb["uch"][0:16, :], sb["pmv"][0:16, 512:1024],
            sb["scale16"][0:16, 0:1], None, ALU.mult)),
             reads=["pmv", "scale16"], writes=["uch"])
        P.op("vector", (lambda e: e.tensor_tensor(
            sb["uch"][0:16, :], sb["uch"][0:16, :], sb["ngr"][0:16, :], ALU.add)),
             reads=["ngr"], writes=["uch"])
        P.op("vector", (lambda e, ch=ch: e.reduce_max(
            sb["hh8"][0:16, 8 + ch:9 + ch], sb["uch"][0:16, :], axis=AX.X)),
             reads=["uch"], writes=["hhmx"])
        if ch == (1024 + 2047) // 512:
            P.op("vector", (lambda e, cc=(1024 + 2047) % 512: e.tensor_copy(
                sb["hh8"][0:16, 16:17], sb["uch"][0:16, cc:cc + 1])),
                 reads=["uch"], writes=["far0"])
        P.op("vector", (lambda e: e.tensor_copy(
            sb["uchr"][0:16, :],
            AP(sb["uch"][0:1, 0:1].tensor, 511, [[512, 16], [-1, 512]]))),
             reads=["uch"], writes=["uchr"])
        P.dma("sy", dr["u_rev"].ap()[:, ND_B - 512 * (ch + 1): ND_B - 512 * ch],
              sb["uchr"][0:16, :], reads=["uchr"], writes=["urevd"])
    P.op("vector", (lambda e, nchk=nchk: e.reduce_max(
        sb["hh8"][0:16, 0:1], sb["hh8"][0:16, 8:8 + nchk], axis=AX.X)),
         reads=["hhmx"], writes=["hh0"])
    P.op("vector", (lambda e: e.tensor_scalar_max(sb["hh8"][0:16, 1:2], sb["hh8"][0:16, 0:1],
                                                  0.0)),
         reads=["hh0"], writes=["hh1"])
    P.op("vector", (lambda e: e.tensor_scalar(sb["hh8"][0:16, 2:3], sb["scale16"][0:16, 0:1],
                                              -1.0, None, ALU.mult)),
         reads=["scale16"], writes=["hh2"])
    P.op("vector", (lambda e: e.tensor_tensor(sb["hh8"][0:16, 3:4], sb["hh8"][0:16, 2:3],
                                              sb["scale16"][0:16, 0:1], ALU.max)),
         reads=["hh2", "scale16"], writes=["hh3"])
    P.op("vector", (lambda e: e.tensor_scalar(sb["hh8"][0:16, 4:5], sb["hh8"][0:16, 3:4],
                                              1.02, None, ALU.mult)),
         reads=["hh3"], writes=["hh4"])
    P.op("vector", (lambda e: e.tensor_add(sb["hh8"][0:16, 5:6], sb["hh8"][0:16, 4:5],
                                           sb["hh8"][0:16, 1:2])),
         reads=["hh4", "hh1"], writes=["hh5"])
    P.op("vector", (lambda e: e.tensor_scalar(sb["hh8"][0:16, 6:7], sb["hh8"][0:16, 5:6],
                                              -1.0, None, ALU.mult)),
         reads=["hh5"], writes=["negC"])
    P.op("vector", (lambda e: e.tensor_add(sb["hh8"][0:16, 7:8], sb["hh8"][0:16, 6:7],
                                           sb["hh8"][0:16, 16:17])),
         reads=["negC", "far0"], writes=["farb"])
    for (cc, dst) in ((6, "CbN"), (7, "CbF")):
        P.op("tensor", (lambda e, cc=cc: e.transpose(
            sb["pmv"][0:1, 512:528], sb["hh8"][0:16, cc:cc + 1], sb["c_idf32"][0:16, 0:16])),
             reads=["negC", "farb", "c_idf32"], writes=["pmv"])
        P.op("vector", (lambda e: e.tensor_copy(sb["row16"][0:1, 0:16],
                                                sb["pmv"][0:1, 512:528])),
             reads=["pmv"], writes=["row16"])
        P.op("tensor", (lambda e: e.matmul(sb["pmv"][:, 512:528], sb["c_onesrow"][0:1, :],
                                           sb["row16"][0:1, 0:16], start=True, stop=True)),
             reads=["c_onesrow", "row16"], writes=["pmv"])
        P.op("vector", (lambda e, dst=dst: e.tensor_copy(sb[dst][:, :],
                                                         sb["pmv"][:, 512:528])),
             reads=["pmv"], writes=[dst])
    for h in range(H):
        P.dma("g", dr["u2"].ap().rearrange("(h p) w -> h p w", h=H)[h, :, :],
              AP(dr["u_rev"].ap().tensor, h * ND_B + Z0R, [[1, 128], [1, WU]]),
              reads=["urevd"], writes=["u2d"])

    q1 = dr["q1_fm"].ap()
    k1 = dr["k1_fm"].ap()
    v1 = dr["v1_tm"].ap()
    xn32 = dr["xn32"].ap().rearrange("(n p) d -> n p d", p=128)
    y32 = dr["y32"].ap().rearrange("(n p) d -> n p d", p=128)
    Vs, Obf = sb["Vs"], sb["Obf"]

    for b in range(B_LOC):
        for vch in range(4):
            P.dma("g", Vs[:, vch * 4 * VD:(vch + 1) * 4 * VD],
                  AP(v1.tensor, (b * S + vch * 4 * 128) * VD,
                     [[VD, 128], [128 * VD, 4], [1, VD]]),
                  writes=["Vs%d" % vch])
        for (t, src) in (("sqs", "sqv"), ("sks", "skv")):
            P.dma("g", sb[t][0:16, :],
                  AP(dr[src].ap().tensor, b * S, [[T_LOC, 16], [1, S]]), writes=[t])
        for hp in range(NCH):
            hA, hB = 2 * hp, 2 * hp + 1
            for (tb, src) in (("Qhr", q1), ("Khr", k1)):
                t = "%s%d" % (tb, hp % 2)
                P.dma("g", sb[t][:, :],
                      AP(src.tensor, (hp * 128) * T_LOC + b * S, [[T_LOC, 128], [1, S]]),
                      writes=[t])
            for (srcn, sv, dst) in (("Qhr%d" % (hp % 2), "sqs", "Qh"),
                                    ("Khr%d" % (hp % 2), "sks", "Kh")):
                srct = sb[srcn]
                for qc in range(4):
                    P.op("tensor", (lambda e, hp=hp, qc=qc, sv=sv: e.matmul(
                        sb["pr"][:, 0:512], sb["c_headexp"][0:16, hp * 128:(hp + 1) * 128],
                        sb[sv][0:16, qc * 512:(qc + 1) * 512], start=True, stop=True)),
                         reads=["c_headexp", sv], writes=["prA"])
                    P.op("vector", (lambda e, srct=srct, qc=qc, dst=dst: e.tensor_tensor(
                        sb[dst][:, qc * 512:(qc + 1) * 512],
                        AP(srct[0:1, 0:1].tensor, qc * 512, [[S, 128], [1, 512]]),
                        sb["pr"][:, 0:512], ALU.mult)),
                         reads=[srcn, "prA"], writes=[dst])
            P.dma("g", sb["u2A"][:, :],
                  dr["u2"].ap().rearrange("(h p) w -> h p w", h=H)[hA, :, :],
                  reads=["u2d"], writes=["u2A"])
            P.dma("g", sb["u2B"][:, :],
                  dr["u2"].ap().rearrange("(h p) w -> h p w", h=H)[hB, :, :],
                  reads=["u2d"], writes=["u2B"])
            for (hh, col) in ((hA, 0), (hB, 1)):
                for kt in range(16):
                    P.op("tensor", (lambda e, hh=hh, kt=kt: e.matmul(
                        sb["pmv"][0:65, 512:513],
                        AP(Vs[0:1, 0:1].tensor, kt * VD + hh * 65, [[16 * VD, 128], [1, 65]]),
                        sb["onescol_bf"][:, :], start=(kt == 0), stop=(kt == 15))),
                         reads=["Vs%d" % (kt // 4), "onescol_bf"], writes=["pmv"],
                         no_inc=(kt < 15))
                P.op("vector", (lambda e, col=col: e.tensor_scalar(
                    sb["mvsb"][0:64, col:col + 1], sb["pmv"][0:64, 512:513],
                    1.0 / S, None, ALU.mult)),
                     reads=["pmv"], writes=["mvsb"])
            for qc in range(4):
                nkt = 4 * qc + 4

                def s4_qk(kt, qc=qc):
                    qoff = max(0, (kt - 4 * qc) * 128)
                    w = 512 - qoff
                    par = kt % 2
                    for (hh, pL, base, tp) in ((hA, "pL%dA" % par, 0, (0, 0)),
                                               (hB, "pL%dB" % par, 64, (64, 0))):
                        P.op("tensor", (lambda e, kt=kt, qc=qc, qoff=qoff, w=w,
                                        pL=pL, base=base, tp=tp: e.matmul(
                            sb[pL][:, qoff:512],
                            AP(sb["Kh"][0:1, 0:1].tensor, base * S + kt * 128,
                               [[S, 64], [1, 128]]),
                            AP(sb["Qh"][0:1, 0:1].tensor, base * S + qc * 512 + qoff,
                               [[S, 64], [1, w]]),
                            start=True, stop=True, tile_position=tp)),
                             reads=["Qh", "Kh"], writes=[pL])

                def s4_post(kt, qc=qc, nkt=nkt, hA=hA, hB=hB):
                    qoff = max(0, (kt - 4 * qc) * 128)
                    w = 512 - qoff
                    par = kt % 2
                    in_band = (kt >= 4 * qc - 1)
                    if in_band:
                        mW = 639 - 512 * qc + 128 * kt
                        for (pL, u2t) in (("pL%dA" % par, "u2A"), ("pL%dB" % par, "u2B")):
                            P.op("vector", (lambda e, pL=pL, u2t=u2t, qoff=qoff, w=w, mW=mW:
                                            e.tensor_tensor(
                                sb[pL][:, qoff:512], sb[pL][:, qoff:512],
                                AP(sb[u2t][0:1, 0:1].tensor, mW - qoff, [[WU, 128], [-1, w]]),
                                ALU.add)),
                                 reads=[u2t], writes=[pL])
                    bias_t = "CbN" if in_band else "CbF"
                    for (hh, pL, pt) in ((hA, "pL%dA" % par, "ptA%d" % par),
                                         (hB, "pL%dB" % par, "ptB%d" % par)):
                        P.op("scalar", (lambda e, pL=pL, pt=pt, qoff=qoff, hh=hh,
                                        bias_t=bias_t: e.activation(
                            sb[pt][:, qoff:512], sb[pL][:, qoff:512], AF.Exp,
                            bias=sb[bias_t][:, hh:hh + 1])),
                             reads=[pL, bias_t], writes=[pt])
                    first, last = (kt == 0), (kt == nkt - 1)
                    for (hh, pt, po) in ((hA, "ptA%d" % par, "poA"),
                                         (hB, "ptB%d" % par, "poB")):
                        P.op("tensor", (lambda e, hh=hh, pt=pt, po=po, kt=kt,
                                        qoff=qoff, first=first, last=last: e.matmul(
                            sb[po][0:65, qoff:512],
                            AP(Vs[0:1, 0:1].tensor, kt * VD + hh * 65,
                               [[16 * VD, 128], [1, 65]]),
                            sb[pt][:, qoff:512],
                            start=first, stop=last)),
                             reads=["Vs%d" % (kt // 4), pt], writes=[po],
                             no_inc=not last)

                for kt in range(nkt):
                    s4_qk(kt)
                    if kt > 0:
                        s4_post(kt - 1)
                s4_post(nkt - 1)
                p2 = qc % 2
                sgA, sgAt = sb["sgA%d" % p2], "sgA%d" % p2
                sgB, sgBt = sb["sgB%d" % p2], "sgB%d" % p2
                dn, dnt = sb["dn%d" % p2], "dn%d" % p2
                st2, st2t = sb["stg2"], "stg2"
                P.op("vector", (lambda e, sgA=sgA: e.tensor_copy(
                    sgA[0:65, 0:512], sb["poA"][0:65, 0:512])),
                     reads=["poA"], writes=[sgAt])
                P.op("scalar", (lambda e, sgB=sgB: e.copy(
                    sgB[0:65, 0:512], sb["poB"][0:65, 0:512])),
                     reads=["poB"], writes=[sgBt])
                P.op("vector", (lambda e, dn=dn, sgA=sgA: e.reciprocal(
                    dn[64:65, 0:512], sgA[64:65, 0:512])),
                     reads=[sgAt], writes=[dnt + "A"])
                P.op("vector", (lambda e, dn=dn, sgB=sgB: e.reciprocal(
                    dn[64:65, 512:1024], sgB[64:65, 0:512])),
                     reads=[sgBt], writes=[dnt + "B"])
                P.op("tensor", (lambda e, dn=dn: e.matmul(
                    sb["pr"][0:64, 0:512], sb["onesq"][64:65, 0:64],
                    dn[64:65, 0:512],
                    start=True, stop=True, tile_position=(64, 0))),
                     reads=[dnt + "A", "onesq"], writes=["prA"])
                P.op("tensor", (lambda e, dn=dn: e.matmul(
                    sb["pr"][0:64, 512:1024], sb["onesq"][64:65, 0:64],
                    dn[64:65, 512:1024],
                    start=True, stop=True, tile_position=(64, 0))),
                     reads=[dnt + "B", "onesq"], writes=["pmv"])
                P.op("vector", (lambda e, sgA=sgA, hp=hp, qc=qc: e.tensor_tensor(
                    AP(Obf[0:1, 0:1].tensor, hp * S + qc * 512,
                       [[NCH * S, 64], [1, 512]]),
                    sgA[0:64, 0:512], sb["pr"][0:64, 0:512], ALU.mult)),
                     reads=[sgAt, "prA"], writes=["Obf"])
                if qc == 0:
                    P.op("vector", (lambda e, hp=hp: e.tensor_copy(
                        AP(Obf[0:1, 0:1].tensor, hp * S, [[NCH * S, 64], [1, 1]]),
                        sb["mvsb"][0:64, 0:1])),
                         reads=["mvsb"], writes=["Obf"])
                P.op("vector", (lambda e, sgB=sgB, st2=st2: e.tensor_tensor(
                    st2[0:64, 0:512], sgB[0:64, 0:512],
                    sb["pr"][0:64, 512:1024], ALU.mult)),
                     reads=[sgBt, "pmv"], writes=[st2t])
                if qc == 0:
                    P.op("vector", (lambda e, st2=st2: e.tensor_copy(
                        st2[0:64, 0:1], sb["mvsb"][0:64, 1:2])),
                         reads=["mvsb"], writes=[st2t])
                P.op("tensor", (lambda e, st2=st2: e.matmul(
                    sb["pr"][64:128, 0:512], sb["c_idf32"][0:64, 0:64],
                    st2[0:64, 0:512],
                    start=True, stop=True, tile_position=(0, 64))),
                     reads=[st2t, "c_idf32"], writes=["prA"])
                P.op("vector", (lambda e, hp=hp, qc=qc: e.tensor_copy(
                    AP(Obf[0:1, 0:1].tensor, 64 * (NCH * S) + hp * S + qc * 512,
                       [[NCH * S, 64], [1, 512]]),
                    sb["pr"][64:128, 0:512])),
                     reads=["prA"], writes=["Obf"])

        for tt in range(16):
            gi = 16 * b + tt
            xnt = sb["xnt"]
            P.dma("g", xnt[:, :], xn32[gi, :, :], writes=["xnt"])
            for dh in range(2):
                po = "poA" if dh == 0 else "poB"
                for kc in range(NCH):
                    P.op("tensor", (lambda e, tt=tt, dh=dh, kc=kc, po=po: e.matmul(
                        sb[po][:, :],
                        AP(Obf[0:1, 0:1].tensor, kc * S + tt * 128, [[NCH * S, 128], [1, 128]]),
                        sb["Wo1"][:, kc * D + dh * 512: kc * D + dh * 512 + 512],
                        start=(kc == 0), stop=(kc == NCH - 1))),
                         reads=["Obf", "Wo1"], writes=[po], no_inc=(kc < NCH - 1))
                P.op("vector", (lambda e, dh=dh, po=po, xnt=xnt: e.tensor_add(
                    sb["ysb"][:, dh * 512:(dh + 1) * 512], sb[po][:, :],
                    xnt[:, dh * 512:(dh + 1) * 512])),
                     reads=[po, "xnt"], writes=["ysb"])
            P.dma("sy", y32[gi, :, :], sb["ysb"][:, :], reads=["ysb"], writes=["ydma"])


def emit_stage4b(P, dr, sb):
    fc0 = fconst(P, sb, 0.0)
    beps = fconst(P, sb, RMS_EPS)
    emit_load_consts(P, dr, sb, ["ln2_bc"])
    P.dma("g", sb["W1sb"][:, :].rearrange("p (kc d) -> p kc d", kc=NCH),
          dr["W1"].ap().rearrange("(kc p) d -> p kc d", p=128), writes=["W1sb"])
    P.dma("g", sb["W2sb"][:, :].rearrange("p (kc d) -> p kc d", kc=32),
          dr["W2"].ap().rearrange("(kc p) d -> p kc d", p=128), writes=["W2sb"])
    y32 = dr["y32"].ap().rearrange("(n p) d -> n p d", p=128)
    x1 = dr["x1"].ap().rearrange("(n p) d -> n p d", p=128)
    based = dr["base"].ap().rearrange("(n p) d -> n p d", p=128)
    hfm = sb["hfm"]
    for g in range(16):
        tfm = sb["t_fm%d" % (g % 2)]
        tft = "t_fm%d" % (g % 2)
        for ti in range(4):
            gi = 4 * g + ti
            yt = sb["yt%d" % (ti % 2)]
            ytt = "yt%d" % (ti % 2)
            P.dma("g", yt[:, :], y32[gi, :, :], writes=[ytt])
            P.op("scalar", (lambda e, yt=yt, b=fc0: e.activation(
                sb["scr"][:, :], yt[:, :], AF.Square, bias=b,
                accum_out=sb["mv"][:, 0:1])),
                 reads=[ytt, "cpool"], writes=["scr", "mv"])
            P.op("vector", (lambda e: e.tensor_scalar(sb["mv"][:, 1:2], sb["mv"][:, 0:1],
                                                      1.0 / D, None, ALU.mult)),
                 reads=["mv"], writes=["mv1"])
            P.op("scalar", (lambda e, b=beps: e.activation(
                sb["mv"][:, 2:3], sb["mv"][:, 1:2], AF.Ln, bias=b)),
                 reads=["mv1", "cpool"], writes=["mv2"])
            P.op("scalar", (lambda e, b=fc0: e.activation(
                sb["mv"][:, 3:4], sb["mv"][:, 2:3], AF.Exp, scale=-0.5, bias=b)),
                 reads=["mv2", "cpool"], writes=["mv3"])
            P.op("vector", (lambda e, yt=yt: e.tensor_scalar(
                sb["scr"][:, :], yt[:, :], sb["mv"][:, 3:4], None, ALU.mult)),
                 reads=[ytt, "mv3"], writes=["scr"])
            P.op("vector", (lambda e: e.tensor_tensor(
                sb["tbf"][:, :], sb["scr"][:, :], sb["ln2_bc"][:, :], ALU.mult)),
                 reads=["scr", "ln2_bc"], writes=["tbf"])
            for kc in range(NCH):
                P.dma("sc", AP(tfm[0:1, 0:1].tensor, kc * 512 + ti * 128,
                               [[NCH * 512, 128], [1, 128]]),
                      sb["tbf"][:, kc * 128:(kc + 1) * 128],
                      reads=["tbf"], writes=[tft], transpose=True)
        for dc2 in range(32):
            pg = sb["pg%d" % (dc2 % 2)]
            pgt = "pg%d" % (dc2 % 2)
            for kc in range(NCH):
                P.op("tensor", (lambda e, dc2=dc2, kc=kc, pg=pg, tfm=tfm: e.matmul(
                    pg[:, :], sb["W1sb"][:, kc * 4096 + dc2 * 128: kc * 4096 + dc2 * 128 + 128],
                    AP(tfm[0:1, 0:1].tensor, kc * 512, [[NCH * 512, 128], [1, 512]]),
                    start=(kc == 0), stop=(kc == NCH - 1))),
                     reads=["W1sb", tft], writes=[pgt], no_inc=(kc < NCH - 1))
            P.op("vector", (lambda e, dc2=dc2, pg=pg: e.tensor_scalar_max(
                AP(hfm[0:1, 0:1].tensor, dc2 * 512, [[32 * 512, 128], [1, 512]]),
                pg[:, :], 0.0)),
                 reads=[pgt], writes=["hfm"])
        for ti in range(4):
            gi = 4 * g + ti
            yt = sb["yt%d" % (ti % 2)]
            ytt = "yt%d" % (ti % 2)
            P.dma("g", yt[:, :], y32[gi, :, :], writes=[ytt])
            xt = sb["scr"]
            P.dma("g", xt[:, :], x1[gi, :, :], writes=["scr"])
            for dh in range(2):
                pg = sb["pg%d" % (2 + dh)]
                pgt = "pg%d" % (2 + dh)
                for k2 in range(32):
                    P.op("tensor", (lambda e, ti=ti, dh=dh, k2=k2, pg=pg: e.matmul(
                        pg[:, :],
                        AP(hfm[0:1, 0:1].tensor, k2 * 512 + ti * 128,
                           [[32 * 512, 128], [1, 128]]),
                        sb["W2sb"][:, k2 * D + dh * 512: k2 * D + dh * 512 + 512],
                        start=(k2 == 0), stop=(k2 == 31))),
                         reads=["hfm", "W2sb"], writes=[pgt], no_inc=(k2 < 31))
                P.op("vector", (lambda e, dh=dh, pg=pg, yt=yt: e.tensor_add(
                    sb["bse"][:, dh * 512:(dh + 1) * 512], pg[:, :],
                    yt[:, dh * 512:(dh + 1) * 512])),
                     reads=[pgt, ytt], writes=["bse"])
                P.op("vector", (lambda e, dh=dh, xt=xt: e.tensor_add(
                    sb["bse"][:, dh * 512:(dh + 1) * 512],
                    sb["bse"][:, dh * 512:(dh + 1) * 512],
                    xt[:, dh * 512:(dh + 1) * 512])),
                     reads=["scr"], writes=["bse"])
            P.dma("sy", based[gi, :, :], sb["bse"][:, :], reads=["bse"], writes=["bdma"])


def emit_stage5(P, dr, sb):
    fc0 = fconst(P, sb, 0.0)
    P.dma("g", dr["W1e_bf"].ap(), dr["W1e"].ap(), writes=["w1c"])
    P.dma("g", dr["W2e_bf"].ap(), dr["W2e"].ap(), writes=["w2c"])
    gates = dr["gates"].ap()
    based = dr["base"].ap()
    outd = dr["y_out"].ap()
    xnfm = dr["xn_fm_bf"].ap()
    TC = 1024
    for pair in range(T_LOC // (2 * TC)):
        for j in range(2):
            tc = 2 * pair + j
            P.dma("g", sb["gt%d" % j][:, :].rearrange("p (t e) -> p t e", t=8),
                  AP(gates.tensor, tc * TC * 8, [[8, 128], [128 * 8, 8], [1, 8]]),
                  writes=["gt%d" % j])
            P.dma("g", sb["acc%d" % j][:, :].rearrange("p (t d) -> p t d", t=8),
                  AP(based.tensor, tc * TC * D, [[D, 128], [128 * D, 8], [1, D]]),
                  writes=["acc%d" % j])
            P.dma("g", sb["xfm%d" % j][:, :].rearrange("p (kc n) -> p kc n", kc=NCH),
                  AP(xnfm.tensor, tc * TC, [[T_LOC, 128], [128 * T_LOC, NCH], [1, TC]]),
                  writes=["xfm%d" % j])
        for ex in range(MOE_E):
            for half in range(2):
                P.dma("sc", AP(sb["W1f"][0:1, 0:1].tensor, half * TC,
                               [[NCH * MOE_F, 128], [MOE_F, NCH], [1, TC]]),
                      AP(dr["W1e_bf"].ap().tensor, ex * D * MOE_F + half * TC,
                         [[MOE_F, 128], [128 * MOE_F, NCH], [1, TC]]),
                      reads=["w1c"], writes=["W1f%d" % half])
                P.dma("sy", AP(sb["W2h"][0:1, 0:1].tensor, half * NCH * D,
                               [[16 * D, 128], [D, NCH], [1, D]]),
                      AP(dr["W2e_bf"].ap().tensor, ex * MOE_F * D + half * TC * D,
                         [[D, 128], [128 * D, NCH], [1, D]]),
                      reads=["w2c"], writes=["W2h%d" % half])
            for j in range(2):
                xft = "xfm%d" % j
                for half in range(2):
                    for dc2 in range(NCH):
                        pg = sb["pg%d" % (dc2 % 2)]
                        pgt = "pg%d" % (dc2 % 2)
                        for qc2 in range(2):
                            for kc in range(NCH):
                                P.op("tensor", (lambda e, half=half, dc2=dc2, qc2=qc2,
                                                kc=kc, pg=pg, xft=xft: e.matmul(
                                    pg[:, :],
                                    sb["W1f"][:, kc * MOE_F + half * TC + dc2 * 128:
                                              kc * MOE_F + half * TC + dc2 * 128 + 128],
                                    AP(sb[xft][0:1, 0:1].tensor, kc * TC + qc2 * 512,
                                       [[NCH * TC, 128], [1, 512]]),
                                    start=(kc == 0), stop=(kc == NCH - 1))),
                                     reads=["W1f%d" % half, xft], writes=[pgt],
                                     no_inc=(kc < NCH - 1))
                            P.op("scalar", (lambda e, half=half, dc2=dc2, qc2=qc2, pg=pg,
                                            b=fc0: e.activation(
                                AP(sb["Gfm"][0:1, 0:1].tensor,
                                   half * NCH * TC + dc2 * TC + qc2 * 512,
                                   [[16 * TC, 128], [1, 512]]),
                                pg[:, :], AF.Gelu, bias=b)),
                                 reads=[pgt, "cpool"], writes=["Gfm"])
                acct = "acc%d" % j
                gtt = "gt%d" % j
                for t8 in range(8):
                    for dh in range(2):
                        pg = sb["pgW%d" % (dh % 2)]
                        pgt = "pgW%d" % (dh % 2)
                        for half in range(2):
                            for k2 in range(NCH):
                                P.op("tensor", (lambda e, t8=t8, dh=dh, half=half, k2=k2,
                                                pg=pg: e.matmul(
                                    pg[:, :],
                                    AP(sb["Gfm"][0:1, 0:1].tensor,
                                       half * NCH * TC + k2 * TC + t8 * 128,
                                       [[16 * TC, 128], [1, 128]]),
                                    AP(sb["W2h"][0:1, 0:1].tensor,
                                       half * NCH * D + k2 * D + dh * 512,
                                       [[16 * D, 128], [1, 512]]),
                                    start=(half == 0 and k2 == 0),
                                    stop=(half == 1 and k2 == NCH - 1))),
                                     reads=["Gfm", "W2h%d" % half], writes=[pgt],
                                     no_inc=not (half == 1 and k2 == NCH - 1))
                        P.op("vector", (lambda e, ex=ex, t8=t8, dh=dh, pg=pg,
                                        acct=acct, gtt=gtt: e.scalar_tensor_tensor(
                            AP(sb[acct][0:1, 0:1].tensor, t8 * D + dh * 512,
                               [[NCH * D, 128], [1, 512]]),
                            pg[:, :],
                            AP(sb[gtt][0:1, 0:1].tensor, t8 * 8 + ex, [[64, 128], [1, 1]]),
                            AP(sb[acct][0:1, 0:1].tensor, t8 * D + dh * 512,
                               [[NCH * D, 128], [1, 512]]),
                            op0=ALU.mult, op1=ALU.add)),
                             reads=[pgt, gtt], writes=[acct])
        for j in range(2):
            tc = 2 * pair + j
            P.dma("g", AP(outd.tensor, tc * TC * D, [[D, 128], [128 * D, 8], [1, D]]),
                  sb["acc%d" % j][:, :].rearrange("p (t d) -> p t d", t=8),
                  reads=["acc%d" % j], writes=["odma"])


IN, OUT, INT = "ExternalInput", "ExternalOutput", "Internal"

P1_SPECS = {
    "x_in": ([T_LOC, D], F32, IN),
    "Wq": ([D, D], F32, IN), "Wk": ([D, D], F32, IN), "Wv": ([D, D], F32, IN),
    "Wo": ([D, D], F32, IN),
    "Wr": ([D, D], F32, IN), "Wk2": ([D, D], F32, IN), "Wv2": ([D, D], F32, IN),
    "mu_ch": ([128, 16], F32, IN),
    "gamma_bc": ([128, D], F32, IN), "beta_bc": ([128, D], F32, IN),
    "Wq1": ([D, D], F32, IN), "Wk1": ([D, D], F32, IN), "Wv1": ([D, D], F32, IN),
    "Wo1": ([D, D], F32, IN), "rel_bias": ([32, 16], F32, IN),
    "scale16": ([16, 1], F32, IN), "scale128": ([128, 1], F32, IN),
    "ln1_bc": ([128, D], F32, IN), "ln2_bc": ([128, D], F32, IN),
    "W1": ([D, 4096], F32, IN), "W2": ([4096, D], F32, IN),
    "Wg": ([D, MOE_E], F32, IN),
    "c_idf32": ([128, 128], F32, IN), "c_idbf": ([128, 128], BF16, IN),
    "c_headmask": ([128, NCH * H], F32, IN), "c_headexp": ([16, NCH * 128], F32, IN),
    "c_mask01": ([128, 256], F32, IN), "c_onesrow": ([1, 128], F32, IN),
    "c_oht": ([32, ND_B], F32, IN), "c_negrow": ([16, ND_B], F32, IN),
    "Wq_bf": ([D, D], BF16, INT), "Wk_bf": ([D, D], BF16, INT),
    "Wv_bf": ([D, D], BF16, INT), "Wr_bf": ([D, D], BF16, INT),
    "Wk2_bf": ([D, D], BF16, INT), "Wv2_bf": ([D, D], BF16, INT),
    "Wo_bf": ([D, D], BF16, INT),
    "q0_fm": ([D, T_LOC], BF16, INT), "k0_fm": ([D, T_LOC], BF16, INT),
    "v0_tm": ([T_LOC, VD], BF16, INT), "xB": ([T_LOC, D], F32, INT),
    "ssq0": ([16, T_LOC], F32, INT), "ssk0": ([16, T_LOC], F32, INT),
    "x1": ([T_LOC, D], F32, INT),
    "xn32": ([T_LOC, D], F32, INT),
    "q1_fm": ([D, T_LOC], BF16, INT), "k1_fm": ([D, T_LOC], BF16, INT),
    "v1_tm": ([T_LOC, VD], BF16, INT),
    "ssq1": ([16, T_LOC], F32, INT), "ssk1": ([16, T_LOC], F32, INT),
    "sqv": ([16, T_LOC], F32, INT), "skv": ([16, T_LOC], F32, INT),
    "u_rev": ([16, ND_B], F32, INT), "u2": ([16 * 128, WU], F32, INT),
    "y32": ([T_LOC, D], F32, INT),
    "probs": ([T_LOC, MOE_E], F32, OUT), "pT": ([MOE_E, T_LOC], F32, OUT),
    "xn_fm_bf": ([D, T_LOC], BF16, OUT),
    "base": ([T_LOC, D], F32, OUT),
}

P2_SPECS = {
    "probs_gathered": ([MOE_E * N_CORES, T_LOC], F32, IN),
    "probs": ([T_LOC, MOE_E], F32, IN),
    "xn_fm_bf": ([D, T_LOC], BF16, IN),
    "base": ([T_LOC, D], F32, IN),
    "W1e": ([MOE_E * D, MOE_F], F32, IN), "W2e": ([MOE_E * MOE_F, D], F32, IN),
    "c_iota8": ([128, 8], F32, IN), "c_iotaB": ([128, 8], F32, IN),
    "c_onesrow": ([1, 128], F32, IN), "c_p2e": ([128, 8], F32, IN),
    "mid128": ([128], F32, INT), "cnt128": ([128], F32, INT),
    "thr8": ([8], F32, INT),
    "gates": ([T_LOC, MOE_E], F32, INT),
    "W1e_bf": ([MOE_E * D, MOE_F], BF16, INT),
    "W2e_bf": ([MOE_E * MOE_F, D], BF16, INT),
    "y_out": ([T_LOC, D], F32, OUT),
}


def _sz(dt):
    return 2 if dt == BF16 else 4


def make_sb(nc, table, psmap):
    sb = {}
    off = (nc.sbuf_base + 63) & ~63
    for name, (shape, dtype) in table.items():
        nb = shape[1] * _sz(dtype)
        sb[name] = nc.alloc_sbuf_tensor_at("sb_" + name, list(shape), dtype, offset=off)
        off += (nb + 63) & ~63
    assert off <= nc.sbuf_top, (off, nc.sbuf_top)
    sb.update(psmap)
    return sb


S1_TABLE = {
    "c_idf32": ([128, 128], F32), "c_idbf": ([128, 128], BF16),
    "c_headmask": ([128, NCH * H], F32),
    "WT_Wq": ([128, NCH * D], BF16), "WT_Wk": ([128, NCH * D], BF16),
    "WT_Wv": ([128, NCH * D], BF16), "WT_Wr": ([128, NCH * D], BF16),
    "WT_Wk2": ([128, NCH * D], BF16), "WT_Wv2": ([128, NCH * D], BF16),
    "cpool": ([128, 16], F32), "mu_ch": ([128, 16], F32), "mu1": ([128, 16], F32),
    "s_bn6": ([128, 12], F32), "s_mv": ([128, 2 * NTILE], F32),
    "s_nm": ([128, NTILE], F32), "s_rs": ([128, NTILE], F32),
    "s_t64": ([128, NTILE], F32), "s_t64b": ([128, NTILE], F32),
    "xs0": ([128, D], F32), "xs1": ([128, D], F32),
    "xn_tm": ([128, D], F32), "xn32g": ([128, NCH * 513], F32),
    "xnbf": ([128, NCH * 512], BF16), "dbuf": ([128, 512], F32),
    "xrb": ([128, NCH * 512], BF16), "xkb": ([128, NCH * 512], BF16),
    "kact": ([128, NCH * 512], BF16),
    "relu_t": ([128, 512], F32), "r_sb": ([128, D], F32), "b_sb": ([128, 512], F32),
    "xB_sb": ([128, D], F32),
    "v_sb0": ([128, VD], BF16), "v_sb1": ([128, VD], BF16),
    "q_sb0": ([128, 512], BF16), "q_sb1": ([128, 512], BF16),
    "q2": ([128, 512], F32), "ss_sb": ([128, 512], F32),
}

S2_TABLE = {
    "c_mask01": ([128, 256], F32), "c_idf32": ([128, 128], F32),
    "c_onesrow": ([1, 128], F32), "onesq": ([128, 64], F32), "cpool": ([128, 16], F32),
    "gamma_bc": ([128, D], F32), "beta_bc": ([128, D], F32),
    "WT_Wo": ([128, NCH * D], BF16),
    "Qhr0": ([128, S], BF16), "Qhr1": ([128, S], BF16),
    "Khr0": ([128, S], BF16), "Khr1": ([128, S], BF16),
    "Vs": ([128, 16 * VD], BF16), "O32": ([128, NCH * S], F32),
    "mx2": ([128, 12], F32), "negC_row": ([1, 16], F32), "Cb": ([128, 16], F32),
    "ptA0": ([128, 512], BF16), "ptA1": ([128, 512], BF16),
    "ptB0": ([128, 512], BF16), "ptB1": ([128, 512], BF16),
    "recx": ([128, 512], F32), "otm": ([128, D], F32),
    "bn6": ([128, 12], F32), "mv": ([128, 8], F32),
    "lnbf": ([128, D], BF16), "lnfm": ([128, NCH * 512], BF16),
    "sgA0": ([128, 512], F32), "sgA1": ([128, 512], F32),
    "sgB0": ([128, 512], F32), "sgB1": ([128, 512], F32),
    "dn0": ([128, 1024], F32), "dn1": ([128, 1024], F32),
    "stg2": ([128, 512], F32),
}

S3_TABLE = {
    "c_idf32": ([128, 128], F32), "c_headmask": ([128, NCH * H], F32),
    "cpool": ([128, 16], F32), "ln1_bc": ([128, D], F32),
    "scale128": ([128, 1], F32),
    "Wq1": ([128, NCH * D], BF16), "Wk1": ([128, NCH * D], BF16),
    "Wv1": ([128, NCH * D], BF16), "Wg": ([128, NCH * 8], F32),
    "s_bn6": ([128, 12], F32), "s_mv": ([128, 2 * NTILE], F32),
    "s_nm": ([128, NTILE], F32), "s_rs": ([128, NTILE], F32),
    "s_ri": ([128, NTILE], F32), "s_t64": ([128, NTILE], F32),
    "s_t64b": ([128, NTILE], F32),
    "xs0": ([128, D], F32), "xs1": ([128, D], F32),
    "xn_tm": ([128, D], F32), "xni_t": ([128, D], F32), "xni_bf": ([128, D], BF16),
    "xn_fm32": ([128, NCH * 512], F32), "xn_fm_bf": ([128, NCH * 512], BF16),
    "xni_fm": ([128, NCH * 512], BF16),
    "lg": ([128, 20], F32), "probs_t": ([128, 8], F32), "pT_t": ([128, 128], F32),
    "q_sb0": ([128, 512], BF16), "q_sb1": ([128, 512], BF16),
    "q2": ([128, 512], F32), "ss_sb": ([128, 512], F32),
    "ssl": ([128, 1024], F32), "ssl2": ([128, 1024], F32),
    "v_sb0": ([128, VD], BF16), "v_sb1": ([128, VD], BF16),
}

S4G_TABLE = {
    "c_iota8": ([128, 8], F32), "c_iotaB": ([128, 8], F32),
    "c_onesrow": ([1, 128], F32), "c_p2e": ([128, 8], F32),
    "cpool": ([128, 16], F32),
    "G": ([128, 4096], F32), "bis": ([128, 192], F32), "mid_pp": ([128, 1], F32),
    "cmpb": ([128, 4096], F32), "cnt_pp": ([128, 1], F32),
    "thr_bc": ([128, 8], F32), "g8": ([128, 40], F32),
}

S4A_TABLE = {
    "c_idf32": ([128, 128], F32), "c_onesrow": ([1, 128], F32),
    "c_headexp": ([16, NCH * 128], F32),
    "ohts": ([32, 512], F32), "ngr": ([16, 512], F32),
    "uch": ([16, 512], F32), "uchr": ([16, 512], F32),
    "cpool": ([128, 16], F32), "scale16": ([16, 1], F32), "relb": ([32, 16], F32),
    "onesq": ([128, 64], F32), "onescol_bf": ([128, 1], BF16),
    "Wo1": ([128, NCH * D], BF16),
    "hh8": ([16, 18], F32), "row16": ([1, 16], F32),
    "CbN": ([128, 16], F32), "CbF": ([128, 16], F32),
    "Qhr0": ([128, S], BF16), "Qhr1": ([128, S], BF16),
    "Khr0": ([128, S], BF16), "Khr1": ([128, S], BF16),
    "Vs": ([128, 16 * VD], BF16), "Obf": ([128, NCH * S], BF16),
    "Qh": ([128, S], BF16), "Kh": ([128, S], BF16),
    "sqs": ([16, S], F32), "sks": ([16, S], F32),
    "u2A": ([128, WU], F32), "u2B": ([128, WU], F32),
    "ptA0": ([128, 512], BF16), "ptA1": ([128, 512], BF16),
    "ptB0": ([128, 512], BF16), "ptB1": ([128, 512], BF16),
    "mvsb": ([128, 2], F32), "xnt": ([128, D], F32), "ysb": ([128, D], F32),
    "sgA0": ([128, 512], F32), "sgA1": ([128, 512], F32),
    "sgB0": ([128, 512], F32), "sgB1": ([128, 512], F32),
    "dn0": ([128, 1024], F32), "dn1": ([128, 1024], F32),
    "stg2": ([128, 512], F32),
}

S4B_TABLE = {
    "cpool": ([128, 16], F32), "ln2_bc": ([128, D], F32),
    "W1sb": ([128, NCH * 4096], BF16), "W2sb": ([128, 32 * D], BF16),
    "t_fm0": ([128, NCH * 512], BF16), "t_fm1": ([128, NCH * 512], BF16),
    "hfm": ([128, 32 * 512], BF16),
    "yt0": ([128, D], F32), "yt1": ([128, D], F32),
    "scr": ([128, D], F32), "mv": ([128, 8], F32),
    "tbf": ([128, D], BF16), "bse": ([128, D], F32),
}

S5_TABLE = {
    "cpool": ([128, 16], F32), "gt0": ([128, 64], F32), "gt1": ([128, 64], F32),
    "acc0": ([128, NCH * D], F32), "acc1": ([128, NCH * D], F32),
    "xfm0": ([128, NCH * 1024], BF16), "xfm1": ([128, NCH * 1024], BF16),
    "W1f": ([128, NCH * MOE_F], BF16), "W2h": ([128, 16 * D], BF16),
    "Gfm": ([128, 16 * 1024], BF16),
}


def _alloc_psum_banks(nc):
    return {
        "B0": nc.alloc_psum_tensor("psb0", [128, 512], F32),
        "B1": nc.alloc_psum_tensor("psb1", [128, 512], F32),
        "B2": nc.alloc_psum_tensor("psb2", [128, 512], F32),
        "B3": nc.alloc_psum_tensor("psb3", [128, 512], F32),
        "B45": nc.alloc_psum_tensor("psb45", [128, 1024], F32),
        "B6": nc.alloc_psum_tensor("psb6", [128, 512], F32),
        "B7": nc.alloc_psum_tensor("psb7", [128, 512], F32),
    }


def build_part1():
    nc = bass.Bass("TRN2", target_bir_lowering=False)
    dr = {}
    for name, (shape, dtype, kind) in P1_SPECS.items():
        dr[name] = nc.dram_tensor(name, shape, dtype, kind=kind)

    es = contextlib.ExitStack()
    with es:
        B = _alloc_psum_banks(nc)
        P = Prog(nc)

        sb1 = make_sb(nc, S1_TABLE,
                      {"pg0": B["B0"], "pg1": B["B1"], "pg2": B["B2"], "pg3": B["B3"],
                       "pss": B["B6"], "ptf": B["B45"]})
        emit_load_consts(P, dr, sb1, ["c_idf32", "c_idbf", "c_headmask"])
        emit_stage1(P, dr, sb1)
        P.stage_barrier()

        sb2 = make_sb(nc, S2_TABLE,
                      {"pL0A": B["B0"], "pL1A": B["B1"], "pL0B": B["B2"],
                       "pL1B": B["B3"], "poA": B["B6"], "poB": B["B7"],
                       "pr": B["B45"]})
        emit_stage2(P, dr, sb2)
        P.stage_barrier()

        sb3 = make_sb(nc, S3_TABLE,
                      {"pg0": B["B0"], "pg1": B["B1"], "pg2": B["B2"], "pg3": B["B3"],
                       "pssq": B["B6"], "pssk": B["B7"], "ptf": B["B45"]})
        emit_stage3(P, dr, sb3)
        P.stage_barrier()

        sb4a = make_sb(nc, S4A_TABLE,
                       {"pL0A": B["B0"], "pL1A": B["B1"], "pL0B": B["B2"],
                        "pL1B": B["B3"], "poA": B["B6"], "poB": B["B7"],
                        "pr": B["B45"], "pmv": B["B45"]})
        emit_stage4a(P, dr, sb4a)
        P.stage_barrier()

        sb4b = make_sb(nc, S4B_TABLE,
                       {"pg0": B["B0"], "pg1": B["B1"], "pg2": B["B2"], "pg3": B["B3"]})
        emit_stage4b(P, dr, sb4b)

        P.op("sync", (lambda e: e.nop()), extra_waits=P.barrier_token())
        P.finalize(es)
    return nc


def build_part2():
    nc = bass.Bass("TRN2", target_bir_lowering=False)
    dr = {}
    for name, (shape, dtype, kind) in P2_SPECS.items():
        dr[name] = nc.dram_tensor(name, shape, dtype, kind=kind)

    es = contextlib.ExitStack()
    with es:
        B = _alloc_psum_banks(nc)
        P = Prog(nc)

        sb4g = make_sb(nc, S4G_TABLE, {"pthr": B["B0"]})
        emit_stage4g(P, dr, sb4g)
        P.stage_barrier()

        sb5 = make_sb(nc, S5_TABLE,
                      {"pg0": B["B0"], "pg1": B["B1"], "pgW0": B["B2"],
                       "pgW1": B["B3"]})
        emit_stage5(P, dr, sb5)

        P.op("sync", (lambda e: e.nop()), extra_waits=P.barrier_token())
        P.finalize(es)
    return nc


def shard_inputs_part1(inputs, consts):
    x = np.ascontiguousarray(inputs["L0_attn_x"].reshape(32 * S, D))
    mu_r = np.asarray(inputs["L0_ffn_mu_r"], dtype=np.float32)
    mu_k = np.asarray(inputs["L0_ffn_mu_k"], dtype=np.float32)
    mu_ch = np.concatenate([
        np.ascontiguousarray(mu_r.reshape(NCH, 128).T),
        np.ascontiguousarray(mu_k.reshape(NCH, 128).T)], axis=1)
    scale = np.asarray(inputs["L1_attn_scale"], np.float32)
    base = {
        "Wq": np.asarray(inputs["L0_attn_Wq"], np.float32),
        "Wk": np.asarray(inputs["L0_attn_Wk"], np.float32),
        "Wv": np.asarray(inputs["L0_attn_Wv"], np.float32),
        "Wo": np.asarray(inputs["L0_attn_Wo"], np.float32),
        "Wr": np.asarray(inputs["L0_ffn_W_r"], np.float32),
        "Wk2": np.asarray(inputs["L0_ffn_W_k"], np.float32),
        "Wv2": np.asarray(inputs["L0_ffn_W_v"], np.float32),
        "mu_ch": np.ascontiguousarray(mu_ch, np.float32),
        "gamma_bc": np.ascontiguousarray(
            np.broadcast_to(np.asarray(inputs["L0_attn_gamma"], np.float32), (128, D))),
        "beta_bc": np.ascontiguousarray(
            np.broadcast_to(np.asarray(inputs["L0_attn_beta"], np.float32), (128, D))),
        "Wq1": np.asarray(inputs["L1_attn_Wq"], np.float32),
        "Wk1": np.asarray(inputs["L1_attn_Wk"], np.float32),
        "Wv1": np.asarray(inputs["L1_attn_Wv"], np.float32),
        "Wo1": np.asarray(inputs["L1_attn_Wo"], np.float32),
        "rel_bias": np.asarray(inputs["L1_attn_rel_bias"], np.float32),
        "scale16": np.ascontiguousarray(scale.reshape(16, 1)),
        "scale128": np.ascontiguousarray(np.repeat(scale, 8).reshape(128, 1)),
        "ln1_bc": np.ascontiguousarray(
            np.broadcast_to(np.asarray(inputs["L1_attn_ln1_w"], np.float32), (128, D))),
        "ln2_bc": np.ascontiguousarray(
            np.broadcast_to(np.asarray(inputs["L1_attn_ln2_w"], np.float32), (128, D))),
        "W1": np.asarray(inputs["L1_attn_W1"], np.float32),
        "W2": np.asarray(inputs["L1_attn_W2"], np.float32),
        "Wg": np.asarray(inputs["L1_ffn_Wg"], np.float32),
    }
    for k in ("c_idf32", "c_idbf", "c_headmask", "c_headexp", "c_mask01",
              "c_onesrow", "c_oht", "c_negrow"):
        base[k] = consts[k]
    maps = []
    for c in range(N_CORES):
        m = dict(base)
        m["x_in"] = np.ascontiguousarray(x[c * T_LOC:(c + 1) * T_LOC])
        maps.append(m)
    return maps


def shard_inputs_part2(inputs, consts, res1):
    pT_all = np.concatenate([np.asarray(r["pT"], np.float32) for r in res1], axis=0)
    base = {
        "probs_gathered": np.ascontiguousarray(pT_all),
        "W1e": np.ascontiguousarray(
            np.asarray(inputs["L1_ffn_W1"], np.float32).reshape(MOE_E * D, MOE_F)),
        "W2e": np.ascontiguousarray(
            np.asarray(inputs["L1_ffn_W2"], np.float32).reshape(MOE_E * MOE_F, D)),
        "c_iota8": consts["c_iota8"], "c_iotaB": consts["c_iotaB"],
        "c_onesrow": consts["c_onesrow"], "c_p2e": consts["c_p2e"],
    }
    maps = []
    for c in range(N_CORES):
        m = dict(base)
        for k in ("probs", "xn_fm_bf", "base"):
            m[k] = np.asarray(res1[c][k])
        maps.append(m)
    return maps


def kernel(**inputs):
    inputs = {k: np.asarray(v) for k, v in inputs.items()}
    consts = host_consts()
    res1 = run_bass_kernel_spmd(build_part1(), shard_inputs_part1(inputs, consts),
                                core_ids=list(range(N_CORES))).results
    res2 = run_bass_kernel_spmd(build_part2(), shard_inputs_part2(inputs, consts, res1),
                                core_ids=list(range(N_CORES))).results
    out = np.concatenate([np.asarray(r["y_out"], np.float32) for r in res2], axis=0)
    return out.reshape(32, 2048, 1024)


# version 13 (final)
# speedup vs baseline: 1.0602x; 1.0602x over previous
import contextlib
import math

import numpy as np
import ml_dtypes

import concourse.bass as bass
import concourse.mybir as mybir
from concourse.ap import AP
from concourse.bass_utils import run_bass_kernel_spmd

F32 = mybir.dt.float32
BF16 = mybir.dt.bfloat16
AF = mybir.ActivationFunctionType
ALU = mybir.AluOpType
AX = mybir.AxisListType

N_CORES = 8
S = 2048
B_LOC = 4
T_LOC = S * B_LOC
D = 1024
H = 16
HD = 64
NCH = 8
NTILE = T_LOC // 128
LN_EPS0 = 1e-5
RMS_EPS = 1e-6
KQ_EPS = 1e-6
MOE_E = 8
MOE_F = 2048
CAP = (N_CORES * T_LOC) // MOE_E
VD = 16 * 65

BFNP = ml_dtypes.bfloat16


class CH:
    __slots__ = ("name", "n", "sem")

    def __init__(self, name):
        self.name = name
        self.n = 0
        self.sem = None


class Prog:
    ENGINES = ("sync", "tensor", "vector", "scalar", "gpsimd")

    def __init__(self, nc):
        self.nc = nc
        self.ops = {e: [] for e in self.ENGINES}
        self.ch = {e: CH("c_" + e) for e in self.ENGINES}
        self.dch = {q: CH("d_" + q) for q in ("g", "sy", "sc")}
        self.chains = list(self.ch.values()) + list(self.dch.values())
        self.last_w = {}
        self.readers = {}

    def _dep_add(self, deps, entry, self_chain, is_pe):
        for c, v in entry.items():
            if is_pe and c is self.ch["tensor"]:
                continue
            deps[c] = max(deps.get(c, 0), v)

    def op(self, eng, emit, reads=(), writes=(), no_inc=False, dq=None,
           extra_waits=(), inc_amt=None):
        chain = self.dch[dq] if dq else self.ch[eng]
        is_pe = (eng == "tensor") and dq is None
        deps = {}
        for (c, v) in extra_waits:
            deps[c] = max(deps.get(c, 0), v)
        for b in reads:
            if b in self.last_w:
                self._dep_add(deps, self.last_w[b], chain, is_pe)
        for b in writes:
            if b in self.last_w:
                self._dep_add(deps, self.last_w[b], chain, is_pe)
            if b in self.readers:
                self._dep_add(deps, self.readers[b], chain, is_pe)
        amt = inc_amt if inc_amt is not None else (16 if dq else 1)
        if no_inc:
            val = chain.n + amt
            inc = None
        else:
            chain.n += amt
            val = chain.n
            inc = chain
        tok = (chain, val)
        for b in reads:
            self.readers.setdefault(b, {})
            self.readers[b][chain] = val
        for b in writes:
            self.last_w[b] = {chain: val}
            self.readers[b] = {}
        waits = sorted(((c, v) for c, v in deps.items()), key=lambda t: t[0].name)
        self.ops[eng].append((waits, emit, inc, amt))
        return tok

    def dma(self, q, out, in_, reads=(), writes=(), **kw):
        eng = {"g": "gpsimd", "sy": "sync", "sc": "scalar"}[q]
        return self.op(eng, (lambda e, o=out, i=in_, kw=kw: e.dma_start(out=o, in_=i, **kw)),
                       reads=reads, writes=writes, dq=q)

    def barrier_token(self):
        return [(c, c.n) for c in self.chains if c.n > 0]

    def stage_barrier(self):
        tok = self.barrier_token()
        for eng in self.ENGINES:
            self.op(eng, (lambda e: e.nop()), extra_waits=tok)

    def finalize(self, es):
        for c in self.chains:
            c.sem = es.enter_context(self.nc.semaphore(c.name))
        blk = es.enter_context(self.nc.Block())
        for ename in self.ENGINES:
            ops = self.ops[ename]
            if not ops:
                continue

            def mk(ops):
                def body(eng):
                    hw = {}
                    for waits, emit, inc, amt in ops:
                        for (c, v) in waits:
                            if hw.get(id(c), 0) < v:
                                eng.wait_ge(c.sem, v)
                                hw[id(c)] = v
                        ins = emit(eng)
                        if inc is not None:
                            ins.then_inc(inc.sem, amt)
                return body

            getattr(blk, ename)(mk(ops))


def host_consts():
    c = {}
    c["c_idf32"] = np.eye(128, dtype=np.float32)
    c["c_idbf"] = np.eye(128, dtype=np.float32).astype(BFNP)
    i = np.arange(128)[:, None]
    cc = np.arange(256)[None, :]
    c["c_mask01"] = np.where(i <= cc - 128, 1.0, 0.0).astype(np.float32)
    c["c_onesrow"] = np.ones((1, 128), dtype=np.float32)
    hm = np.zeros((128, NCH, H), dtype=np.float32)
    he = np.zeros((16, NCH, 128), dtype=np.float32)
    for ch in range(NCH):
        hm[0:64, ch, 2 * ch] = 1.0
        hm[64:128, ch, 2 * ch + 1] = 1.0
        he[2 * ch, ch, 0:64] = 1.0
        he[2 * ch + 1, ch, 64:128] = 1.0
    c["c_headmask"] = np.ascontiguousarray(hm.reshape(128, NCH * H))
    c["c_headexp"] = np.ascontiguousarray(he.reshape(16, NCH * 128))
    oht, negrow, doff, nd = l1_bias_consts()
    c["c_oht"] = oht
    c["c_negrow"] = np.broadcast_to(negrow, (16, nd)).copy()
    iota = np.broadcast_to(np.arange(8, dtype=np.float32), (128, 8)).copy()
    c["c_iota8"] = iota
    c["c_iotaB"] = (iota - 1000.0).astype(np.float32)
    c["c_p2e"] = (np.arange(128)[:, None] % 8 == np.arange(8)[None, :]).astype(np.float32)
    return c


def l1_bias_consts():
    NB = 32
    MAXD = 128
    DOFF = 1024
    ND = 4096
    d = np.arange(ND) - DOFF
    rp = np.maximum(d, 0)
    max_exact = NB // 2
    is_small = rp < max_exact
    safe = np.maximum(rp, 1).astype(np.float32)
    large = max_exact + (
        (np.log(safe / np.float32(max_exact)) / np.float32(math.log(MAXD / max_exact)))
        * (NB - max_exact)
    ).astype(np.int32)
    large = np.minimum(large, NB - 1)
    bucket = np.where(is_small, rp, large)
    oht = np.zeros((NB, ND), dtype=np.float32)
    valid = d >= 1
    oht[bucket[valid], np.arange(ND)[valid]] = 1.0
    negrow = np.where(d <= 0, -1e9, 0.0).astype(np.float32)
    return oht, negrow, DOFF, ND


def v3(t, p_pitch, offset, dims):
    return AP(t[0:1, 0:1].tensor, offset, dims)


def emit_stats_tail(P, sb, nm_name, rs_name, eps, inner_name=None, inner_eps=None):
    mv = sb["s_mv"]
    mean_view = v3(mv, None, 0, [[128, 128], [2, 64]])
    var_view = v3(mv, None, 1, [[128, 128], [2, 64]])
    P.op("vector", (lambda e, o=sb[nm_name][:, :], i=mean_view:
                    e.tensor_scalar(o, i, -1.0, None, ALU.mult)),
         reads=["s_mv"], writes=[nm_name])
    fc0 = fconst(P, sb, 0.0)
    bias_eps = fconst(P, sb, eps)
    P.op("scalar", (lambda e, o=sb["s_t64"][:, :], i=var_view, b=bias_eps:
                    e.activation(o, i, AF.Ln, bias=b)),
         reads=["s_mv", "cpool"], writes=["s_t64"])
    P.op("scalar", (lambda e, o=sb[rs_name][:, :], i=sb["s_t64"][:, :]:
                    e.activation(o, i, AF.Exp, scale=-0.5, bias=fc0)),
         reads=["s_t64", "cpool"], writes=[rs_name])
    if inner_name is not None:
        P.op("vector", (lambda e, o=sb["s_t64"][:, :], a=sb[rs_name][:, :]:
                        e.tensor_mul(o, a, a)),
             reads=[rs_name], writes=["s_t64"])
        P.op("vector", (lambda e, o=sb["s_t64"][:, :], a=sb["s_t64"][:, :], b=var_view:
                        e.tensor_mul(o, a, b)),
             reads=["s_t64", "s_mv"], writes=["s_t64"])
        bias_ieps = fconst(P, sb, inner_eps)
        P.op("scalar", (lambda e, o=sb["s_t64b"][:, :], i=sb["s_t64"][:, :], b=bias_ieps:
                        e.activation(o, i, AF.Ln, bias=b)),
             reads=["s_t64", "cpool"], writes=["s_t64b"])
        P.op("scalar", (lambda e, o=sb[inner_name][:, :], i=sb["s_t64b"][:, :]:
                        e.activation(o, i, AF.Exp, scale=-0.5, bias=fc0)),
             reads=["s_t64b", "cpool"], writes=[inner_name])


def emit_stats_tiles(P, sb, x, tiles):
    bn6 = sb["s_bn6"]
    mv = sb["s_mv"]
    for ti in tiles:
        xb = sb["xs%d" % (ti % 2)]
        xtag = "xs%d" % (ti % 2)
        P.dma("g", xb[:, :], x[ti, :, :], writes=[xtag])
        P.op("vector", (lambda e, o=bn6[:, 0:6], i=xb[:, 0:512]: e.bn_stats(o, i)),
             reads=[xtag], writes=["s_bn6"])
        P.op("vector", (lambda e, o=bn6[:, 6:12], i=xb[:, 512:1024]: e.bn_stats(o, i)),
             reads=[xtag], writes=["s_bn6"])
        P.op("vector", (lambda e, o=mv[:, 2 * ti:2 * ti + 2], i=bn6[:, :]: e.bn_aggr(o, i)),
             reads=["s_bn6"], writes=["s_mv"])


def emit_stats_pass(P, dr, sb, x_dram_name, nm_name, rs_name, eps, inner_name=None,
                    inner_eps=None):
    x = dr[x_dram_name].ap().rearrange("(n p) d -> n p d", p=128)
    emit_stats_tiles(P, sb, x, range(NTILE))
    emit_stats_tail(P, sb, nm_name, rs_name, eps, inner_name, inner_eps)


def emit_load_consts(P, dr, sb, names):
    for nm in names:
        P.dma("g", sb[nm][:, :], dr[nm].ap(), writes=[nm])


def fconst(P, sb, val):
    d = sb.setdefault("_cpool_map", {})
    if val not in d:
        idx = len(d)
        assert idx < 16
        d[val] = idx
        P.op("vector", (lambda e, idx=idx, val=val: e.memset(
            sb["cpool"][:, idx:idx + 1], float(val))), writes=["cpool"])
    return sb["cpool"][:, d[val]:d[val] + 1]


S1_WNAMES = ["Wq", "Wk", "Wv", "Wr", "Wk2", "Wv2"]


def emit_stage1(P, dr, sb):
    x = dr["x_in"].ap().rearrange("(n p) d -> n p d", p=128)
    xB = dr["xB"].ap().rearrange("(n p) d -> n p d", p=128)
    v0 = dr["v0_tm"].ap().rearrange("(n p) d -> n p d", p=128)
    q0 = dr["q0_fm"].ap()
    k0 = dr["k0_fm"].ap()
    ssq0 = dr["ssq0"].ap()
    ssk0 = dr["ssk0"].ap()

    xn32 = sb["xn32g"]
    xnbf = sb["xnbf"]
    dbuf = sb["dbuf"]
    xrb = sb["xrb"]
    xkb = sb["xkb"]
    kact = sb["kact"]

    emit_stats_pass(P, dr, sb, "x_in", "s_nm", "s_rs", LN_EPS0)

    for nm in S1_WNAMES:
        P.dma("g", dr[nm + "_bf"].ap(), dr[nm].ap(), writes=[nm + "_bf"])
        for kc in range(NCH):
            P.dma("sc", sb["WT_" + nm][:, kc * D:(kc + 1) * D],
                  dr[nm + "_bf"].ap()[:, kc * 128:(kc + 1) * 128],
                  reads=[nm + "_bf"], writes=["WT_" + nm], transpose=True)
    P.dma("g", sb["mu_ch"][:, :], dr["mu_ch"].ap(), writes=["mu_ch"])
    P.op("vector", (lambda e: e.tensor_scalar(sb["mu1"][:, :], sb["mu_ch"][:, :],
                                              -1.0, None, ALU.add)),
         reads=["mu_ch"], writes=["mu1"])
    for t in ("v_sb0", "v_sb1"):
        P.op("gpsimd", (lambda e, t=t: e.memset(
            AP(sb[t][0:1, 0:1].tensor, 64, [[VD, 128], [65, 16]]), 1.0)), writes=[t])

    def xn32_v(kc, j0, n, extra=None):
        return AP(xn32[0:1, 0:1].tensor, kc * 513 + j0,
                  [[NCH * 513, 128], [1, n]] if extra is None else extra)

    def emit_group(g):
        seq_start = (g % 4 == 0)
        if seq_start:
            P.op("vector", (lambda e: e.memset(
                AP(xn32[0:1, 0:1].tensor, 0, [[NCH * 513, 128], [513, NCH], [1, 1]]), 0.0)),
                 writes=["xn32g_c0"], reads=[])
        for ti in range(4):
            gi = 4 * g + ti
            xs = sb["xs%d" % (ti % 2)]
            xst = "xs%d" % (ti % 2)
            P.dma("g", xs[:, :], x[gi, :, :], writes=[xst])
            P.op("vector", (lambda e, xs=xs, gi=gi: e.tensor_scalar(
                sb["xn_tm"][:, :], xs[:, :],
                sb["s_nm"][:, gi:gi + 1], sb["s_rs"][:, gi:gi + 1],
                ALU.add, ALU.mult)),
                 reads=[xst, "s_nm", "s_rs"], writes=["xn_tm"])
            for kc in range(NCH):
                pslc = sb["ptf"][:, kc * 128:(kc + 1) * 128]
                P.op("tensor", (lambda e, kc=kc, pslc=pslc: e.transpose(
                    pslc, sb["xn_tm"][:, kc * 128:(kc + 1) * 128], sb["c_idf32"][:, :])),
                     reads=["xn_tm", "c_idf32"],
                     writes=["ptf%d" % kc, "ptfB%d" % (kc // 4)])
                P.op("vector", (lambda e, kc=kc, ti=ti, pslc=pslc: e.tensor_copy(
                    xn32_v(kc, 1 + ti * 128, 128), pslc)),
                     reads=["ptf%d" % kc], writes=["xn32g", "ptfB%d" % (kc // 4)])
                P.op("scalar", (lambda e, kc=kc, ti=ti, pslc=pslc: e.copy(
                    AP(xnbf[0:1, 0:1].tensor, kc * 512 + ti * 128,
                       [[NCH * 512, 128], [1, 128]]), pslc)),
                     reads=["ptf%d" % kc], writes=["xnbf", "ptfB%d" % (kc // 4)])
        for kc in range(NCH):
            P.op("vector", (lambda e, kc=kc: e.tensor_sub(
                dbuf[:, :], xn32_v(kc, 1, 512), xn32_v(kc, 0, 512))),
                 reads=["xn32g", "xn32g_c0"], writes=["dbuf"])
            for (dst, dstt, mucol) in ((xrb, "xrb", 0), (xkb, "xkb", 1)):
                P.op("vector", (lambda e, kc=kc, dst=dst, mucol=mucol: e.scalar_tensor_tensor(
                    AP(dst[0:1, 0:1].tensor, kc * 512, [[NCH * 512, 128], [1, 512]]),
                    dbuf[:, :],
                    sb["mu1"][:, mucol * NCH + kc: mucol * NCH + kc + 1],
                    xn32_v(kc, 1, 512),
                    op0=ALU.mult, op1=ALU.add)),
                     reads=["dbuf", "mu1", "xn32g"], writes=[dstt])
        P.op("vector", (lambda e: e.tensor_copy(
            AP(xn32[0:1, 0:1].tensor, 0, [[NCH * 513, 128], [513, NCH], [1, 1]]),
            AP(xn32[0:1, 0:1].tensor, 512, [[NCH * 513, 128], [513, NCH], [1, 1]]))),
             reads=["xn32g"], writes=["xn32g_c0", "xn32g_c0w"])

        for dc in range(NCH):
            pg = sb["pg%d" % (dc % 2)]
            pgt = "pg%d" % (dc % 2)
            for kc in range(NCH):
                P.op("tensor", (lambda e, dc=dc, kc=kc, pg=pg: e.matmul(
                    pg[:, :], sb["WT_Wk2"][:, kc * D + dc * 128: kc * D + dc * 128 + 128],
                    AP(xkb[0:1, 0:1].tensor, kc * 512, [[NCH * 512, 128], [1, 512]]),
                    start=(kc == 0), stop=(kc == NCH - 1))),
                     reads=["WT_Wk2", "xkb"], writes=[pgt], no_inc=(kc < NCH - 1))
            P.op("vector", (lambda e, pg=pg: e.tensor_scalar(
                sb["relu_t"][:, :], pg[:, :], 0.0, None, ALU.max)),
                 reads=[pgt], writes=["relu_t"])
            P.op("scalar", (lambda e, dc=dc, b=fconst(P, sb, 0.0): e.activation(
                AP(kact[0:1, 0:1].tensor, dc * 512, [[NCH * 512, 128], [1, 512]]),
                sb["relu_t"][:, :], AF.Square, bias=b)),
                 reads=["relu_t", "cpool"], writes=["kact"])

        for ti in range(4):
            gi = 4 * g + ti
            xf = sb["xs%d" % (ti % 2)]
            xft = "xs%d" % (ti % 2)
            P.dma("g", xf[:, :], x[gi, :, :], writes=[xft])
            for dh in range(2):
                pg = sb["pg2"]
                for kc in range(NCH):
                    P.op("tensor", (lambda e, ti=ti, dh=dh, kc=kc, pg=pg: e.matmul(
                        pg[:, :],
                        AP(xrb[0:1, 0:1].tensor, kc * 512 + ti * 128,
                           [[NCH * 512, 128], [1, 128]]),
                        sb["WT_Wr"][:, kc * D + dh * 512: kc * D + dh * 512 + 512],
                        start=(kc == 0), stop=(kc == NCH - 1))),
                         reads=["xrb", "WT_Wr"], writes=["pg2"], no_inc=(kc < NCH - 1))
                P.op("scalar", (lambda e, dh=dh, pg=pg, b=fconst(P, sb, 0.0): e.activation(
                    sb["r_sb"][:, dh * 512:dh * 512 + 512], pg[:, :], AF.Sigmoid, bias=b)),
                     reads=["pg2", "cpool"], writes=["r_sb"])
            for dh in range(2):
                pg = sb["pg3"]
                for kc in range(NCH):
                    P.op("tensor", (lambda e, ti=ti, dh=dh, kc=kc, pg=pg: e.matmul(
                        pg[:, :],
                        AP(kact[0:1, 0:1].tensor, kc * 512 + ti * 128,
                           [[NCH * 512, 128], [1, 128]]),
                        sb["WT_Wv2"][:, kc * D + dh * 512: kc * D + dh * 512 + 512],
                        start=(kc == 0), stop=(kc == NCH - 1))),
                         reads=["kact", "WT_Wv2"], writes=["pg3"], no_inc=(kc < NCH - 1))
                P.op("vector", (lambda e, dh=dh, pg=pg: e.tensor_mul(
                    sb["b_sb"][:, :], pg[:, :], sb["r_sb"][:, dh * 512:dh * 512 + 512])),
                     reads=["pg3", "r_sb"], writes=["b_sb"])
                P.op("vector", (lambda e, dh=dh, xf=xf: e.tensor_add(
                    sb["xB_sb"][:, dh * 512:dh * 512 + 512], sb["b_sb"][:, :],
                    xf[:, dh * 512:dh * 512 + 512])),
                     reads=["b_sb", xft], writes=["xB_sb"])
            P.dma("sy", xB[gi, :, :], sb["xB_sb"][:, :], reads=["xB_sb"], writes=["xB_sb_dma"])
            for dh in range(2):
                pg = sb["pg%d" % (dh % 2)]
                pgt = "pg%d" % (dh % 2)
                for kc in range(NCH):
                    P.op("tensor", (lambda e, ti=ti, dh=dh, kc=kc, pg=pg: e.matmul(
                        pg[:, :],
                        AP(xnbf[0:1, 0:1].tensor, kc * 512 + ti * 128,
                           [[NCH * 512, 128], [1, 128]]),
                        sb["WT_Wv"][:, kc * D + dh * 512: kc * D + dh * 512 + 512],
                        start=(kc == 0), stop=(kc == NCH - 1))),
                         reads=["xnbf", "WT_Wv"], writes=[pgt], no_inc=(kc < NCH - 1))
                vsb = sb["v_sb%d" % (ti % 2)]
                P.op("vector", (lambda e, dh=dh, pg=pg, vsb=vsb: e.tensor_copy(
                    AP(vsb[0:1, 0:1].tensor, dh * 32, [[VD, 128], [1, 32], [65, 16]]),
                    AP(pg[0:1, 0:1].tensor, 0, [[512, 128], [16, 32], [1, 16]]))),
                     reads=[pgt], writes=["v_sb%d" % (ti % 2)])
            P.dma("sy", v0[gi, :, :], sb["v_sb%d" % (ti % 2)][:, :],
                  reads=["v_sb%d" % (ti % 2)], writes=["v_dma%d" % (ti % 2)])

        for (wname, dst_fm, ss_dst) in (("Wq", q0, ssq0), ("Wk", k0, ssk0)):
            for dc in range(NCH):
                pg = sb["pg%d" % (2 + dc % 2)]
                pgt = "pg%d" % (2 + dc % 2)
                for kc in range(NCH):
                    P.op("tensor", (lambda e, wname=wname, dc=dc, kc=kc, pg=pg: e.matmul(
                        pg[:, :],
                        sb["WT_" + wname][:, kc * D + dc * 128: kc * D + dc * 128 + 128],
                        AP(xnbf[0:1, 0:1].tensor, kc * 512, [[NCH * 512, 128], [1, 512]]),
                        start=(kc == 0), stop=(kc == NCH - 1))),
                         reads=["WT_" + wname, "xnbf"], writes=[pgt], no_inc=(kc < NCH - 1))
                qsb = sb["q_sb%d" % (dc % 2)]
                P.op("scalar", (lambda e, pg=pg, qsb=qsb: e.copy(qsb[:, :], pg[:, :])),
                     reads=[pgt], writes=["q_sb%d" % (dc % 2)])
                P.dma("sy", AP(dst_fm.tensor, (dc * 128) * T_LOC + g * 512,
                               [[T_LOC, 128], [1, 512]]),
                      qsb[:, :], reads=["q_sb%d" % (dc % 2)], writes=["qdma%d" % (dc % 2)])
                P.op("scalar", (lambda e, pg=pg, b=fconst(P, sb, 0.0): e.activation(
                    sb["q2"][:, :], pg[:, :], AF.Square, bias=b)),
                     reads=[pgt, "cpool"], writes=["q2"])
                P.op("tensor", (lambda e, dc=dc: e.matmul(
                    sb["pss"][0:16, :], sb["c_headmask"][:, dc * 16:dc * 16 + 16],
                    sb["q2"][:, :], start=(dc == 0), stop=(dc == NCH - 1))),
                     reads=["c_headmask", "q2"], writes=["pss"], no_inc=(dc < NCH - 1))
            P.op("vector", (lambda e: e.tensor_copy(sb["ss_sb"][0:16, :], sb["pss"][0:16, :])),
                 reads=["pss"], writes=["ss_sb"])
            P.dma("sy", AP(ss_dst.tensor, g * 512, [[T_LOC, 16], [1, 512]]),
                  sb["ss_sb"][0:16, :], reads=["ss_sb"], writes=["ss_dma"])

    for g in range(16):
        emit_group(g)

    emit_stats_tiles(P, sb, x, range(4))
    emit_stats_tail(P, sb, "s_nm", "s_rs", LN_EPS0)
    emit_group(0)


def emit_stage2(P, dr, sb):
    P.op("gpsimd", (lambda e: e.memset(sb["onesq"][:, :], 1.0)), writes=["onesq"])
    P.dma("g", dr["Wo_bf"].ap(), dr["Wo"].ap(), writes=["Wo_bf"])
    for kc in range(NCH):
        P.dma("sc", sb["WT_Wo"][:, kc * D:(kc + 1) * D],
              dr["Wo_bf"].ap()[:, kc * 128:(kc + 1) * 128],
              reads=["Wo_bf"], writes=["WT_Wo"], transpose=True)
    emit_load_consts(P, dr, sb, ["c_mask01", "c_idf32", "c_onesrow", "gamma_bc", "beta_bc"])
    fc0 = fconst(P, sb, 0.0)

    q0 = dr["q0_fm"].ap()
    k0 = dr["k0_fm"].ap()
    v0 = dr["v0_tm"].ap()
    xB = dr["xB"].ap().rearrange("(n p) d -> n p d", p=128)
    x1 = dr["x1"].ap().rearrange("(n p) d -> n p d", p=128)

    Vs, O32 = sb["Vs"], sb["O32"]

    for b in range(B_LOC):
        for vch in range(4):
            P.dma("g", Vs[:, vch * 4 * VD:(vch + 1) * 4 * VD],
                  AP(v0.tensor, (b * S + vch * 4 * 128) * VD,
                     [[VD, 128], [128 * VD, 4], [1, VD]]),
                  writes=["Vs%d" % vch])
        for (nm, dst0) in (("ssq0", 0), ("ssk0", 1)):
            for c4 in range(4):
                P.dma("g", sb["recx"][0:16, 0:512],
                      AP(dr[nm].ap().tensor, b * S + c4 * 512, [[T_LOC, 16], [1, 512]]),
                      writes=["recx"])
                P.op("vector", (lambda e, c4=c4: e.reduce_max(
                    sb["mx2"][0:16, 8 + c4:9 + c4], sb["recx"][0:16, 0:512], axis=AX.X)),
                     reads=["recx"], writes=["mx2p"])
            P.op("vector", (lambda e, dst0=dst0: e.reduce_max(
                sb["mx2"][0:16, dst0:dst0 + 1], sb["mx2"][0:16, 8:12], axis=AX.X)),
                 reads=["mx2p"], writes=["mx2"])
        P.op("vector", (lambda e: e.tensor_mul(sb["mx2"][0:16, 2:3], sb["mx2"][0:16, 0:1],
                                               sb["mx2"][0:16, 1:2])),
             reads=["mx2"], writes=["mx2b"])
        P.op("scalar", (lambda e: e.activation(sb["mx2"][0:16, 3:4], sb["mx2"][0:16, 2:3],
                                               AF.Ln, bias=fc0[0:16, :])),
             reads=["mx2b", "cpool"], writes=["mx2c"])
        P.op("scalar", (lambda e: e.activation(sb["mx2"][0:16, 4:5], sb["mx2"][0:16, 3:4],
                                               AF.Exp, scale=0.5, bias=fc0[0:16, :])),
             reads=["mx2c", "cpool"], writes=["mx2d"])
        P.op("vector", (lambda e: e.tensor_scalar(sb["mx2"][0:16, 5:6], sb["mx2"][0:16, 4:5],
                                                  -0.127551, None, ALU.mult)),
             reads=["mx2d"], writes=["mx2e"])
        P.op("tensor", (lambda e: e.transpose(sb["pr"][0:1, 0:16], sb["mx2"][0:16, 5:6],
                                              sb["c_idf32"][0:16, 0:16])),
             reads=["mx2e", "c_idf32"], writes=["prA"])
        P.op("vector", (lambda e: e.tensor_copy(sb["negC_row"][0:1, 0:16], sb["pr"][0:1, 0:16])),
             reads=["prA"], writes=["negC_row"])
        P.op("tensor", (lambda e: e.matmul(sb["pr"][:, 0:16], sb["c_onesrow"][0:1, :],
                                           sb["negC_row"][0:1, 0:16], start=True, stop=True)),
             reads=["c_onesrow", "negC_row"], writes=["prA"])
        P.op("vector", (lambda e: e.tensor_copy(sb["Cb"][:, :], sb["pr"][:, 0:16])),
             reads=["prA"], writes=["Cb"])

        pend = [None]
        for hp in range(NCH):
            hA, hB = 2 * hp, 2 * hp + 1
            for (tb, src) in (("Qhr", q0), ("Khr", k0)):
                t = "%s%d" % (tb, hp % 2)
                P.dma("g", sb[t][:, :],
                      AP(src.tensor, (hp * 128) * T_LOC + b * S, [[T_LOC, 128], [1, S]]),
                      writes=[t])
            Qhr = sb["Qhr%d" % (hp % 2)]
            Khr = sb["Khr%d" % (hp % 2)]
            qt = "Qhr%d" % (hp % 2)
            kt_t = "Khr%d" % (hp % 2)
            for qc in range(4):
                nkt = 4 * qc + 4

                def s2_qk(kt, qc=qc, Qhr=Qhr, Khr=Khr, qt=qt, kt_t=kt_t):
                    qoff = max(0, (kt - 4 * qc) * 128)
                    w = 512 - qoff
                    par = kt % 2
                    for (hh, pL, base, tp) in ((hA, "pL%dA" % par, 0, (0, 0)),
                                               (hB, "pL%dB" % par, 64, (64, 0))):
                        P.op("tensor", (lambda e, kt=kt, qc=qc, qoff=qoff, w=w,
                                        pL=pL, base=base, tp=tp,
                                        Qhr=Qhr, Khr=Khr: e.matmul(
                            sb[pL][:, qoff:512],
                            AP(Khr[0:1, 0:1].tensor, base * S + kt * 128,
                               [[S, 64], [1, 128]]),
                            AP(Qhr[0:1, 0:1].tensor, base * S + qc * 512 + qoff,
                               [[S, 64], [1, w]]),
                            start=True, stop=True, tile_position=tp)),
                             reads=[qt, kt_t], writes=[pL])

                def s2_post(kt, qc=qc, nkt=nkt, hA=hA, hB=hB):
                    qoff = max(0, (kt - 4 * qc) * 128)
                    par = kt % 2
                    for (hh, pL, pt) in ((hA, "pL%dA" % par, "ptA%d" % par),
                                         (hB, "pL%dB" % par, "ptB%d" % par)):
                        P.op("scalar", (lambda e, pL=pL, pt=pt, qoff=qoff, hh=hh: e.activation(
                            sb[pt][:, qoff:512], sb[pL][:, qoff:512], AF.Exp,
                            scale=0.125, bias=sb["Cb"][:, hh:hh + 1])),
                             reads=[pL, "Cb"], writes=[pt])
                        if kt >= 4 * qc:
                            P.op("vector", (lambda e, pt=pt, qoff=qoff: e.tensor_tensor(
                                sb[pt][:, qoff:qoff + 128], sb[pt][:, qoff:qoff + 128],
                                sb["c_mask01"][:, 128:256], ALU.mult)),
                                 reads=["c_mask01", pt], writes=[pt])
                    first, last = (kt == 0), (kt == nkt - 1)
                    for (hh, pt, po) in ((hA, "ptA%d" % par, "poA"),
                                         (hB, "ptB%d" % par, "poB")):
                        P.op("tensor", (lambda e, hh=hh, pt=pt, po=po, kt=kt,
                                        qoff=qoff, first=first, last=last: e.matmul(
                            sb[po][0:65, qoff:512],
                            AP(Vs[0:1, 0:1].tensor, kt * VD + hh * 65,
                               [[16 * VD, 128], [1, 65]]),
                            sb[pt][:, qoff:512],
                            start=first, stop=last)),
                             reads=["Vs%d" % (kt // 4), pt], writes=[po],
                             no_inc=not last)

                for kt in range(nkt):
                    s2_qk(kt)
                    if kt > 0:
                        s2_post(kt - 1)
                    if kt == 1 and pend[0] is not None:
                        pend[0]()
                        pend[0] = None
                s2_post(nkt - 1)
                p2 = qc % 2
                sgA, sgAt = sb["sgA%d" % p2], "sgA%d" % p2
                sgB, sgBt = sb["sgB%d" % p2], "sgB%d" % p2
                dn, dnt = sb["dn%d" % p2], "dn%d" % p2
                st2, st2t = sb["stg2"], "stg2"
                P.op("vector", (lambda e, sgA=sgA: e.tensor_copy(
                    sgA[0:65, 0:512], sb["poA"][0:65, 0:512])),
                     reads=["poA"], writes=[sgAt])
                P.op("scalar", (lambda e, sgB=sgB: e.copy(
                    sgB[0:65, 0:512], sb["poB"][0:65, 0:512])),
                     reads=["poB"], writes=[sgBt])

                def evac(hp=hp, qc=qc, sgA=sgA, sgAt=sgAt, sgB=sgB, sgBt=sgBt,
                         dn=dn, dnt=dnt, st2=st2, st2t=st2t):
                    P.op("vector", (lambda e: e.reciprocal(
                        dn[64:65, 0:512], sgA[64:65, 0:512])),
                         reads=[sgAt], writes=[dnt + "A"])
                    P.op("vector", (lambda e: e.reciprocal(
                        dn[64:65, 512:1024], sgB[64:65, 0:512])),
                         reads=[sgBt], writes=[dnt + "B"])
                    P.op("tensor", (lambda e: e.matmul(
                        sb["pr"][0:64, 0:512], sb["onesq"][64:65, 0:64],
                        dn[64:65, 0:512],
                        start=True, stop=True, tile_position=(64, 0))),
                         reads=[dnt + "A", "onesq"], writes=["prA"])
                    P.op("tensor", (lambda e: e.matmul(
                        sb["pr"][0:64, 512:1024], sb["onesq"][64:65, 0:64],
                        dn[64:65, 512:1024],
                        start=True, stop=True, tile_position=(64, 0))),
                         reads=[dnt + "B", "onesq"], writes=["pmv"])
                    P.op("vector", (lambda e: e.tensor_tensor(
                        AP(O32[0:1, 0:1].tensor, hp * S + qc * 512,
                           [[NCH * S, 64], [1, 512]]),
                        sgA[0:64, 0:512], sb["pr"][0:64, 0:512], ALU.mult)),
                         reads=[sgAt, "prA"], writes=["O32"])
                    P.op("vector", (lambda e: e.tensor_tensor(
                        st2[0:64, 0:512], sgB[0:64, 0:512],
                        sb["pr"][0:64, 512:1024], ALU.mult)),
                         reads=[sgBt, "pmv"], writes=[st2t])
                    P.op("tensor", (lambda e: e.matmul(
                        sb["pr"][64:128, 0:512], sb["c_idf32"][0:64, 0:64],
                        st2[0:64, 0:512],
                        start=True, stop=True, tile_position=(0, 64))),
                         reads=[st2t, "c_idf32"], writes=["prA"])
                    P.op("vector", (lambda e: e.tensor_copy(
                        AP(O32[0:1, 0:1].tensor, 64 * (NCH * S) + hp * S + qc * 512,
                           [[NCH * S, 64], [1, 512]]),
                        sb["pr"][64:128, 0:512])),
                         reads=["prA"], writes=["O32"])

                pend[0] = evac

        if pend[0] is not None:
            pend[0]()
            pend[0] = None
        for g in range(4):
            for ti in range(4):
                tt = 4 * g + ti
                for kc in range(NCH):
                    slot = kc % 4
                    bankn = "pL%d%s" % (slot % 2, "A" if slot < 2 else "B")
                    pslc = sb[bankn][:, (kc // 4) * 128 + (slot // 2) * 256:][:, 0:128]
                    P.op("tensor", (lambda e, kc=kc, tt=tt, pslc=pslc: e.transpose(
                        pslc,
                        AP(O32[0:1, 0:1].tensor, kc * S + tt * 128, [[NCH * S, 128], [1, 128]]),
                        sb["c_idf32"][:, :])),
                         reads=["O32", "c_idf32"], writes=[bankn])
                    P.op("vector", (lambda e, kc=kc, pslc=pslc: e.tensor_copy(
                        sb["otm"][:, kc * 128:(kc + 1) * 128], pslc)),
                         reads=[bankn], writes=["otm"])
                P.op("vector", (lambda e: e.bn_stats(sb["bn6"][:, 0:6], sb["otm"][:, 0:512])),
                     reads=["otm"], writes=["bn6"])
                P.op("vector", (lambda e: e.bn_stats(sb["bn6"][:, 6:12], sb["otm"][:, 512:1024])),
                     reads=["otm"], writes=["bn6"])
                P.op("vector", (lambda e: e.bn_aggr(sb["mv"][:, 0:2], sb["bn6"][:, :])),
                     reads=["bn6"], writes=["mv"])
                P.op("vector", (lambda e: e.tensor_scalar(sb["mv"][:, 2:3], sb["mv"][:, 0:1],
                                                          -1.0, None, ALU.mult)),
                     reads=["mv"], writes=["mv2"])
                P.op("scalar", (lambda e, bb=fconst(P, sb, LN_EPS0): e.activation(
                    sb["mv"][:, 3:4], sb["mv"][:, 1:2], AF.Ln, bias=bb)),
                     reads=["mv", "cpool"], writes=["mv3"])
                P.op("scalar", (lambda e: e.activation(sb["mv"][:, 4:5], sb["mv"][:, 3:4],
                                                       AF.Exp, scale=-0.5, bias=fc0)),
                     reads=["mv3", "cpool"], writes=["mv4"])
                P.op("vector", (lambda e: e.tensor_scalar(
                    sb["otm"][:, :], sb["otm"][:, :], sb["mv"][:, 2:3], sb["mv"][:, 4:5],
                    ALU.add, ALU.mult)),
                     reads=["mv2", "mv4"], writes=["otm"])
                P.op("vector", (lambda e: e.tensor_mul(sb["otm"][:, :], sb["otm"][:, :],
                                                       sb["gamma_bc"][:, :])),
                     reads=["gamma_bc"], writes=["otm"])
                P.op("vector", (lambda e: e.tensor_tensor(sb["lnbf"][:, :], sb["otm"][:, :],
                                                          sb["beta_bc"][:, :], ALU.add)),
                     reads=["otm", "beta_bc"], writes=["lnbf"])
                for kc in range(NCH):
                    P.dma("sc", AP(sb["lnfm"][0:1, 0:1].tensor, kc * 512 + ti * 128,
                                   [[NCH * 512, 128], [1, 128]]),
                          sb["lnbf"][:, kc * 128:(kc + 1) * 128],
                          reads=["lnbf"], writes=["lnfm"], transpose=True)
            for ti in range(4):
                tt = 4 * g + ti
                gi = 16 * b + tt
                xbt = sb["otm"]
                P.dma("g", xbt[:, :], xB[gi, :, :], writes=["otm"])
                for dh in range(2):
                    po = "poA" if dh == 0 else "poB"
                    for kc in range(NCH):
                        P.op("tensor", (lambda e, ti=ti, dh=dh, kc=kc, po=po: e.matmul(
                            sb[po][:, :],
                            AP(sb["lnfm"][0:1, 0:1].tensor, kc * 512 + ti * 128,
                               [[NCH * 512, 128], [1, 128]]),
                            sb["WT_Wo"][:, kc * D + dh * 512: kc * D + dh * 512 + 512],
                            start=(kc == 0), stop=(kc == NCH - 1))),
                             reads=["lnfm", "WT_Wo"], writes=[po], no_inc=(kc < NCH - 1))
                    P.op("vector", (lambda e, dh=dh, po=po, xbt=xbt: e.tensor_add(
                        xbt[:, dh * 512:(dh + 1) * 512], sb[po][:, :],
                        xbt[:, dh * 512:(dh + 1) * 512])),
                         reads=[po], writes=["otm"])
                P.dma("sy", x1[gi, :, :], sb["otm"][:, :], reads=["otm"], writes=["x1dma"])


def emit_stage3(P, dr, sb):
    fc0 = fconst(P, sb, 0.0)
    emit_load_consts(P, dr, sb, ["c_idf32", "c_headmask", "ln1_bc", "scale128"])
    for nm in ("Wq1", "Wk1", "Wv1"):
        P.dma("g", sb[nm][:, :].rearrange("p (kc d) -> p kc d", kc=NCH),
              dr[nm].ap().rearrange("(kc p) d -> p kc d", p=128), writes=[nm])
    P.dma("g", sb["Wg"][:, :].rearrange("p (kc d) -> p kc d", kc=NCH),
          dr["Wg"].ap().rearrange("(kc p) d -> p kc d", p=128), writes=["Wg"])
    for t in ("v_sb0", "v_sb1"):
        P.op("gpsimd", (lambda e, t=t: e.memset(
            AP(sb[t][0:1, 0:1].tensor, 64, [[VD, 128], [65, 16]]), 1.0)), writes=[t])

    emit_stats_pass(P, dr, sb, "x1", "s_nm", "s_rs", LN_EPS0,
                    inner_name="s_ri", inner_eps=RMS_EPS)

    x = dr["x1"].ap().rearrange("(n p) d -> n p d", p=128)
    xn32d = dr["xn32"].ap().rearrange("(n p) d -> n p d", p=128)
    v1 = dr["v1_tm"].ap().rearrange("(n p) d -> n p d", p=128)
    q1 = dr["q1_fm"].ap()
    k1 = dr["k1_fm"].ap()
    xnfmd = dr["xn_fm_bf"].ap()
    probsd = dr["probs"].ap().rearrange("(n p) d -> n p d", p=128)
    pTd = dr["pT"].ap()

    xnfm32 = sb["xn_fm32"]
    xnfmbf = sb["xn_fm_bf"]
    xnifm = sb["xni_fm"]

    for g in range(16):
        for ti in range(4):
            gi = 4 * g + ti
            xs = sb["xs%d" % (ti % 2)]
            xst = "xs%d" % (ti % 2)
            P.dma("g", xs[:, :], x[gi, :, :], writes=[xst])
            P.op("vector", (lambda e, xs=xs, gi=gi: e.tensor_scalar(
                sb["xn_tm"][:, :], xs[:, :],
                sb["s_nm"][:, gi:gi + 1], sb["s_rs"][:, gi:gi + 1],
                ALU.add, ALU.mult)),
                 reads=[xst, "s_nm", "s_rs"], writes=["xn_tm"])
            P.dma("sy", xn32d[gi, :, :], sb["xn_tm"][:, :], reads=["xn_tm"], writes=["xndma"])
            P.op("vector", (lambda e, gi=gi: e.tensor_scalar(
                sb["xni_t"][:, :], sb["xn_tm"][:, :], sb["s_ri"][:, gi:gi + 1], None,
                ALU.mult)),
                 reads=["xn_tm", "s_ri"], writes=["xni_t"])
            P.op("vector", (lambda e: e.tensor_tensor(
                sb["xni_bf"][:, :], sb["xni_t"][:, :], sb["ln1_bc"][:, :], ALU.mult)),
                 reads=["xni_t", "ln1_bc"], writes=["xni_bf"])
            for kc in range(NCH):
                P.dma("sc", AP(xnifm[0:1, 0:1].tensor, kc * 512 + ti * 128,
                               [[NCH * 512, 128], [1, 128]]),
                      sb["xni_bf"][:, kc * 128:(kc + 1) * 128],
                      reads=["xni_bf"], writes=["xni_fm"], transpose=True)
            for kc in range(NCH):
                pslc = sb["ptf"][:, kc * 128:(kc + 1) * 128]
                P.op("tensor", (lambda e, kc=kc, pslc=pslc: e.transpose(
                    pslc, sb["xn_tm"][:, kc * 128:(kc + 1) * 128], sb["c_idf32"][:, :])),
                     reads=["xn_tm", "c_idf32"],
                     writes=["ptf%d" % kc, "ptfB%d" % (kc // 4)])
                P.op("vector", (lambda e, kc=kc, ti=ti, pslc=pslc: e.tensor_copy(
                    AP(xnfm32[0:1, 0:1].tensor, kc * 512 + ti * 128,
                       [[NCH * 512, 128], [1, 128]]), pslc)),
                     reads=["ptf%d" % kc], writes=["xn_fm32", "ptfB%d" % (kc // 4)])
                P.op("scalar", (lambda e, kc=kc, ti=ti, pslc=pslc: e.copy(
                    AP(xnfmbf[0:1, 0:1].tensor, kc * 512 + ti * 128,
                       [[NCH * 512, 128], [1, 128]]), pslc)),
                     reads=["ptf%d" % kc], writes=["xn_fm_bf", "ptfB%d" % (kc // 4)])
            for kc in range(NCH):
                P.op("tensor", (lambda e, kc=kc, ti=ti: e.matmul(
                    sb["pssq"][:, 0:8],
                    AP(xnfm32[0:1, 0:1].tensor, kc * 512 + ti * 128,
                       [[NCH * 512, 128], [1, 128]]),
                    sb["Wg"][:, kc * 8:(kc + 1) * 8],
                    start=(kc == 0), stop=(kc == NCH - 1))),
                     reads=["xn_fm32", "Wg"], writes=["pssq"], no_inc=(kc < NCH - 1))
            P.op("vector", (lambda e: e.tensor_copy(sb["lg"][:, 0:8], sb["pssq"][:, 0:8])),
                 reads=["pssq"], writes=["lg"])
            P.op("vector", (lambda e: e.reduce_max(sb["lg"][:, 8:9], sb["lg"][:, 0:8],
                                                   axis=AX.X)),
                 reads=["lg"], writes=["lgmx"])
            P.op("vector", (lambda e: e.tensor_scalar(sb["lg"][:, 9:10], sb["lg"][:, 8:9],
                                                      -1.0, None, ALU.mult)),
                 reads=["lgmx"], writes=["lgnm"])
            P.op("scalar", (lambda e: e.activation(sb["lg"][:, 10:18], sb["lg"][:, 0:8],
                                                   AF.Exp, bias=sb["lg"][:, 9:10])),
                 reads=["lg", "lgnm"], writes=["lge"])
            P.op("vector", (lambda e: e.reduce_sum(sb["lg"][:, 18:19], sb["lg"][:, 10:18],
                                                   axis=AX.X)),
                 reads=["lge"], writes=["lgs"])
            P.op("vector", (lambda e: e.reciprocal(sb["lg"][:, 19:20], sb["lg"][:, 18:19])),
                 reads=["lgs"], writes=["lgr"])
            P.op("vector", (lambda e: e.tensor_scalar(
                sb["probs_t"][:, 0:8], sb["lg"][:, 10:18], sb["lg"][:, 19:20], None,
                ALU.mult)),
                 reads=["lge", "lgr"], writes=["probs_t"])
            P.dma("sy", probsd[gi, :, :], sb["probs_t"][:, 0:8],
                  reads=["probs_t"], writes=["probsdma"])
            P.op("tensor", (lambda e: e.transpose(
                sb["pssq"][0:8, 8:136], sb["probs_t"][:, 0:8], sb["c_idf32"][:, :])),
                 reads=["probs_t", "c_idf32"], writes=["pssq"])
            P.op("vector", (lambda e: e.tensor_copy(sb["pT_t"][0:8, 0:128],
                                                    sb["pssq"][0:8, 8:136])),
                 reads=["pssq"], writes=["pT_t"])
            P.dma("sy", AP(pTd.tensor, gi * 128, [[T_LOC, 8], [1, 128]]),
                  sb["pT_t"][0:8, 0:128], reads=["pT_t"], writes=["pTdma"])

        P.dma("sy", AP(xnfmd.tensor, g * 512, [[T_LOC, 128], [128 * T_LOC, NCH], [1, 512]]),
              AP(xnfmbf[0:1, 0:1].tensor, 0, [[NCH * 512, 128], [512, NCH], [1, 512]]),
              reads=["xn_fm_bf"], writes=["xnfmdma"])

        for (wname, dst_fm, ss_t) in (("Wq1", q1, "pssq"), ("Wk1", k1, "pssk")):
            for dc in range(NCH):
                pg = sb["pg%d" % (dc % 2)]
                pgt = "pg%d" % (dc % 2)
                for kc in range(NCH):
                    P.op("tensor", (lambda e, wname=wname, dc=dc, kc=kc, pg=pg: e.matmul(
                        pg[:, :],
                        sb[wname][:, kc * D + dc * 128: kc * D + dc * 128 + 128],
                        AP(xnifm[0:1, 0:1].tensor, kc * 512, [[NCH * 512, 128], [1, 512]]),
                        start=(kc == 0), stop=(kc == NCH - 1))),
                         reads=[wname, "xni_fm"], writes=[pgt], no_inc=(kc < NCH - 1))
                qsb = sb["q_sb%d" % (dc % 2)]
                P.op("scalar", (lambda e, pg=pg, qsb=qsb: e.copy(qsb[:, :], pg[:, :])),
                     reads=[pgt], writes=["q_sb%d" % (dc % 2)])
                P.dma("sy", AP(dst_fm.tensor, (dc * 128) * T_LOC + g * 512,
                               [[T_LOC, 128], [1, 512]]),
                      qsb[:, :], reads=["q_sb%d" % (dc % 2)], writes=["qdma%d" % (dc % 2)])
                P.op("scalar", (lambda e, pg=pg, b=fc0: e.activation(
                    sb["q2"][:, :], pg[:, :], AF.Square, bias=b)),
                     reads=[pgt, "cpool"], writes=["q2"])
                P.op("tensor", (lambda e, dc=dc, ss_t=ss_t: e.matmul(
                    sb[ss_t][0:16, :], sb["c_headmask"][:, dc * 16:dc * 16 + 16],
                    sb["q2"][:, :], start=(dc == 0), stop=(dc == NCH - 1))),
                     reads=["c_headmask", "q2"], writes=[ss_t], no_inc=(dc < NCH - 1))
            P.op("vector", (lambda e, ss_t=ss_t: e.tensor_copy(
                sb["ss_sb"][0:16, :], sb[ss_t][0:16, :])),
                 reads=[ss_t], writes=["ss_sb"])
            ssd = dr["ssq1"] if wname == "Wq1" else dr["ssk1"]
            P.dma("sy", AP(ssd.ap().tensor, g * 512, [[T_LOC, 16], [1, 512]]),
                  sb["ss_sb"][0:16, :], reads=["ss_sb"], writes=["ss_dma"])
        for ti in range(4):
            gi = 4 * g + ti
            for dh in range(2):
                pg = sb["pg%d" % (2 + dh % 2)]
                pgt = "pg%d" % (2 + dh % 2)
                for kc in range(NCH):
                    P.op("tensor", (lambda e, ti=ti, dh=dh, kc=kc, pg=pg: e.matmul(
                        pg[:, :],
                        AP(xnifm[0:1, 0:1].tensor, kc * 512 + ti * 128,
                           [[NCH * 512, 128], [1, 128]]),
                        sb["Wv1"][:, kc * D + dh * 512: kc * D + dh * 512 + 512],
                        start=(kc == 0), stop=(kc == NCH - 1))),
                         reads=["xni_fm", "Wv1"], writes=[pgt], no_inc=(kc < NCH - 1))
                vsb = sb["v_sb%d" % (ti % 2)]
                P.op("vector", (lambda e, dh=dh, pg=pg, vsb=vsb: e.tensor_copy(
                    AP(vsb[0:1, 0:1].tensor, dh * 8 * 65, [[VD, 128], [65, 8], [1, 64]]),
                    AP(pg[0:1, 0:1].tensor, 0, [[512, 128], [64, 8], [1, 64]]))),
                     reads=[pgt], writes=["v_sb%d" % (ti % 2)])
            P.dma("sy", v1[gi, :, :], sb["v_sb%d" % (ti % 2)][:, :],
                  reads=["v_sb%d" % (ti % 2)], writes=["v_dma%d" % (ti % 2)])

    for (ssn, out_n, fold_scale) in (("ssq1", "sqv", True), ("ssk1", "skv", False)):
        P.dma("g", sb["ssl"][:, :], dr[ssn].ap().rearrange("h (c t) -> (h c) t", c=8),
              reads=["ss_dma"], writes=["ssl"])
        P.op("scalar", (lambda e, b=fconst(P, sb, KQ_EPS): e.activation(
            sb["ssl2"][:, :], sb["ssl"][:, :], AF.Ln, bias=b)),
             reads=["ssl", "cpool"], writes=["ssl2"])
        P.op("scalar", (lambda e, b=fc0: e.activation(
            sb["ssl"][:, :], sb["ssl2"][:, :], AF.Exp, scale=-0.5, bias=b)),
             reads=["ssl2", "cpool"], writes=["ssl"])
        if fold_scale:
            P.op("vector", (lambda e: e.tensor_scalar(
                sb["ssl"][:, :], sb["ssl"][:, :], sb["scale128"][:, 0:1], None,
                ALU.mult)),
                 reads=["scale128"], writes=["ssl"])
        P.dma("sy", dr[out_n].ap().rearrange("h (c t) -> (h c) t", c=8),
              sb["ssl"][:, :], reads=["ssl"], writes=["ssodma"])


N_BISECT = 42


def emit_stage4g(P, dr, sb, eng="vector", collective=False):
    fcBIG = 1e30
    emit_load_consts(P, dr, sb, ["c_iota8", "c_iotaB", "c_onesrow", "c_p2e"])
    if collective:
        rg = [list(range(N_CORES))]
        P.op("gpsimd", (lambda e: e.collective_compute(
            "AllGather", ALU.bypass, replica_groups=rg,
            ins=[dr["pT"].ap()], outs=[dr["probs_gathered"].ap()])),
             reads=["pTdma"], writes=["pgath"], inc_amt=16)
    P.dma("g", sb["G"][0:64, :], dr["probs_gathered"].ap()[:, 0:4096],
          reads=["pgath"], writes=["G"])
    P.dma("g", sb["G"][64:128, :], dr["probs_gathered"].ap()[:, 4096:8192],
          reads=["pgath"], writes=["G"])
    P.op(eng, (lambda e: e.memset(sb["bis"][0:1, 0:8], 0.0)), writes=["loE"])
    P.op(eng, (lambda e: e.memset(sb["bis"][0:1, 8:16], 1.0)), writes=["hiE"])
    for it in range(N_BISECT):
        P.op(eng, (lambda e: e.tensor_add(sb["bis"][0:1, 16:24], sb["bis"][0:1, 0:8],
                                          sb["bis"][0:1, 8:16])),
             reads=["loE", "hiE"], writes=["midE"])
        P.op(eng, (lambda e: e.tensor_scalar(sb["bis"][0:1, 16:24], sb["bis"][0:1, 16:24],
                                             0.5, None, ALU.mult)),
             reads=[], writes=["midE"])
        P.op(eng, (lambda e: e.tensor_copy(
            AP(sb["bis"][0:1, 0:1].tensor, 32, [[192, 1], [1, 8], [64, 2], [8, 8]]),
            AP(sb["bis"][0:1, 0:1].tensor, 16, [[192, 1], [1, 8], [0, 2], [0, 8]]))),
             reads=["midE"], writes=["midrow"])
        P.dma("sy", dr["mid128"].ap(), sb["bis"][0:1, 32:160],
              reads=["midrow"], writes=["mid128d"])
        P.dma("g", sb["mid_pp"][:, 0:1],
              dr["mid128"].ap().rearrange("(p one) -> p one", one=1),
              reads=["mid128d"], writes=["mid_pp"])
        P.op(eng, (lambda e: e.tensor_scalar(sb["cmpb"][:, :], sb["G"][:, :],
                                             sb["mid_pp"][:, 0:1], None, ALU.is_gt)),
             reads=["G", "mid_pp"], writes=["cmpb"])
        P.op(eng, (lambda e: e.reduce_sum(sb["cnt_pp"][:, 0:1], sb["cmpb"][:, :],
                                          axis=AX.X)),
             reads=["cmpb"], writes=["cnt_pp"])
        P.dma("sy", dr["cnt128"].ap(), sb["cnt_pp"][:, 0:1],
              reads=["cnt_pp"], writes=["cnt128d"])
        P.dma("g", sb["bis"][0:1, 32:160],
              dr["cnt128"].ap().rearrange("(one p) -> one p", one=1),
              reads=["cnt128d"], writes=["cntrow"])
        P.op(eng, (lambda e: e.tensor_reduce(
            AP(sb["bis"][0:1, 0:1].tensor, 24, [[192, 1], [1, 8], [1, 1], [1, 1]]),
            AP(sb["bis"][0:1, 0:1].tensor, 32, [[192, 1], [1, 8], [64, 2], [8, 8]]),
            AX.XY, ALU.add)),
             reads=["cntrow"], writes=["cntE"])
        P.op(eng, (lambda e: e.tensor_scalar(sb["bis"][0:1, 160:168], sb["bis"][0:1, 24:32],
                                             float(CAP), None, ALU.is_ge)),
             reads=["cntE"], writes=["selE"])
        P.op(eng, (lambda e: e.tensor_sub(sb["bis"][0:1, 168:176], sb["bis"][0:1, 16:24],
                                          sb["bis"][0:1, 0:8])),
             reads=["midE", "loE"], writes=["dml"])
        P.op(eng, (lambda e: e.tensor_mul(sb["bis"][0:1, 168:176], sb["bis"][0:1, 168:176],
                                          sb["bis"][0:1, 160:168])),
             reads=["selE", "dml"], writes=["dml"])
        P.op(eng, (lambda e: e.tensor_add(sb["bis"][0:1, 0:8], sb["bis"][0:1, 0:8],
                                          sb["bis"][0:1, 168:176])),
             reads=["dml", "loE"], writes=["loE"])
        P.op(eng, (lambda e: e.tensor_sub(sb["bis"][0:1, 168:176], sb["bis"][0:1, 8:16],
                                          sb["bis"][0:1, 16:24])),
             reads=["hiE", "midE"], writes=["dhm"])
        P.op(eng, (lambda e: e.tensor_mul(sb["bis"][0:1, 168:176], sb["bis"][0:1, 168:176],
                                          sb["bis"][0:1, 160:168])),
             reads=["selE", "dhm"], writes=["dhm"])
        P.op(eng, (lambda e: e.tensor_add(sb["bis"][0:1, 8:16], sb["bis"][0:1, 16:24],
                                          sb["bis"][0:1, 168:176])),
             reads=["dhm", "midE"], writes=["hiE"])
    P.dma("sy", dr["thr8"].ap(), sb["bis"][0:1, 0:8], reads=["loE"], writes=["thr8d"])
    P.dma("g", sb["thr_bc"][:, :],
          AP(dr["thr8"].ap().tensor, 0, [[0, 128], [1, 8]]),
          reads=["thr8d"], writes=["thr_bc"])

    probs = dr["probs"].ap().rearrange("(n p) d -> n p d", p=128)
    gates = dr["gates"].ap().rearrange("(n p) d -> n p d", p=128)
    g8 = sb["g8"]
    for tile in range(NTILE):
        P.dma("g", g8[:, 0:8], probs[tile, :, :], writes=["g8"])
        P.op("vector", (lambda e: e.tensor_tensor(g8[:, 8:16], g8[:, 0:8],
                                                  sb["thr_bc"][:, :], ALU.is_gt)),
             reads=["g8", "thr_bc"], writes=["selT"])
        P.op("vector", (lambda e: e.reduce_max(g8[:, 16:17], g8[:, 8:16], axis=AX.X)),
             reads=["selT"], writes=["asg"])
        P.op("vector", (lambda e: e.tensor_mul(g8[:, 24:32], g8[:, 8:16], g8[:, 0:8])),
             reads=["selT", "g8"], writes=["m1"])
        P.op("vector", (lambda e: e.tensor_scalar(g8[:, 32:40], g8[:, 8:16], -1.0, fcBIG,
                                                  ALU.add, ALU.mult)),
             reads=["selT"], writes=["m2"])
        P.op("vector", (lambda e: e.tensor_add(g8[:, 24:32], g8[:, 24:32], g8[:, 32:40])),
             reads=["m1", "m2"], writes=["m1"])
        P.op("vector", (lambda e: e.reduce_max(g8[:, 17:18], g8[:, 24:32], axis=AX.X)),
             reads=["m1"], writes=["mx1"])
        P.op("vector", (lambda e: e.tensor_scalar(g8[:, 24:32], g8[:, 24:32],
                                                  g8[:, 17:18], None, ALU.is_ge)),
             reads=["m1", "mx1"], writes=["eq1"])
        P.op("vector", (lambda e: e.tensor_mul(g8[:, 24:32], g8[:, 24:32],
                                               sb["c_iotaB"][:, :])),
             reads=["eq1", "c_iotaB"], writes=["eq1"])
        P.op("vector", (lambda e: e.tensor_scalar(g8[:, 24:32], g8[:, 24:32], 1000.0, None,
                                                  ALU.add)),
             reads=["eq1"], writes=["eq1"])
        P.op("vector", (lambda e: e.tensor_reduce(g8[:, 18:19], g8[:, 24:32], AX.X,
                                                  ALU.min)),
             reads=["eq1"], writes=["idx1"])
        P.op("vector", (lambda e: e.reduce_max(g8[:, 19:20], g8[:, 0:8], axis=AX.X)),
             reads=["g8"], writes=["mx2b"])
        P.op("vector", (lambda e: e.tensor_scalar(g8[:, 32:40], g8[:, 0:8],
                                                  g8[:, 19:20], None, ALU.is_ge)),
             reads=["g8", "mx2b"], writes=["eq2"])
        P.op("vector", (lambda e: e.tensor_mul(g8[:, 32:40], g8[:, 32:40],
                                               sb["c_iotaB"][:, :])),
             reads=["eq2", "c_iotaB"], writes=["eq2"])
        P.op("vector", (lambda e: e.tensor_scalar(g8[:, 32:40], g8[:, 32:40], 1000.0, None,
                                                  ALU.add)),
             reads=["eq2"], writes=["eq2"])
        P.op("vector", (lambda e: e.tensor_reduce(g8[:, 20:21], g8[:, 32:40], AX.X,
                                                  ALU.min)),
             reads=["eq2"], writes=["idx2"])
        P.op("vector", (lambda e: e.tensor_sub(g8[:, 21:22], g8[:, 18:19], g8[:, 20:21])),
             reads=["idx1", "idx2"], writes=["dd"])
        P.op("vector", (lambda e: e.scalar_tensor_tensor(
            g8[:, 22:23], g8[:, 21:22], g8[:, 16:17], g8[:, 20:21],
            op0=ALU.mult, op1=ALU.add)),
             reads=["dd", "asg", "idx2"], writes=["idxf"])
        P.op("vector", (lambda e: e.tensor_scalar(g8[:, 32:40], sb["c_iota8"][:, :],
                                                  g8[:, 22:23], None, ALU.is_equal)),
             reads=["c_iota8", "idxf"], writes=["oh"])
        P.op("vector", (lambda e: e.tensor_mul(g8[:, 32:40], g8[:, 32:40], g8[:, 0:8])),
             reads=["oh", "g8"], writes=["oh"])
        P.dma("sy", gates[tile, :, :], g8[:, 32:40], reads=["oh"], writes=["gdma"])


ND_B = 4096
Z0R = 2432
WU = 1024


def emit_stage4a(P, dr, sb):
    fc0 = fconst(P, sb, 0.0)
    emit_load_consts(P, dr, sb, ["c_idf32", "c_onesrow", "c_headexp", "scale16"])
    P.op("gpsimd", (lambda e: e.memset(sb["onesq"][:, :], 1.0)), writes=["onesq"])
    P.op("gpsimd", (lambda e: e.memset(sb["onescol_bf"][:, :], 1.0)), writes=["onescol_bf"])
    P.dma("g", sb["Wo1"][:, :].rearrange("p (kc d) -> p kc d", kc=NCH),
          dr["Wo1"].ap().rearrange("(kc p) d -> p kc d", p=128), writes=["Wo1"])
    P.dma("g", sb["relb"][0:32, 0:16], dr["rel_bias"].ap(), writes=["relb"])

    nchk = ND_B // 512
    for ch in range(nchk):
        P.dma("g", sb["ohts"][0:32, :], dr["c_oht"].ap()[:, ch * 512:(ch + 1) * 512],
              writes=["ohts"])
        P.dma("g", sb["ngr"][0:16, :], dr["c_negrow"].ap()[:, ch * 512:(ch + 1) * 512],
              writes=["ngr"])
        P.op("tensor", (lambda e: e.matmul(
            sb["pmv"][0:16, 512:1024], sb["relb"][0:32, 0:16],
            sb["ohts"][0:32, :], start=True, stop=True)),
             reads=["relb", "ohts"], writes=["pmv"])
        P.op("vector", (lambda e: e.tensor_scalar(
            sb["uch"][0:16, :], sb["pmv"][0:16, 512:1024],
            sb["scale16"][0:16, 0:1], None, ALU.mult)),
             reads=["pmv", "scale16"], writes=["uch"])
        P.op("vector", (lambda e: e.tensor_tensor(
            sb["uch"][0:16, :], sb["uch"][0:16, :], sb["ngr"][0:16, :], ALU.add)),
             reads=["ngr"], writes=["uch"])
        P.op("vector", (lambda e, ch=ch: e.reduce_max(
            sb["hh8"][0:16, 8 + ch:9 + ch], sb["uch"][0:16, :], axis=AX.X)),
             reads=["uch"], writes=["hhmx"])
        if ch == (1024 + 2047) // 512:
            P.op("vector", (lambda e, cc=(1024 + 2047) % 512: e.tensor_copy(
                sb["hh8"][0:16, 16:17], sb["uch"][0:16, cc:cc + 1])),
                 reads=["uch"], writes=["far0"])
        P.op("vector", (lambda e: e.tensor_copy(
            sb["uchr"][0:16, :],
            AP(sb["uch"][0:1, 0:1].tensor, 511, [[512, 16], [-1, 512]]))),
             reads=["uch"], writes=["uchr"])
        P.dma("sy", dr["u_rev"].ap()[:, ND_B - 512 * (ch + 1): ND_B - 512 * ch],
              sb["uchr"][0:16, :], reads=["uchr"], writes=["urevd"])
    P.op("vector", (lambda e, nchk=nchk: e.reduce_max(
        sb["hh8"][0:16, 0:1], sb["hh8"][0:16, 8:8 + nchk], axis=AX.X)),
         reads=["hhmx"], writes=["hh0"])
    P.op("vector", (lambda e: e.tensor_scalar_max(sb["hh8"][0:16, 1:2], sb["hh8"][0:16, 0:1],
                                                  0.0)),
         reads=["hh0"], writes=["hh1"])
    P.op("vector", (lambda e: e.tensor_scalar(sb["hh8"][0:16, 2:3], sb["scale16"][0:16, 0:1],
                                              -1.0, None, ALU.mult)),
         reads=["scale16"], writes=["hh2"])
    P.op("vector", (lambda e: e.tensor_tensor(sb["hh8"][0:16, 3:4], sb["hh8"][0:16, 2:3],
                                              sb["scale16"][0:16, 0:1], ALU.max)),
         reads=["hh2", "scale16"], writes=["hh3"])
    P.op("vector", (lambda e: e.tensor_scalar(sb["hh8"][0:16, 4:5], sb["hh8"][0:16, 3:4],
                                              1.02, None, ALU.mult)),
         reads=["hh3"], writes=["hh4"])
    P.op("vector", (lambda e: e.tensor_add(sb["hh8"][0:16, 5:6], sb["hh8"][0:16, 4:5],
                                           sb["hh8"][0:16, 1:2])),
         reads=["hh4", "hh1"], writes=["hh5"])
    P.op("vector", (lambda e: e.tensor_scalar(sb["hh8"][0:16, 6:7], sb["hh8"][0:16, 5:6],
                                              -1.0, None, ALU.mult)),
         reads=["hh5"], writes=["negC"])
    P.op("vector", (lambda e: e.tensor_add(sb["hh8"][0:16, 7:8], sb["hh8"][0:16, 6:7],
                                           sb["hh8"][0:16, 16:17])),
         reads=["negC", "far0"], writes=["farb"])
    for (cc, dst) in ((6, "CbN"), (7, "CbF")):
        P.op("tensor", (lambda e, cc=cc: e.transpose(
            sb["pmv"][0:1, 512:528], sb["hh8"][0:16, cc:cc + 1], sb["c_idf32"][0:16, 0:16])),
             reads=["negC", "farb", "c_idf32"], writes=["pmv"])
        P.op("vector", (lambda e: e.tensor_copy(sb["row16"][0:1, 0:16],
                                                sb["pmv"][0:1, 512:528])),
             reads=["pmv"], writes=["row16"])
        P.op("tensor", (lambda e: e.matmul(sb["pmv"][:, 512:528], sb["c_onesrow"][0:1, :],
                                           sb["row16"][0:1, 0:16], start=True, stop=True)),
             reads=["c_onesrow", "row16"], writes=["pmv"])
        P.op("vector", (lambda e, dst=dst: e.tensor_copy(sb[dst][:, :],
                                                         sb["pmv"][:, 512:528])),
             reads=["pmv"], writes=[dst])
    for h in range(H):
        P.dma("g", dr["u2"].ap().rearrange("(h p) w -> h p w", h=H)[h, :, :],
              AP(dr["u_rev"].ap().tensor, h * ND_B + Z0R, [[1, 128], [1, WU]]),
              reads=["urevd"], writes=["u2d"])

    q1 = dr["q1_fm"].ap()
    k1 = dr["k1_fm"].ap()
    v1 = dr["v1_tm"].ap()
    xn32 = dr["xn32"].ap().rearrange("(n p) d -> n p d", p=128)
    y32 = dr["y32"].ap().rearrange("(n p) d -> n p d", p=128)
    Vs, Obf = sb["Vs"], sb["Obf"]

    for b in range(B_LOC):
        for vch in range(4):
            P.dma("g", Vs[:, vch * 4 * VD:(vch + 1) * 4 * VD],
                  AP(v1.tensor, (b * S + vch * 4 * 128) * VD,
                     [[VD, 128], [128 * VD, 4], [1, VD]]),
                  writes=["Vs%d" % vch])
        for (t, src) in (("sqs", "sqv"), ("sks", "skv")):
            P.dma("g", sb[t][0:16, :],
                  AP(dr[src].ap().tensor, b * S, [[T_LOC, 16], [1, S]]), writes=[t])
        pend = [None]
        for hp in range(NCH):
            hA, hB = 2 * hp, 2 * hp + 1
            for (tb, src) in (("Qhr", q1), ("Khr", k1)):
                t = "%s%d" % (tb, hp % 2)
                P.dma("g", sb[t][:, :],
                      AP(src.tensor, (hp * 128) * T_LOC + b * S, [[T_LOC, 128], [1, S]]),
                      writes=[t])
            for (srcn, sv, dst) in (("Qhr%d" % (hp % 2), "sqs", "Qh"),
                                    ("Khr%d" % (hp % 2), "sks", "Kh")):
                srct = sb[srcn]
                for qc in range(4):
                    P.op("tensor", (lambda e, hp=hp, qc=qc, sv=sv: e.matmul(
                        sb["pr"][:, 0:512], sb["c_headexp"][0:16, hp * 128:(hp + 1) * 128],
                        sb[sv][0:16, qc * 512:(qc + 1) * 512], start=True, stop=True)),
                         reads=["c_headexp", sv], writes=["prA"])
                    P.op("vector", (lambda e, srct=srct, qc=qc, dst=dst: e.tensor_tensor(
                        sb[dst][:, qc * 512:(qc + 1) * 512],
                        AP(srct[0:1, 0:1].tensor, qc * 512, [[S, 128], [1, 512]]),
                        sb["pr"][:, 0:512], ALU.mult)),
                         reads=[srcn, "prA"], writes=[dst])
            P.dma("g", sb["u2A"][:, :],
                  dr["u2"].ap().rearrange("(h p) w -> h p w", h=H)[hA, :, :],
                  reads=["u2d"], writes=["u2A"])
            P.dma("g", sb["u2B"][:, :],
                  dr["u2"].ap().rearrange("(h p) w -> h p w", h=H)[hB, :, :],
                  reads=["u2d"], writes=["u2B"])
            for (hh, col) in ((hA, 0), (hB, 1)):
                for kt in range(16):
                    P.op("tensor", (lambda e, hh=hh, kt=kt: e.matmul(
                        sb["pmv"][0:65, 512:513],
                        AP(Vs[0:1, 0:1].tensor, kt * VD + hh * 65, [[16 * VD, 128], [1, 65]]),
                        sb["onescol_bf"][:, :], start=(kt == 0), stop=(kt == 15))),
                         reads=["Vs%d" % (kt // 4), "onescol_bf"], writes=["pmv"],
                         no_inc=(kt < 15))
                P.op("vector", (lambda e, col=col: e.tensor_scalar(
                    sb["mvsb"][0:64, col:col + 1], sb["pmv"][0:64, 512:513],
                    1.0 / S, None, ALU.mult)),
                     reads=["pmv"], writes=["mvsb"])
            for qc in range(4):
                nkt = 4 * qc + 4

                def s4_qk(kt, qc=qc):
                    qoff = max(0, (kt - 4 * qc) * 128)
                    w = 512 - qoff
                    par = kt % 2
                    for (hh, pL, base, tp) in ((hA, "pL%dA" % par, 0, (0, 0)),
                                               (hB, "pL%dB" % par, 64, (64, 0))):
                        P.op("tensor", (lambda e, kt=kt, qc=qc, qoff=qoff, w=w,
                                        pL=pL, base=base, tp=tp: e.matmul(
                            sb[pL][:, qoff:512],
                            AP(sb["Kh"][0:1, 0:1].tensor, base * S + kt * 128,
                               [[S, 64], [1, 128]]),
                            AP(sb["Qh"][0:1, 0:1].tensor, base * S + qc * 512 + qoff,
                               [[S, 64], [1, w]]),
                            start=True, stop=True, tile_position=tp)),
                             reads=["Qh", "Kh"], writes=[pL])

                def s4_post(kt, qc=qc, nkt=nkt, hA=hA, hB=hB):
                    qoff = max(0, (kt - 4 * qc) * 128)
                    w = 512 - qoff
                    par = kt % 2
                    in_band = (kt >= 4 * qc - 1)
                    if in_band:
                        mW = 639 - 512 * qc + 128 * kt
                        for (pL, u2t) in (("pL%dA" % par, "u2A"), ("pL%dB" % par, "u2B")):
                            P.op("vector", (lambda e, pL=pL, u2t=u2t, qoff=qoff, w=w, mW=mW:
                                            e.tensor_tensor(
                                sb[pL][:, qoff:512], sb[pL][:, qoff:512],
                                AP(sb[u2t][0:1, 0:1].tensor, mW - qoff, [[WU, 128], [-1, w]]),
                                ALU.add)),
                                 reads=[u2t], writes=[pL])
                    bias_t = "CbN" if in_band else "CbF"
                    for (hh, pL, pt) in ((hA, "pL%dA" % par, "ptA%d" % par),
                                         (hB, "pL%dB" % par, "ptB%d" % par)):
                        P.op("scalar", (lambda e, pL=pL, pt=pt, qoff=qoff, hh=hh,
                                        bias_t=bias_t: e.activation(
                            sb[pt][:, qoff:512], sb[pL][:, qoff:512], AF.Exp,
                            bias=sb[bias_t][:, hh:hh + 1])),
                             reads=[pL, bias_t], writes=[pt])
                    first, last = (kt == 0), (kt == nkt - 1)
                    for (hh, pt, po) in ((hA, "ptA%d" % par, "poA"),
                                         (hB, "ptB%d" % par, "poB")):
                        P.op("tensor", (lambda e, hh=hh, pt=pt, po=po, kt=kt,
                                        qoff=qoff, first=first, last=last: e.matmul(
                            sb[po][0:65, qoff:512],
                            AP(Vs[0:1, 0:1].tensor, kt * VD + hh * 65,
                               [[16 * VD, 128], [1, 65]]),
                            sb[pt][:, qoff:512],
                            start=first, stop=last)),
                             reads=["Vs%d" % (kt // 4), pt], writes=[po],
                             no_inc=not last)

                for kt in range(nkt):
                    s4_qk(kt)
                    if kt > 0:
                        s4_post(kt - 1)
                    if kt == 1 and pend[0] is not None:
                        pend[0]()
                        pend[0] = None
                s4_post(nkt - 1)
                p2 = qc % 2
                sgA, sgAt = sb["sgA%d" % p2], "sgA%d" % p2
                sgB, sgBt = sb["sgB%d" % p2], "sgB%d" % p2
                dn, dnt = sb["dn%d" % p2], "dn%d" % p2
                st2, st2t = sb["stg2"], "stg2"
                P.op("vector", (lambda e, sgA=sgA: e.tensor_copy(
                    sgA[0:65, 0:512], sb["poA"][0:65, 0:512])),
                     reads=["poA"], writes=[sgAt])
                P.op("scalar", (lambda e, sgB=sgB: e.copy(
                    sgB[0:65, 0:512], sb["poB"][0:65, 0:512])),
                     reads=["poB"], writes=[sgBt])

                def evac(hp=hp, qc=qc, sgA=sgA, sgAt=sgAt, sgB=sgB, sgBt=sgBt,
                         dn=dn, dnt=dnt, st2=st2, st2t=st2t):
                    P.op("vector", (lambda e: e.reciprocal(
                        dn[64:65, 0:512], sgA[64:65, 0:512])),
                         reads=[sgAt], writes=[dnt + "A"])
                    P.op("vector", (lambda e: e.reciprocal(
                        dn[64:65, 512:1024], sgB[64:65, 0:512])),
                         reads=[sgBt], writes=[dnt + "B"])
                    P.op("tensor", (lambda e: e.matmul(
                        sb["pr"][0:64, 0:512], sb["onesq"][64:65, 0:64],
                        dn[64:65, 0:512],
                        start=True, stop=True, tile_position=(64, 0))),
                         reads=[dnt + "A", "onesq"], writes=["prA"])
                    P.op("tensor", (lambda e: e.matmul(
                        sb["pr"][0:64, 512:1024], sb["onesq"][64:65, 0:64],
                        dn[64:65, 512:1024],
                        start=True, stop=True, tile_position=(64, 0))),
                         reads=[dnt + "B", "onesq"], writes=["pmv"])
                    P.op("vector", (lambda e: e.tensor_tensor(
                        AP(Obf[0:1, 0:1].tensor, hp * S + qc * 512,
                           [[NCH * S, 64], [1, 512]]),
                        sgA[0:64, 0:512], sb["pr"][0:64, 0:512], ALU.mult)),
                         reads=[sgAt, "prA"], writes=["Obf"])
                    if qc == 0:
                        P.op("vector", (lambda e: e.tensor_copy(
                            AP(Obf[0:1, 0:1].tensor, hp * S, [[NCH * S, 64], [1, 1]]),
                            sb["mvsb"][0:64, 0:1])),
                             reads=["mvsb"], writes=["Obf"])
                    P.op("vector", (lambda e: e.tensor_tensor(
                        st2[0:64, 0:512], sgB[0:64, 0:512],
                        sb["pr"][0:64, 512:1024], ALU.mult)),
                         reads=[sgBt, "pmv"], writes=[st2t])
                    if qc == 0:
                        P.op("vector", (lambda e: e.tensor_copy(
                            st2[0:64, 0:1], sb["mvsb"][0:64, 1:2])),
                             reads=["mvsb"], writes=[st2t])
                    P.op("tensor", (lambda e: e.matmul(
                        sb["pr"][64:128, 0:512], sb["c_idf32"][0:64, 0:64],
                        st2[0:64, 0:512],
                        start=True, stop=True, tile_position=(0, 64))),
                         reads=[st2t, "c_idf32"], writes=["prA"])
                    P.op("vector", (lambda e: e.tensor_copy(
                        AP(Obf[0:1, 0:1].tensor, 64 * (NCH * S) + hp * S + qc * 512,
                           [[NCH * S, 64], [1, 512]]),
                        sb["pr"][64:128, 0:512])),
                         reads=["prA"], writes=["Obf"])

                pend[0] = evac

        if pend[0] is not None:
            pend[0]()
            pend[0] = None
        for tt in range(16):
            gi = 16 * b + tt
            xnt = sb["xnt"]
            P.dma("g", xnt[:, :], xn32[gi, :, :], writes=["xnt"])
            for dh in range(2):
                po = "poA" if dh == 0 else "poB"
                for kc in range(NCH):
                    P.op("tensor", (lambda e, tt=tt, dh=dh, kc=kc, po=po: e.matmul(
                        sb[po][:, :],
                        AP(Obf[0:1, 0:1].tensor, kc * S + tt * 128, [[NCH * S, 128], [1, 128]]),
                        sb["Wo1"][:, kc * D + dh * 512: kc * D + dh * 512 + 512],
                        start=(kc == 0), stop=(kc == NCH - 1))),
                         reads=["Obf", "Wo1"], writes=[po], no_inc=(kc < NCH - 1))
                P.op("vector", (lambda e, dh=dh, po=po, xnt=xnt: e.tensor_add(
                    sb["ysb"][:, dh * 512:(dh + 1) * 512], sb[po][:, :],
                    xnt[:, dh * 512:(dh + 1) * 512])),
                     reads=[po, "xnt"], writes=["ysb"])
            P.dma("sy", y32[gi, :, :], sb["ysb"][:, :], reads=["ysb"], writes=["ydma"])


def emit_stage4b(P, dr, sb):
    fc0 = fconst(P, sb, 0.0)
    beps = fconst(P, sb, RMS_EPS)
    emit_load_consts(P, dr, sb, ["ln2_bc"])
    P.dma("g", sb["W1sb"][:, :].rearrange("p (kc d) -> p kc d", kc=NCH),
          dr["W1"].ap().rearrange("(kc p) d -> p kc d", p=128), writes=["W1sb"])
    P.dma("g", sb["W2sb"][:, :].rearrange("p (kc d) -> p kc d", kc=32),
          dr["W2"].ap().rearrange("(kc p) d -> p kc d", p=128), writes=["W2sb"])
    y32 = dr["y32"].ap().rearrange("(n p) d -> n p d", p=128)
    x1 = dr["x1"].ap().rearrange("(n p) d -> n p d", p=128)
    based = dr["base"].ap().rearrange("(n p) d -> n p d", p=128)
    hfm = sb["hfm"]
    for g in range(16):
        tfm = sb["t_fm%d" % (g % 2)]
        tft = "t_fm%d" % (g % 2)
        for ti in range(4):
            gi = 4 * g + ti
            yt = sb["yt%d" % (ti % 2)]
            ytt = "yt%d" % (ti % 2)
            P.dma("g", yt[:, :], y32[gi, :, :], writes=[ytt])
            P.op("scalar", (lambda e, yt=yt, b=fc0: e.activation(
                sb["scr"][:, :], yt[:, :], AF.Square, bias=b,
                accum_out=sb["mv"][:, 0:1])),
                 reads=[ytt, "cpool"], writes=["scr", "mv"])
            P.op("vector", (lambda e: e.tensor_scalar(sb["mv"][:, 1:2], sb["mv"][:, 0:1],
                                                      1.0 / D, None, ALU.mult)),
                 reads=["mv"], writes=["mv1"])
            P.op("scalar", (lambda e, b=beps: e.activation(
                sb["mv"][:, 2:3], sb["mv"][:, 1:2], AF.Ln, bias=b)),
                 reads=["mv1", "cpool"], writes=["mv2"])
            P.op("scalar", (lambda e, b=fc0: e.activation(
                sb["mv"][:, 3:4], sb["mv"][:, 2:3], AF.Exp, scale=-0.5, bias=b)),
                 reads=["mv2", "cpool"], writes=["mv3"])
            P.op("vector", (lambda e, yt=yt: e.tensor_scalar(
                sb["scr"][:, :], yt[:, :], sb["mv"][:, 3:4], None, ALU.mult)),
                 reads=[ytt, "mv3"], writes=["scr"])
            P.op("vector", (lambda e: e.tensor_tensor(
                sb["tbf"][:, :], sb["scr"][:, :], sb["ln2_bc"][:, :], ALU.mult)),
                 reads=["scr", "ln2_bc"], writes=["tbf"])
            for kc in range(NCH):
                P.dma("sc", AP(tfm[0:1, 0:1].tensor, kc * 512 + ti * 128,
                               [[NCH * 512, 128], [1, 128]]),
                      sb["tbf"][:, kc * 128:(kc + 1) * 128],
                      reads=["tbf"], writes=[tft], transpose=True)
        for dc2 in range(32):
            pg = sb["pg%d" % (dc2 % 2)]
            pgt = "pg%d" % (dc2 % 2)
            for kc in range(NCH):
                P.op("tensor", (lambda e, dc2=dc2, kc=kc, pg=pg, tfm=tfm: e.matmul(
                    pg[:, :], sb["W1sb"][:, kc * 4096 + dc2 * 128: kc * 4096 + dc2 * 128 + 128],
                    AP(tfm[0:1, 0:1].tensor, kc * 512, [[NCH * 512, 128], [1, 512]]),
                    start=(kc == 0), stop=(kc == NCH - 1))),
                     reads=["W1sb", tft], writes=[pgt], no_inc=(kc < NCH - 1))
            P.op("vector", (lambda e, dc2=dc2, pg=pg: e.tensor_scalar_max(
                AP(hfm[0:1, 0:1].tensor, dc2 * 512, [[32 * 512, 128], [1, 512]]),
                pg[:, :], 0.0)),
                 reads=[pgt], writes=["hfm"])
        for ti in range(4):
            gi = 4 * g + ti
            yt = sb["yt%d" % (ti % 2)]
            ytt = "yt%d" % (ti % 2)
            P.dma("g", yt[:, :], y32[gi, :, :], writes=[ytt])
            xt = sb["scr"]
            P.dma("g", xt[:, :], x1[gi, :, :], writes=["scr"])
            for dh in range(2):
                pg = sb["pg%d" % (2 + dh)]
                pgt = "pg%d" % (2 + dh)
                for k2 in range(32):
                    P.op("tensor", (lambda e, ti=ti, dh=dh, k2=k2, pg=pg: e.matmul(
                        pg[:, :],
                        AP(hfm[0:1, 0:1].tensor, k2 * 512 + ti * 128,
                           [[32 * 512, 128], [1, 128]]),
                        sb["W2sb"][:, k2 * D + dh * 512: k2 * D + dh * 512 + 512],
                        start=(k2 == 0), stop=(k2 == 31))),
                         reads=["hfm", "W2sb"], writes=[pgt], no_inc=(k2 < 31))
                P.op("vector", (lambda e, dh=dh, pg=pg, yt=yt: e.tensor_add(
                    sb["bse"][:, dh * 512:(dh + 1) * 512], pg[:, :],
                    yt[:, dh * 512:(dh + 1) * 512])),
                     reads=[pgt, ytt], writes=["bse"])
                P.op("vector", (lambda e, dh=dh, xt=xt: e.tensor_add(
                    sb["bse"][:, dh * 512:(dh + 1) * 512],
                    sb["bse"][:, dh * 512:(dh + 1) * 512],
                    xt[:, dh * 512:(dh + 1) * 512])),
                     reads=["scr"], writes=["bse"])
            P.dma("sy", based[gi, :, :], sb["bse"][:, :], reads=["bse"], writes=["bdma"])


def emit_stage5(P, dr, sb):
    fc0 = fconst(P, sb, 0.0)
    P.dma("g", dr["W1e_bf"].ap(), dr["W1e"].ap(), writes=["w1c"])
    P.dma("g", dr["W2e_bf"].ap(), dr["W2e"].ap(), writes=["w2c"])
    gates = dr["gates"].ap()
    based = dr["base"].ap()
    outd = dr["y_out"].ap()
    xnfm = dr["xn_fm_bf"].ap()
    TC = 1024
    for pair in range(T_LOC // (2 * TC)):
        for j in range(2):
            tc = 2 * pair + j
            P.dma("g", sb["gt%d" % j][:, :].rearrange("p (t e) -> p t e", t=8),
                  AP(gates.tensor, tc * TC * 8, [[8, 128], [128 * 8, 8], [1, 8]]),
                  writes=["gt%d" % j])
            P.dma("g", sb["acc%d" % j][:, :].rearrange("p (t d) -> p t d", t=8),
                  AP(based.tensor, tc * TC * D, [[D, 128], [128 * D, 8], [1, D]]),
                  writes=["acc%d" % j])
            P.dma("g", sb["xfm%d" % j][:, :].rearrange("p (kc n) -> p kc n", kc=NCH),
                  AP(xnfm.tensor, tc * TC, [[T_LOC, 128], [128 * T_LOC, NCH], [1, TC]]),
                  writes=["xfm%d" % j])
        for ex in range(MOE_E):
            for half in range(2):
                P.dma("sc", AP(sb["W1f"][0:1, 0:1].tensor, half * TC,
                               [[NCH * MOE_F, 128], [MOE_F, NCH], [1, TC]]),
                      AP(dr["W1e_bf"].ap().tensor, ex * D * MOE_F + half * TC,
                         [[MOE_F, 128], [128 * MOE_F, NCH], [1, TC]]),
                      reads=["w1c"], writes=["W1f%d" % half])
                P.dma("sy", AP(sb["W2h"][0:1, 0:1].tensor, half * NCH * D,
                               [[16 * D, 128], [D, NCH], [1, D]]),
                      AP(dr["W2e_bf"].ap().tensor, ex * MOE_F * D + half * TC * D,
                         [[D, 128], [128 * D, NCH], [1, D]]),
                      reads=["w2c"], writes=["W2h%d" % half])
            for j in range(2):
                xft = "xfm%d" % j
                for half in range(2):
                    for dc2 in range(NCH):
                        pg = sb["pg%d" % (dc2 % 2)]
                        pgt = "pg%d" % (dc2 % 2)
                        for qc2 in range(2):
                            for kc in range(NCH):
                                P.op("tensor", (lambda e, half=half, dc2=dc2, qc2=qc2,
                                                kc=kc, pg=pg, xft=xft: e.matmul(
                                    pg[:, :],
                                    sb["W1f"][:, kc * MOE_F + half * TC + dc2 * 128:
                                              kc * MOE_F + half * TC + dc2 * 128 + 128],
                                    AP(sb[xft][0:1, 0:1].tensor, kc * TC + qc2 * 512,
                                       [[NCH * TC, 128], [1, 512]]),
                                    start=(kc == 0), stop=(kc == NCH - 1))),
                                     reads=["W1f%d" % half, xft], writes=[pgt],
                                     no_inc=(kc < NCH - 1))
                            P.op("scalar", (lambda e, half=half, dc2=dc2, qc2=qc2, pg=pg,
                                            b=fc0: e.activation(
                                AP(sb["Gfm"][0:1, 0:1].tensor,
                                   half * NCH * TC + dc2 * TC + qc2 * 512,
                                   [[16 * TC, 128], [1, 512]]),
                                pg[:, :], AF.Gelu, bias=b)),
                                 reads=[pgt, "cpool"], writes=["Gfm"])
                acct = "acc%d" % j
                gtt = "gt%d" % j
                for t8 in range(8):
                    for dh in range(2):
                        pg = sb["pgW%d" % (dh % 2)]
                        pgt = "pgW%d" % (dh % 2)
                        for half in range(2):
                            for k2 in range(NCH):
                                P.op("tensor", (lambda e, t8=t8, dh=dh, half=half, k2=k2,
                                                pg=pg: e.matmul(
                                    pg[:, :],
                                    AP(sb["Gfm"][0:1, 0:1].tensor,
                                       half * NCH * TC + k2 * TC + t8 * 128,
                                       [[16 * TC, 128], [1, 128]]),
                                    AP(sb["W2h"][0:1, 0:1].tensor,
                                       half * NCH * D + k2 * D + dh * 512,
                                       [[16 * D, 128], [1, 512]]),
                                    start=(half == 0 and k2 == 0),
                                    stop=(half == 1 and k2 == NCH - 1))),
                                     reads=["Gfm", "W2h%d" % half], writes=[pgt],
                                     no_inc=not (half == 1 and k2 == NCH - 1))
                        P.op("vector", (lambda e, ex=ex, t8=t8, dh=dh, pg=pg,
                                        acct=acct, gtt=gtt: e.scalar_tensor_tensor(
                            AP(sb[acct][0:1, 0:1].tensor, t8 * D + dh * 512,
                               [[NCH * D, 128], [1, 512]]),
                            pg[:, :],
                            AP(sb[gtt][0:1, 0:1].tensor, t8 * 8 + ex, [[64, 128], [1, 1]]),
                            AP(sb[acct][0:1, 0:1].tensor, t8 * D + dh * 512,
                               [[NCH * D, 128], [1, 512]]),
                            op0=ALU.mult, op1=ALU.add)),
                             reads=[pgt, gtt], writes=[acct])
        for j in range(2):
            tc = 2 * pair + j
            P.dma("g", AP(outd.tensor, tc * TC * D, [[D, 128], [128 * D, 8], [1, D]]),
                  sb["acc%d" % j][:, :].rearrange("p (t d) -> p t d", t=8),
                  reads=["acc%d" % j], writes=["odma"])


IN, OUT, INT = "ExternalInput", "ExternalOutput", "Internal"

P1_SPECS = {
    "x_in": ([T_LOC, D], F32, IN),
    "Wq": ([D, D], F32, IN), "Wk": ([D, D], F32, IN), "Wv": ([D, D], F32, IN),
    "Wo": ([D, D], F32, IN),
    "Wr": ([D, D], F32, IN), "Wk2": ([D, D], F32, IN), "Wv2": ([D, D], F32, IN),
    "mu_ch": ([128, 16], F32, IN),
    "gamma_bc": ([128, D], F32, IN), "beta_bc": ([128, D], F32, IN),
    "Wq1": ([D, D], F32, IN), "Wk1": ([D, D], F32, IN), "Wv1": ([D, D], F32, IN),
    "Wo1": ([D, D], F32, IN), "rel_bias": ([32, 16], F32, IN),
    "scale16": ([16, 1], F32, IN), "scale128": ([128, 1], F32, IN),
    "ln1_bc": ([128, D], F32, IN), "ln2_bc": ([128, D], F32, IN),
    "W1": ([D, 4096], F32, IN), "W2": ([4096, D], F32, IN),
    "Wg": ([D, MOE_E], F32, IN),
    "c_idf32": ([128, 128], F32, IN), "c_idbf": ([128, 128], BF16, IN),
    "c_headmask": ([128, NCH * H], F32, IN), "c_headexp": ([16, NCH * 128], F32, IN),
    "c_mask01": ([128, 256], F32, IN), "c_onesrow": ([1, 128], F32, IN),
    "c_oht": ([32, ND_B], F32, IN), "c_negrow": ([16, ND_B], F32, IN),
    "Wq_bf": ([D, D], BF16, INT), "Wk_bf": ([D, D], BF16, INT),
    "Wv_bf": ([D, D], BF16, INT), "Wr_bf": ([D, D], BF16, INT),
    "Wk2_bf": ([D, D], BF16, INT), "Wv2_bf": ([D, D], BF16, INT),
    "Wo_bf": ([D, D], BF16, INT),
    "q0_fm": ([D, T_LOC], BF16, INT), "k0_fm": ([D, T_LOC], BF16, INT),
    "v0_tm": ([T_LOC, VD], BF16, INT), "xB": ([T_LOC, D], F32, INT),
    "ssq0": ([16, T_LOC], F32, INT), "ssk0": ([16, T_LOC], F32, INT),
    "x1": ([T_LOC, D], F32, INT),
    "xn32": ([T_LOC, D], F32, INT),
    "q1_fm": ([D, T_LOC], BF16, INT), "k1_fm": ([D, T_LOC], BF16, INT),
    "v1_tm": ([T_LOC, VD], BF16, INT),
    "ssq1": ([16, T_LOC], F32, INT), "ssk1": ([16, T_LOC], F32, INT),
    "sqv": ([16, T_LOC], F32, INT), "skv": ([16, T_LOC], F32, INT),
    "u_rev": ([16, ND_B], F32, INT), "u2": ([16 * 128, WU], F32, INT),
    "y32": ([T_LOC, D], F32, INT),
    "probs": ([T_LOC, MOE_E], F32, OUT), "pT": ([MOE_E, T_LOC], F32, OUT),
    "xn_fm_bf": ([D, T_LOC], BF16, OUT),
    "base": ([T_LOC, D], F32, OUT),
}

P2_SPECS = {
    "probs_gathered": ([MOE_E * N_CORES, T_LOC], F32, IN),
    "probs": ([T_LOC, MOE_E], F32, IN),
    "xn_fm_bf": ([D, T_LOC], BF16, IN),
    "base": ([T_LOC, D], F32, IN),
    "W1e": ([MOE_E * D, MOE_F], F32, IN), "W2e": ([MOE_E * MOE_F, D], F32, IN),
    "c_iota8": ([128, 8], F32, IN), "c_iotaB": ([128, 8], F32, IN),
    "c_onesrow": ([1, 128], F32, IN), "c_p2e": ([128, 8], F32, IN),
    "mid128": ([128], F32, INT), "cnt128": ([128], F32, INT),
    "thr8": ([8], F32, INT),
    "gates": ([T_LOC, MOE_E], F32, INT),
    "W1e_bf": ([MOE_E * D, MOE_F], BF16, INT),
    "W2e_bf": ([MOE_E * MOE_F, D], BF16, INT),
    "y_out": ([T_LOC, D], F32, OUT),
}


def _sz(dt):
    return 2 if dt == BF16 else 4


def make_sb(nc, table, psmap):
    sb = {}
    off = (nc.sbuf_base + 63) & ~63
    for name, (shape, dtype) in table.items():
        nb = shape[1] * _sz(dtype)
        sb[name] = nc.alloc_sbuf_tensor_at("sb_" + name, list(shape), dtype, offset=off)
        off += (nb + 63) & ~63
    assert off <= nc.sbuf_top, (off, nc.sbuf_top)
    sb.update(psmap)
    return sb


S1_TABLE = {
    "c_idf32": ([128, 128], F32), "c_idbf": ([128, 128], BF16),
    "c_headmask": ([128, NCH * H], F32),
    "WT_Wq": ([128, NCH * D], BF16), "WT_Wk": ([128, NCH * D], BF16),
    "WT_Wv": ([128, NCH * D], BF16), "WT_Wr": ([128, NCH * D], BF16),
    "WT_Wk2": ([128, NCH * D], BF16), "WT_Wv2": ([128, NCH * D], BF16),
    "cpool": ([128, 16], F32), "mu_ch": ([128, 16], F32), "mu1": ([128, 16], F32),
    "s_bn6": ([128, 12], F32), "s_mv": ([128, 2 * NTILE], F32),
    "s_nm": ([128, NTILE], F32), "s_rs": ([128, NTILE], F32),
    "s_t64": ([128, NTILE], F32), "s_t64b": ([128, NTILE], F32),
    "xs0": ([128, D], F32), "xs1": ([128, D], F32),
    "xn_tm": ([128, D], F32), "xn32g": ([128, NCH * 513], F32),
    "xnbf": ([128, NCH * 512], BF16), "dbuf": ([128, 512], F32),
    "xrb": ([128, NCH * 512], BF16), "xkb": ([128, NCH * 512], BF16),
    "kact": ([128, NCH * 512], BF16),
    "relu_t": ([128, 512], F32), "r_sb": ([128, D], F32), "b_sb": ([128, 512], F32),
    "xB_sb": ([128, D], F32),
    "v_sb0": ([128, VD], BF16), "v_sb1": ([128, VD], BF16),
    "q_sb0": ([128, 512], BF16), "q_sb1": ([128, 512], BF16),
    "q2": ([128, 512], F32), "ss_sb": ([128, 512], F32),
}

S2_TABLE = {
    "c_mask01": ([128, 256], F32), "c_idf32": ([128, 128], F32),
    "c_onesrow": ([1, 128], F32), "onesq": ([128, 64], F32), "cpool": ([128, 16], F32),
    "gamma_bc": ([128, D], F32), "beta_bc": ([128, D], F32),
    "WT_Wo": ([128, NCH * D], BF16),
    "Qhr0": ([128, S], BF16), "Qhr1": ([128, S], BF16),
    "Khr0": ([128, S], BF16), "Khr1": ([128, S], BF16),
    "Vs": ([128, 16 * VD], BF16), "O32": ([128, NCH * S], F32),
    "mx2": ([128, 12], F32), "negC_row": ([1, 16], F32), "Cb": ([128, 16], F32),
    "ptA0": ([128, 512], BF16), "ptA1": ([128, 512], BF16),
    "ptB0": ([128, 512], BF16), "ptB1": ([128, 512], BF16),
    "recx": ([128, 512], F32), "otm": ([128, D], F32),
    "bn6": ([128, 12], F32), "mv": ([128, 8], F32),
    "lnbf": ([128, D], BF16), "lnfm": ([128, NCH * 512], BF16),
    "sgA0": ([128, 512], F32), "sgA1": ([128, 512], F32),
    "sgB0": ([128, 512], F32), "sgB1": ([128, 512], F32),
    "dn0": ([128, 1024], F32), "dn1": ([128, 1024], F32),
    "stg2": ([128, 512], F32),
}

S3_TABLE = {
    "c_idf32": ([128, 128], F32), "c_headmask": ([128, NCH * H], F32),
    "cpool": ([128, 16], F32), "ln1_bc": ([128, D], F32),
    "scale128": ([128, 1], F32),
    "Wq1": ([128, NCH * D], BF16), "Wk1": ([128, NCH * D], BF16),
    "Wv1": ([128, NCH * D], BF16), "Wg": ([128, NCH * 8], F32),
    "s_bn6": ([128, 12], F32), "s_mv": ([128, 2 * NTILE], F32),
    "s_nm": ([128, NTILE], F32), "s_rs": ([128, NTILE], F32),
    "s_ri": ([128, NTILE], F32), "s_t64": ([128, NTILE], F32),
    "s_t64b": ([128, NTILE], F32),
    "xs0": ([128, D], F32), "xs1": ([128, D], F32),
    "xn_tm": ([128, D], F32), "xni_t": ([128, D], F32), "xni_bf": ([128, D], BF16),
    "xn_fm32": ([128, NCH * 512], F32), "xn_fm_bf": ([128, NCH * 512], BF16),
    "xni_fm": ([128, NCH * 512], BF16),
    "lg": ([128, 20], F32), "probs_t": ([128, 8], F32), "pT_t": ([128, 128], F32),
    "q_sb0": ([128, 512], BF16), "q_sb1": ([128, 512], BF16),
    "q2": ([128, 512], F32), "ss_sb": ([128, 512], F32),
    "ssl": ([128, 1024], F32), "ssl2": ([128, 1024], F32),
    "v_sb0": ([128, VD], BF16), "v_sb1": ([128, VD], BF16),
}

S4G_TABLE = {
    "c_iota8": ([128, 8], F32), "c_iotaB": ([128, 8], F32),
    "c_onesrow": ([1, 128], F32), "c_p2e": ([128, 8], F32),
    "cpool": ([128, 16], F32),
    "G": ([128, 4096], F32), "bis": ([128, 192], F32), "mid_pp": ([128, 1], F32),
    "cmpb": ([128, 4096], F32), "cnt_pp": ([128, 1], F32),
    "thr_bc": ([128, 8], F32), "g8": ([128, 40], F32),
}

S4A_TABLE = {
    "c_idf32": ([128, 128], F32), "c_onesrow": ([1, 128], F32),
    "c_headexp": ([16, NCH * 128], F32),
    "ohts": ([32, 512], F32), "ngr": ([16, 512], F32),
    "uch": ([16, 512], F32), "uchr": ([16, 512], F32),
    "cpool": ([128, 16], F32), "scale16": ([16, 1], F32), "relb": ([32, 16], F32),
    "onesq": ([128, 64], F32), "onescol_bf": ([128, 1], BF16),
    "Wo1": ([128, NCH * D], BF16),
    "hh8": ([16, 18], F32), "row16": ([1, 16], F32),
    "CbN": ([128, 16], F32), "CbF": ([128, 16], F32),
    "Qhr0": ([128, S], BF16), "Qhr1": ([128, S], BF16),
    "Khr0": ([128, S], BF16), "Khr1": ([128, S], BF16),
    "Vs": ([128, 16 * VD], BF16), "Obf": ([128, NCH * S], BF16),
    "Qh": ([128, S], BF16), "Kh": ([128, S], BF16),
    "sqs": ([16, S], F32), "sks": ([16, S], F32),
    "u2A": ([128, WU], F32), "u2B": ([128, WU], F32),
    "ptA0": ([128, 512], BF16), "ptA1": ([128, 512], BF16),
    "ptB0": ([128, 512], BF16), "ptB1": ([128, 512], BF16),
    "mvsb": ([128, 2], F32), "xnt": ([128, D], F32), "ysb": ([128, D], F32),
    "sgA0": ([128, 512], F32), "sgA1": ([128, 512], F32),
    "sgB0": ([128, 512], F32), "sgB1": ([128, 512], F32),
    "dn0": ([128, 1024], F32), "dn1": ([128, 1024], F32),
    "stg2": ([128, 512], F32),
}

S4B_TABLE = {
    "cpool": ([128, 16], F32), "ln2_bc": ([128, D], F32),
    "W1sb": ([128, NCH * 4096], BF16), "W2sb": ([128, 32 * D], BF16),
    "t_fm0": ([128, NCH * 512], BF16), "t_fm1": ([128, NCH * 512], BF16),
    "hfm": ([128, 32 * 512], BF16),
    "yt0": ([128, D], F32), "yt1": ([128, D], F32),
    "scr": ([128, D], F32), "mv": ([128, 8], F32),
    "tbf": ([128, D], BF16), "bse": ([128, D], F32),
}

S5_TABLE = {
    "cpool": ([128, 16], F32), "gt0": ([128, 64], F32), "gt1": ([128, 64], F32),
    "acc0": ([128, NCH * D], F32), "acc1": ([128, NCH * D], F32),
    "xfm0": ([128, NCH * 1024], BF16), "xfm1": ([128, NCH * 1024], BF16),
    "W1f": ([128, NCH * MOE_F], BF16), "W2h": ([128, 16 * D], BF16),
    "Gfm": ([128, 16 * 1024], BF16),
}


def _alloc_psum_banks(nc):
    return {
        "B0": nc.alloc_psum_tensor("psb0", [128, 512], F32),
        "B1": nc.alloc_psum_tensor("psb1", [128, 512], F32),
        "B2": nc.alloc_psum_tensor("psb2", [128, 512], F32),
        "B3": nc.alloc_psum_tensor("psb3", [128, 512], F32),
        "B45": nc.alloc_psum_tensor("psb45", [128, 1024], F32),
        "B6": nc.alloc_psum_tensor("psb6", [128, 512], F32),
        "B7": nc.alloc_psum_tensor("psb7", [128, 512], F32),
    }


def build_part1():
    nc = bass.Bass("TRN2", target_bir_lowering=False)
    dr = {}
    for name, (shape, dtype, kind) in P1_SPECS.items():
        dr[name] = nc.dram_tensor(name, shape, dtype, kind=kind)

    es = contextlib.ExitStack()
    with es:
        B = _alloc_psum_banks(nc)
        P = Prog(nc)

        sb1 = make_sb(nc, S1_TABLE,
                      {"pg0": B["B0"], "pg1": B["B1"], "pg2": B["B2"], "pg3": B["B3"],
                       "pss": B["B6"], "ptf": B["B45"]})
        emit_load_consts(P, dr, sb1, ["c_idf32", "c_idbf", "c_headmask"])
        emit_stage1(P, dr, sb1)
        P.stage_barrier()

        sb2 = make_sb(nc, S2_TABLE,
                      {"pL0A": B["B0"], "pL1A": B["B1"], "pL0B": B["B2"],
                       "pL1B": B["B3"], "poA": B["B6"], "poB": B["B7"],
                       "pr": B["B45"]})
        emit_stage2(P, dr, sb2)
        P.stage_barrier()

        sb3 = make_sb(nc, S3_TABLE,
                      {"pg0": B["B0"], "pg1": B["B1"], "pg2": B["B2"], "pg3": B["B3"],
                       "pssq": B["B6"], "pssk": B["B7"], "ptf": B["B45"]})
        emit_stage3(P, dr, sb3)
        P.stage_barrier()

        sb4a = make_sb(nc, S4A_TABLE,
                       {"pL0A": B["B0"], "pL1A": B["B1"], "pL0B": B["B2"],
                        "pL1B": B["B3"], "poA": B["B6"], "poB": B["B7"],
                        "pr": B["B45"], "pmv": B["B45"]})
        emit_stage4a(P, dr, sb4a)
        P.stage_barrier()

        sb4b = make_sb(nc, S4B_TABLE,
                       {"pg0": B["B0"], "pg1": B["B1"], "pg2": B["B2"], "pg3": B["B3"]})
        emit_stage4b(P, dr, sb4b)

        P.op("sync", (lambda e: e.nop()), extra_waits=P.barrier_token())
        P.finalize(es)
    return nc


def build_part2():
    nc = bass.Bass("TRN2", target_bir_lowering=False)
    dr = {}
    for name, (shape, dtype, kind) in P2_SPECS.items():
        dr[name] = nc.dram_tensor(name, shape, dtype, kind=kind)

    es = contextlib.ExitStack()
    with es:
        B = _alloc_psum_banks(nc)
        P = Prog(nc)

        sb4g = make_sb(nc, S4G_TABLE, {"pthr": B["B0"]})
        emit_stage4g(P, dr, sb4g)
        P.stage_barrier()

        sb5 = make_sb(nc, S5_TABLE,
                      {"pg0": B["B0"], "pg1": B["B1"], "pgW0": B["B2"],
                       "pgW1": B["B3"]})
        emit_stage5(P, dr, sb5)

        P.op("sync", (lambda e: e.nop()), extra_waits=P.barrier_token())
        P.finalize(es)
    return nc


def shard_inputs_part1(inputs, consts):
    x = np.ascontiguousarray(inputs["L0_attn_x"].reshape(32 * S, D))
    mu_r = np.asarray(inputs["L0_ffn_mu_r"], dtype=np.float32)
    mu_k = np.asarray(inputs["L0_ffn_mu_k"], dtype=np.float32)
    mu_ch = np.concatenate([
        np.ascontiguousarray(mu_r.reshape(NCH, 128).T),
        np.ascontiguousarray(mu_k.reshape(NCH, 128).T)], axis=1)
    scale = np.asarray(inputs["L1_attn_scale"], np.float32)
    base = {
        "Wq": np.asarray(inputs["L0_attn_Wq"], np.float32),
        "Wk": np.asarray(inputs["L0_attn_Wk"], np.float32),
        "Wv": np.asarray(inputs["L0_attn_Wv"], np.float32),
        "Wo": np.asarray(inputs["L0_attn_Wo"], np.float32),
        "Wr": np.asarray(inputs["L0_ffn_W_r"], np.float32),
        "Wk2": np.asarray(inputs["L0_ffn_W_k"], np.float32),
        "Wv2": np.asarray(inputs["L0_ffn_W_v"], np.float32),
        "mu_ch": np.ascontiguousarray(mu_ch, np.float32),
        "gamma_bc": np.ascontiguousarray(
            np.broadcast_to(np.asarray(inputs["L0_attn_gamma"], np.float32), (128, D))),
        "beta_bc": np.ascontiguousarray(
            np.broadcast_to(np.asarray(inputs["L0_attn_beta"], np.float32), (128, D))),
        "Wq1": np.asarray(inputs["L1_attn_Wq"], np.float32),
        "Wk1": np.asarray(inputs["L1_attn_Wk"], np.float32),
        "Wv1": np.asarray(inputs["L1_attn_Wv"], np.float32),
        "Wo1": np.asarray(inputs["L1_attn_Wo"], np.float32),
        "rel_bias": np.asarray(inputs["L1_attn_rel_bias"], np.float32),
        "scale16": np.ascontiguousarray(scale.reshape(16, 1)),
        "scale128": np.ascontiguousarray(np.repeat(scale, 8).reshape(128, 1)),
        "ln1_bc": np.ascontiguousarray(
            np.broadcast_to(np.asarray(inputs["L1_attn_ln1_w"], np.float32), (128, D))),
        "ln2_bc": np.ascontiguousarray(
            np.broadcast_to(np.asarray(inputs["L1_attn_ln2_w"], np.float32), (128, D))),
        "W1": np.asarray(inputs["L1_attn_W1"], np.float32),
        "W2": np.asarray(inputs["L1_attn_W2"], np.float32),
        "Wg": np.asarray(inputs["L1_ffn_Wg"], np.float32),
    }
    for k in ("c_idf32", "c_idbf", "c_headmask", "c_headexp", "c_mask01",
              "c_onesrow", "c_oht", "c_negrow"):
        base[k] = consts[k]
    maps = []
    for c in range(N_CORES):
        m = dict(base)
        m["x_in"] = np.ascontiguousarray(x[c * T_LOC:(c + 1) * T_LOC])
        maps.append(m)
    return maps


def shard_inputs_part2(inputs, consts, res1):
    pT_all = np.concatenate([np.asarray(r["pT"], np.float32) for r in res1], axis=0)
    base = {
        "probs_gathered": np.ascontiguousarray(pT_all),
        "W1e": np.ascontiguousarray(
            np.asarray(inputs["L1_ffn_W1"], np.float32).reshape(MOE_E * D, MOE_F)),
        "W2e": np.ascontiguousarray(
            np.asarray(inputs["L1_ffn_W2"], np.float32).reshape(MOE_E * MOE_F, D)),
        "c_iota8": consts["c_iota8"], "c_iotaB": consts["c_iotaB"],
        "c_onesrow": consts["c_onesrow"], "c_p2e": consts["c_p2e"],
    }
    maps = []
    for c in range(N_CORES):
        m = dict(base)
        for k in ("probs", "xn_fm_bf", "base"):
            m[k] = np.asarray(res1[c][k])
        maps.append(m)
    return maps


def kernel(**inputs):
    inputs = {k: np.asarray(v) for k, v in inputs.items()}
    consts = host_consts()
    res1 = run_bass_kernel_spmd(build_part1(), shard_inputs_part1(inputs, consts),
                                core_ids=list(range(N_CORES))).results
    res2 = run_bass_kernel_spmd(build_part2(), shard_inputs_part2(inputs, consts, res1),
                                core_ids=list(range(N_CORES))).results
    out = np.concatenate([np.asarray(r["y_out"], np.float32) for r in res2], axis=0)
    return out.reshape(32, 2048, 1024)
